# version 01
# baseline (speedup 1.0000x reference)
import dataclasses
import numpy as np
import ml_dtypes
import concourse.bass as bass
import concourse.mybir as mybir
from concourse.bass_utils import run_bass_kernel_spmd
from contextlib import ExitStack

F32 = mybir.dt.float32
BF16 = mybir.dt.bfloat16
AF = mybir.ActivationFunctionType
ALU = mybir.AluOpType
AX = mybir.AxisListType

B = 8
S = 4096
E = 2048
NH = 16
NKV = 4
D = 128
CAP = 50.0
EPS_NRM = 1e-6
EPS_LN = 1e-5
HF = 8192
CH = 512
NCH = S // CH
NT = S // 128

_DEBUG = False


def _sylvester(n):
    H = np.array([[1.0]], dtype=np.float64)
    while H.shape[0] < n:
        H = np.block([[H, H], [H, -H]])
    return H


def host_constants():
    c = {}
    c["c_ident"] = np.eye(128, dtype=np.float32).astype(ml_dtypes.bfloat16)
    P = np.zeros((128, 128), np.float32)
    for i in range(64):
        P[i + 64, i] = -1.0
        P[i, i + 64] = 1.0
    c["c_p128"] = P.astype(ml_dtypes.bfloat16)
    c["c_ones"] = np.ones((128, 128), np.float32).astype(ml_dtypes.bfloat16)
    c["c_h64"] = (_sylvester(64) / 64.0).astype(ml_dtypes.bfloat16)
    c["c_h128"] = (_sylvester(128) / np.sqrt(2.0)).astype(ml_dtypes.bfloat16)
    kl = np.arange(128)[:, None]
    ql = np.arange(128)[None, :]
    c["c_maskhi"] = (kl <= ql).astype(np.float32).astype(ml_dtypes.bfloat16)
    c["c_masklo"] = (ql <= kl).astype(np.float32).astype(ml_dtypes.bfloat16)
    half = D // 2
    inv = 1.0 / 10000.0 ** (np.arange(half, dtype=np.float32) / half)
    ang = np.outer(np.arange(S, dtype=np.float32), inv)
    ang = np.concatenate([ang, ang], axis=-1)
    c["c_cosT"] = np.ascontiguousarray(np.cos(ang).T.astype(np.float32))
    c["c_sinT"] = np.ascontiguousarray(np.sin(ang).T.astype(np.float32))
    return c


PE, ACT, DVE, POOL, SP = "pe", "act", "dve", "pool", "sp"
Q_POOL, Q_SP, Q_ACT = "q_pool", "q_sp", "q_act"
_DMA_Q = {POOL: Q_POOL, SP: Q_SP, ACT: Q_ACT}
_ALL_SEMS = (PE, ACT, DVE, POOL, SP, Q_POOL, Q_SP, Q_ACT)


class Buf:
    __slots__ = ("writer", "readers", "name")

    def __init__(self, name, fence):
        self.writer = None
        self.readers = list(fence)
        self.name = name


class OpRec:
    __slots__ = ("build", "waits", "sem", "amount", "tag")

    def __init__(self, build, waits, sem, amount, tag):
        self.build = build
        self.waits = waits
        self.sem = sem
        self.amount = amount
        self.tag = tag


class DramBuf:
    __slots__ = ("writers", "name")

    def __init__(self, name):
        self.writers = []
        self.name = name


class Sched:
    def __init__(self):
        self.ops = {k: [] for k in (PE, ACT, DVE, POOL, SP)}
        self.counts = {k: 0 for k in _ALL_SEMS}
        self.fence = []

    def buf(self, name):
        return Buf(name, self.fence)

    def set_fence(self):
        self.fence = [(k, v) for k, v in self.counts.items() if v > 0]

    def op(self, eng, build, reads=(), writes=(), dreads=(), dwrites=(),
           dma=False, drain=False, tag=""):
        sem = _DMA_Q[eng] if dma else eng
        amount = 16 if dma else 1
        waits = {}

        def add(tok):
            if tok is None:
                return
            s, v = tok
            if (not dma) and eng == PE and s == PE:
                return
            if waits.get(s, 0) < v:
                waits[s] = v

        for b in reads:
            add(b.writer)
        for b in writes:
            add(b.writer)
            for t in b.readers:
                add(t)
        for db in dreads:
            for t in db.writers:
                add(t)
        self.counts[sem] += amount
        tok = (sem, self.counts[sem])
        self.ops[eng].append(OpRec(build, waits, sem, amount, tag))
        for b in reads:
            b.readers.append(tok)
        for b in writes:
            b.writer = tok
            b.readers = []
        for db in dwrites:
            db.writers.append(tok)
        if drain:
            assert dma
            self.ops[eng].append(OpRec(
                lambda e: e.nop(), {sem: self.counts[sem]}, eng, 1, tag + "_drain"))
            self.counts[eng] += 1
        return tok

    def emit(self, block, sems):
        eng_of = {PE: "tensor", ACT: "scalar", DVE: "vector", POOL: "gpsimd",
                  SP: "sync"}
        for eng_name, ops in self.ops.items():
            if not ops:
                continue
            deco = getattr(block, eng_of[eng_name])

            def make(ops_local):
                def _(eng):
                    waited = {}
                    for rec in ops_local:
                        for s, v in sorted(rec.waits.items()):
                            if waited.get(s, 0) >= v:
                                continue
                            eng.wait_ge(sems[s], v)
                            waited[s] = v
                        rec.build(eng).then_inc(sems[rec.sem], rec.amount)
                return _

            deco(make(ops))


class Arena:
    def __init__(self, nc, name, dtype, n_elems):
        self.t = nc.alloc_sbuf_tensor(name, [128, n_elems], dtype)
        self.off = 0
        self.n = n_elems
        self.mark = 0

    def take(self, elems):
        o = self.off
        self.off += elems
        assert self.off <= self.n, f"{self.t} arena overflow {self.off}>{self.n}"
        return o

    def set_mark(self):
        self.mark = self.off

    def reset(self):
        self.off = self.mark

    def ap(self, off, elems, parts=128):
        return self.t[0:parts, off:off + elems]


def _ap3(ap, dims, offset_elems=0):
    part_dim = list(ap.ap[0])
    new_dims = [part_dim] + [[s, c] for (s, c) in dims]
    return dataclasses.replace(ap, offset=ap.offset + offset_elems, ap=new_dims)


def _dram_ap(t, dims, offset_elems=0):
    full = t.ap()
    return dataclasses.replace(full, offset=full.offset + offset_elems,
                               ap=[[s, c] for (s, c) in dims])


def build_nc():
    nc = bass.Bass("TRN2", target_bir_lowering=False)

    x_in = nc.dram_tensor("x", [S, E], F32, kind="ExternalInput")
    wq_in = nc.dram_tensor("wq", [E, NH * D], F32, kind="ExternalInput")
    wk_in = nc.dram_tensor("wk", [E, NKV * D], F32, kind="ExternalInput")
    wv_in = nc.dram_tensor("wv", [E, NKV * D], F32, kind="ExternalInput")
    wo_in = nc.dram_tensor("wo", [NH * D, E], F32, kind="ExternalInput")
    qs_in = nc.dram_tensor("qscale", [D], F32, kind="ExternalInput")
    ks_in = nc.dram_tensor("kscale", [D], F32, kind="ExternalInput")
    w1_in = nc.dram_tensor("w1", [HF, E], F32, kind="ExternalInput")
    w2_in = nc.dram_tensor("w2", [E, HF], F32, kind="ExternalInput")
    w3_in = nc.dram_tensor("w3", [HF, E], F32, kind="ExternalInput")
    cin = {}
    for name, shape, dt in [
        ("c_ident", [128, 128], BF16), ("c_p128", [128, 128], BF16),
        ("c_ones", [128, 128], BF16), ("c_h64", [64, 64], BF16),
        ("c_h128", [128, 128], BF16), ("c_maskhi", [128, 128], BF16),
        ("c_masklo", [128, 128], BF16), ("c_cosT", [128, S], F32),
        ("c_sinT", [128, S], F32),
    ]:
        cin[name] = nc.dram_tensor(name, shape, dt, kind="ExternalInput")
    out_t = nc.dram_tensor("out", [S, E], F32, kind="ExternalOutput")

    skind = "ExternalOutput" if _DEBUG else "Internal"
    w1t_s = nc.dram_tensor("w1t_s", [16, 128, HF], BF16, kind="Internal")
    w3t_s = nc.dram_tensor("w3t_s", [16, 128, HF], BF16, kind="Internal")
    b_s = nc.dram_tensor("b_s", [64, 128, E], BF16, kind=skind)
    w2f_s = nc.dram_tensor("w2f_s", [64, 128, E], BF16, kind=skind)
    qt_s = nc.dram_tensor("qt_s", [NCH, NH, 128, CH], BF16, kind=skind)
    kt_s = nc.dram_tensor("kt_s", [NCH, NKV, 128, CH], BF16, kind=skind)
    v_s = nc.dram_tensor("v_s", [NCH, 4, 128, 516], BF16, kind=skind)
    aot_s = nc.dram_tensor("aot_s", [NCH, 16, 128, CH], BF16, kind=skind)
    x1_s = nc.dram_tensor("x1_s", [NT, 128, E], BF16, kind=skind)
    x1t_s = nc.dram_tensor("x1t_s", [16, 128, S], BF16, kind=skind)
    dbg_r = (nc.dram_tensor("dbg_r", [NT, 128, E], F32, kind="ExternalOutput")
             if _DEBUG else None)
    dbg_st = (nc.dram_tensor("dbg_st", [NT, 128, 28], F32, kind="ExternalOutput")
              if _DEBUG else None)

    sch = Sched()
    DB = {n: DramBuf(n) for n in ("w1t", "w3t", "b", "w2f", "qt", "kt", "v",
                                  "aot", "x1", "x1t")}

    with ExitStack() as ctx:
        ab = Arena(nc, "AB", BF16, 76200)
        af = Arena(nc, "AF", F32, 11500)

        co = {}
        CB = {}
        for name, elems, parts in [
            ("c_ident", 128, 128), ("c_p128", 128, 128), ("c_ones", 128, 128),
            ("c_h64", 64, 64), ("c_h128", 128, 128), ("c_maskhi", 128, 128),
            ("c_masklo", 128, 128),
        ]:
            co[name] = ab.take(elems)
            CB[name] = sch.buf(name)
            dst = ab.ap(co[name], elems, parts)
            src = cin[name].ap()
            sch.op(POOL, (lambda d, s: (lambda g: g.dma_start(out=d, in_=s)))(dst, src),
                   writes=[CB[name]], dma=True, drain=(parts != 128),
                   tag=f"ld_{name}")
        ident = lambda: ab.ap(co["c_ident"], 128)
        p128 = lambda: ab.ap(co["c_p128"], 128)
        ones = lambda: ab.ap(co["c_ones"], 128)
        h64 = lambda: ab.ap(co["c_h64"], 64, 64)
        h128 = lambda: ab.ap(co["c_h128"], 128)
        maskhi = lambda: ab.ap(co["c_maskhi"], 128)
        masklo = lambda: ab.ap(co["c_masklo"], 128)

        kc_o = af.take(2)
        kc_b = sch.buf("kc")
        qs_dst = af.ap(kc_o, 1)
        ks_dst = af.ap(kc_o + 1, 1)
        sch.op(POOL, lambda g: g.dma_start(
            out=qs_dst, in_=_dram_ap(qs_in, [(1, 128), (1, 1)]),
            allow_slow_non_contiguous=True), writes=[kc_b], dma=True, drain=True, tag="ld_qs")
        sch.op(POOL, lambda g: g.dma_start(
            out=ks_dst, in_=_dram_ap(ks_in, [(1, 128), (1, 1)]),
            allow_slow_non_contiguous=True), writes=[kc_b], dma=True, drain=True, tag="ld_ks")
        sch.op(DVE, lambda v: v.tensor_mul(qs_dst, qs_dst, ks_dst),
               reads=[kc_b], writes=[kc_b], tag="kc_mul")
        sch.op(DVE, lambda v: v.tensor_scalar_mul(
            qs_dst, qs_dst, float(1.0 / (CAP * np.sqrt(D)))),
            reads=[kc_b], writes=[kc_b], tag="kc_sc")
        kc_ap = lambda: af.ap(kc_o, 1)

        epsln_o = af.take(1)
        epsln_b = sch.buf("epsln")
        sch.op(DVE, lambda v: v.memset(af.ap(epsln_o, 1), EPS_LN),
               writes=[epsln_b], tag="eps_set")
        epsln_ap = lambda: af.ap(epsln_o, 1)

        ab.set_mark()
        af.set_mark()

        with ExitStack() as pctx:
            ps_tr = pctx.enter_context(nc.psum_tensor("p0_tr", [128, 2048], BF16))
            ps_tr2 = pctx.enter_context(nc.psum_tensor("p0_tr2", [128, 2048], BF16))
            ps_f = pctx.enter_context(nc.psum_tensor("p0_f", [64, 1024], F32))
            ps_g = pctx.enter_context(nc.psum_tensor("p0_g", [128, 1024], F32))
            pstr_t = [ps_tr, ps_tr2]
            ps_trb = [sch.buf("p0tr0"), sch.buf("p0tr1")]
            psf_b = [sch.buf("p0f0"), sch.buf("p0f1")]
            psg_b = [sch.buf("p0g0"), sch.buf("p0g1")]

            slabb_o = [ab.take(2048), ab.take(2048)]
            evacb_o = [ab.take(2048), ab.take(2048)]
            slabb_b = [sch.buf("sb0"), sch.buf("sb1")]
            evacb_b = [sch.buf("ev0"), sch.buf("ev1")]

            for wi, (w_dram, wt_dram) in enumerate([(w1_in, w1t_s), (w3_in, w3t_s)]):
                for hc in range(64):
                    par = hc % 2
                    sb = ab.ap(slabb_o[par], 2048)
                    ev = ab.ap(evacb_o[par], 2048)
                    src = _dram_ap(w_dram, [(2048, 128), (1, 2048)], hc * 128 * 2048)
                    sch.op(POOL, lambda g, s=src, d=sb: g.dma_start(out=d, in_=s),
                           writes=[slabb_b[par]], dma=True, tag=f"w_ld{wi}_{hc}")
                    pst = pstr_t[par]
                    for k in range(16):
                        sch.op(PE, (lambda kk, sbv=sb, p=pst: (lambda t: t.transpose(
                            p[:, kk * 128:(kk + 1) * 128],
                            sbv[:, kk * 128:(kk + 1) * 128], ident())))(k),
                            reads=[slabb_b[par], CB["c_ident"]],
                            writes=[ps_trb[par]], tag="w_tr")
                    sch.op(DVE, lambda v, p=pst, d=ev: v.tensor_copy(d, p[:]),
                           reads=[ps_trb[par]], writes=[evacb_b[par]], tag="w_ev")
                    dst = _dram_ap(wt_dram, [(HF, 128), (128 * HF, 16), (1, 128)],
                                   hc * 128)
                    esrc = _ap3(ev, [(128, 16), (1, 128)])
                    sch.op(SP, lambda q, s=esrc, d=dst: q.dma_start(out=d, in_=s),
                           reads=[evacb_b[par]],
                           dwrites=[DB["w1t"] if wi == 0 else DB["w3t"]],
                           dma=True, tag="w_st")

            t2b_o = [ab.take(4096), ab.take(4096)]
            bb_o = [ab.take(4096), ab.take(4096)]
            t2b_b = [sch.buf("t2b0"), sch.buf("t2b1")]
            bb_b = [sch.buf("bb0"), sch.buf("bb1")]
            for g in range(64):
                par = g % 2
                tb = ab.ap(t2b_o[par], 4096, 64)
                bbv = ab.ap(bb_o[par], 4096, 64)
                src = _dram_ap(w2_in, [(128, 64), (8192, 32), (1, 128)],
                               g * 32 * 8192)
                sch.op(POOL, lambda g_, s=src, d=tb: g_.dma_start(out=d, in_=s),
                       writes=[t2b_b[par]], dma=True, drain=True,
                       tag=f"f1_ld_{g}")
                for m in range(8):
                    mp = m % 2
                    rhs = _ap3(tb, [(1, 16), (128, 32)], m * 16)
                    psl = ps_f[:, mp * 512:(mp + 1) * 512]
                    sch.op(PE, lambda t, r=rhs, p=psl: t.matmul(
                        p, h64(), r, start=True, stop=True),
                        reads=[t2b_b[par], CB["c_h64"]], writes=[psf_b[mp]],
                        tag="f1_mm")
                    dstb = _ap3(bbv, [(1, 512)], m * 512)
                    sch.op(DVE, lambda v, p=psl, d=dstb: v.tensor_copy(d, p),
                           reads=[psf_b[mp]], writes=[bb_b[par]], tag="f1_ev")
                dst = _dram_ap(b_s, [(128 * E, 64), (E, 128), (1, 32)], g * 32)
                bsrc = _ap3(bbv, [(32, 128), (1, 32)])
                sch.op(SP, lambda q, s=bsrc, d=dst: q.dma_start(out=d, in_=s),
                       reads=[bb_b[par]], dwrites=[DB["b"]], dma=True, drain=True,
                       tag=f"f1_st_{g}")

            bslab_o = [ab.take(2048), ab.take(2048)]
            w2fe_o = [ab.take(2048), ab.take(2048)]
            bslab_b = [sch.buf("bs0"), sch.buf("bs1")]
            w2fe_b = [sch.buf("we0"), sch.buf("we1")]
            for ip in range(64):
                par = ip % 2
                bsl = ab.ap(bslab_o[par], 2048)
                wfe = ab.ap(w2fe_o[par], 2048)
                src = _dram_ap(b_s, [(E, 128), (1, E)], ip * 128 * E)
                sch.op(POOL, lambda g_, s=src, d=bsl: g_.dma_start(out=d, in_=s),
                       writes=[bslab_b[par]], dreads=[DB["b"]], dma=True,
                       tag=f"f2_ld_{ip}")
                for dc in range(4):
                    dp = dc % 2
                    psl = ps_g[:, dp * 512:(dp + 1) * 512]
                    sch.op(PE, lambda t, p=psl, r=bsl[:, dc * 512:(dc + 1) * 512]:
                           t.matmul(p, h128(), r, start=True, stop=True),
                           reads=[bslab_b[par], CB["c_h128"]], writes=[psg_b[dp]],
                           tag="f2_mm")
                    sch.op(DVE, lambda v, p=psl, d=wfe[:, dc * 512:(dc + 1) * 512]:
                           v.tensor_copy(d, p),
                           reads=[psg_b[dp]], writes=[w2fe_b[par]], tag="f2_ev")
                dst = _dram_ap(w2f_s, [(E, 128), (1, E)], ip * 128 * E)
                sch.op(SP, lambda q, s=wfe, d=dst: q.dma_start(out=d, in_=s),
                       reads=[w2fe_b[par]], dwrites=[DB["w2f"]], dma=True,
                       tag=f"f2_st_{ip}")

        sch.set_fence()
        ab.reset()
        af.reset()

        with ExitStack() as pctx:
            ps_tr = pctx.enter_context(nc.psum_tensor("p1_tr", [128, 2048], BF16))
            ps_tr2 = pctx.enter_context(nc.psum_tensor("p1_tr2", [128, 2048], BF16))
            ps_a = pctx.enter_context(nc.psum_tensor("p1_a", [128, 1024], F32))
            ps_r = pctx.enter_context(nc.psum_tensor("p1_r", [128, 512], F32))
            ps_s = pctx.enter_context(nc.psum_tensor("p1_s", [128, 512], F32))
            pstr_t = [ps_tr, ps_tr2]
            ps_trb = [sch.buf("p1tr0"), sch.buf("p1tr1")]
            psa_b = [sch.buf("p1a0"), sch.buf("p1a1")]
            psr_b = sch.buf("p1r")
            pss_b = sch.buf("p1s")

            wq_o = ab.take(16 * 2048)
            wk_o = ab.take(16 * 512)
            wv_o = ab.take(16 * 512)
            wq_b = sch.buf("wq"); wk_b = sch.buf("wk"); wv_b = sch.buf("wv")
            for (wt, off, width, buf) in [(wq_in, wq_o, 2048, wq_b),
                                          (wk_in, wk_o, 512, wk_b),
                                          (wv_in, wv_o, 512, wv_b)]:
                for dm in range(16):
                    dstb = ab.ap(off + dm * width, width)
                    src = _dram_ap(wt, [(width, 128), (1, width)], dm * 128 * width)
                    sch.op(POOL, lambda g_, s=src, d=dstb: g_.dma_start(out=d, in_=s),
                           writes=[buf], dma=True, tag="wqkv_ld")

            xb_o = [ab.take(2048), ab.take(2048)]
            xb_b = [sch.buf("xb0"), sch.buf("xb1")]
            xt_o = ab.take(16 * 512)
            xt_b = sch.buf("xt")
            cs_o = [af.take(512), af.take(512)]
            sn_o = [af.take(512), af.take(512)]
            cs_b = [sch.buf("cs0"), sch.buf("cs1")]
            sn_b = [sch.buf("sn0"), sch.buf("sn1")]
            qbf_o = [ab.take(512), ab.take(512)]
            qbf_b = [sch.buf("qb0"), sch.buf("qb1")]
            t1_o = af.take(512); t2_o = af.take(512); qp_o = af.take(512)
            t1_b = sch.buf("t1"); t2_b = sch.buf("t2"); qp_b = sch.buf("qp")
            sqb_o = ab.take(512); sqb_b = sch.buf("sqb")
            nrm_o = af.take(512); nrm_b = sch.buf("nrm")
            rcp_o = af.take(512); rcp_b = sch.buf("rcp")
            kf_o = af.take(512); kf_b = sch.buf("kf")
            qtc_o = ab.take(16 * 512); qtc_b = sch.buf("qtc")
            ktc_o = ab.take(4 * 512); ktc_b = sch.buf("ktc")
            vt_o = ab.take(4 * 516); vt_b = sch.buf("vt")

            def rope_norm(par, psl, psl_buf, out_off, out_buf, is_k, c):
                qb = ab.ap(qbf_o[par], 512)
                sch.op(DVE, lambda v, p=psl, d=qb: v.tensor_copy(d, p),
                       reads=[psl_buf], writes=[qbf_b[par]], tag="r_ev")
                sch.op(PE, lambda t, s=qb: t.matmul(ps_r[:], p128(), s,
                                                    start=True, stop=True),
                       reads=[qbf_b[par], CB["c_p128"]], writes=[psr_b], tag="r_rot")
                csl = af.ap(cs_o[c % 2], 512)
                snl = af.ap(sn_o[c % 2], 512)
                t1 = af.ap(t1_o, 512)
                t2 = af.ap(t2_o, 512)
                qp = af.ap(qp_o, 512)
                sch.op(DVE, lambda v, q=qb: v.tensor_mul(t1, q, csl),
                       reads=[qbf_b[par], cs_b[c % 2]], writes=[t1_b], tag="r_t1")
                sch.op(DVE, lambda v: v.tensor_mul(t2, ps_r[:], snl),
                       reads=[psr_b, sn_b[c % 2]], writes=[t2_b], tag="r_t2")
                sch.op(DVE, lambda v: v.tensor_add(qp, t1, t2),
                       reads=[t1_b, t2_b], writes=[qp_b], tag="r_qp")
                sq = ab.ap(sqb_o, 512)
                sch.op(DVE, lambda v: v.tensor_mul(sq, qp, qp),
                       reads=[qp_b], writes=[sqb_b], tag="r_sq")
                sch.op(PE, lambda t: t.matmul(ps_s[:], ones(), sq,
                                              start=True, stop=True),
                       reads=[sqb_b, CB["c_ones"]], writes=[pss_b], tag="r_ss")
                nr = af.ap(nrm_o, 512)
                sch.op(ACT, lambda a: a.activation(nr, ps_s[:], AF.Sqrt),
                       reads=[pss_b], writes=[nrm_b], tag="r_sqrt")
                sch.op(DVE, lambda v: v.tensor_scalar_add(nr, nr, EPS_NRM),
                       reads=[nrm_b], writes=[nrm_b], tag="r_eps")
                rc = af.ap(rcp_o, 512)
                sch.op(DVE, lambda v: v.reciprocal(rc, nr),
                       reads=[nrm_b], writes=[rcp_b], tag="r_rcp")
                outap = ab.ap(out_off, 512)
                if is_k:
                    kf = af.ap(kf_o, 512)
                    sch.op(DVE, lambda v: v.tensor_mul(kf, qp, rc),
                           reads=[qp_b, rcp_b], writes=[kf_b], tag="r_km")
                    sch.op(DVE, lambda v: v.tensor_scalar_mul(outap, kf, kc_ap()),
                           reads=[kf_b, kc_b], writes=[out_buf], tag="r_ks")
                else:
                    sch.op(DVE, lambda v: v.tensor_mul(outap, qp, rc),
                           reads=[qp_b, rcp_b], writes=[out_buf], tag="r_qm")

            for c in range(NCH):
                cp = c % 2
                csl = af.ap(cs_o[cp], 512)
                snl = af.ap(sn_o[cp], 512)
                sch.op(POOL, lambda g_, d=csl, s=cin["c_cosT"].ap()[:, c * CH:(c + 1) * CH]:
                       g_.dma_start(out=d, in_=s), writes=[cs_b[cp]], dma=True,
                       tag="cs_ld")
                sch.op(POOL, lambda g_, d=snl, s=cin["c_sinT"].ap()[:, c * CH:(c + 1) * CH]:
                       g_.dma_start(out=d, in_=s), writes=[sn_b[cp]], dma=True,
                       tag="sn_ld")
                for t4 in range(4):
                    par = t4 % 2
                    xb = ab.ap(xb_o[par], 2048)
                    row0 = (c * 4 + t4) * 128
                    src = _dram_ap(x_in, [(E, 128), (1, E)], row0 * E)
                    sch.op(POOL, lambda g_, s=src, d=xb: g_.dma_start(out=d, in_=s),
                           writes=[xb_b[par]], dma=True, tag=f"x_ld_{c}_{t4}")
                    pst = pstr_t[par]
                    for k in range(16):
                        sch.op(PE, (lambda kk, xbv=xb, p=pst: (lambda t: t.transpose(
                            p[:, kk * 128:(kk + 1) * 128],
                            xbv[:, kk * 128:(kk + 1) * 128], ident())))(k),
                            reads=[xb_b[par], CB["c_ident"]], writes=[ps_trb[par]],
                            tag="x_tr")
                    xt_dst = _ap3(ab.ap(xt_o, 1), [(512, 16), (1, 128)], t4 * 128)
                    sch.op(DVE, lambda v, p=pst, d=xt_dst: v.tensor_copy(d, p[:]),
                           reads=[ps_trb[par]], writes=[xt_b], tag="x_ev")
                for h in range(NH):
                    par = h % 2
                    psl = ps_a[:, par * 512:(par + 1) * 512]
                    for dc in range(16):
                        lhs = ab.ap(wq_o + dc * 2048 + h * 128, 128)
                        rhs = ab.ap(xt_o + dc * 512, 512)
                        sch.op(PE, lambda t, l=lhs, r=rhs, p=psl, st=(dc == 0),
                               sp=(dc == 15): t.matmul(p, l, r, start=st, stop=sp),
                               reads=[wq_b, xt_b], writes=[psa_b[par]], tag="q_mm")
                    rope_norm(par, psl, psa_b[par], qtc_o + h * 512, qtc_b, False, c)
                dst = _dram_ap(qt_s, [(CH, 128), (128 * CH, 16), (1, CH)],
                               c * NH * 128 * CH)
                qsrc = _ap3(ab.ap(qtc_o, 1), [(512, 16), (1, 512)])
                sch.op(SP, lambda q, s=qsrc, d=dst: q.dma_start(out=d, in_=s),
                       reads=[qtc_b], dwrites=[DB["qt"]], dma=True, tag="qt_st")
                for kv in range(NKV):
                    par = kv % 2
                    psl = ps_a[:, par * 512:(par + 1) * 512]
                    for dc in range(16):
                        lhs = ab.ap(wk_o + dc * 512 + kv * 128, 128)
                        rhs = ab.ap(xt_o + dc * 512, 512)
                        sch.op(PE, lambda t, l=lhs, r=rhs, p=psl, st=(dc == 0),
                               sp=(dc == 15): t.matmul(p, l, r, start=st, stop=sp),
                               reads=[wk_b, xt_b], writes=[psa_b[par]], tag="k_mm")
                    rope_norm(par, psl, psa_b[par], ktc_o + kv * 512, ktc_b, True, c)
                dst = _dram_ap(kt_s, [(CH, 128), (128 * CH, 4), (1, CH)],
                               c * NKV * 128 * CH)
                ksrc = _ap3(ab.ap(ktc_o, 1), [(512, 4), (1, 512)])
                sch.op(SP, lambda q, s=ksrc, d=dst: q.dma_start(out=d, in_=s),
                       reads=[ktc_b], dwrites=[DB["kt"]], dma=True, tag="kt_st")
                ones_dst = _ap3(ab.ap(vt_o, 1), [(516, 4), (129, 4), (1, 1)], 128)
                sch.op(DVE, lambda v, d=ones_dst: v.memset(d, 1.0),
                       writes=[vt_b], tag="v_ones")
                for t4 in range(4):
                    par = t4 % 2
                    psl = ps_a[:, par * 512:(par + 1) * 512]
                    for dc in range(16):
                        lhs = ab.ap(xt_o + dc * 512 + t4 * 128, 128)
                        rhs = ab.ap(wv_o + dc * 512, 512)
                        sch.op(PE, lambda t, l=lhs, r=rhs, p=psl, st=(dc == 0),
                               sp=(dc == 15): t.matmul(p, l, r, start=st, stop=sp),
                               reads=[wv_b, xt_b], writes=[psa_b[par]], tag="v_mm")
                    vdst = _ap3(ab.ap(vt_o, 1), [(129, 4), (1, 128)], t4 * 516)
                    sch.op(DVE, lambda v, p=psl, d=vdst: v.tensor_copy(d, p),
                           reads=[psa_b[par]], writes=[vt_b], tag="v_ev")
                    vst = _dram_ap(v_s, [(516, 128), (1, 516)],
                                   (c * 4 + t4) * 128 * 516)
                    vsr = _ap3(ab.ap(vt_o, 1), [(1, 516)], t4 * 516)
                    sch.op(SP, lambda q, s=vsr, d=vst: q.dma_start(out=d, in_=s),
                           reads=[vt_b], dwrites=[DB["v"]], dma=True, tag="v_st")

        sch.set_fence()
        ab.reset()
        af.reset()

        with ExitStack() as pctx:
            ps_sc = pctx.enter_context(nc.psum_tensor("p2_sc", [128, 1536], F32))
            ps_av = pctx.enter_context(nc.psum_tensor("p2_av", [128, 2048], F32))
            ps_at = pctx.enter_context(nc.psum_tensor("p2_at", [128, 512], BF16))
            pssc_b = sch.buf("ps_sc")
            psav_b = sch.buf("ps_av")
            psat_b = sch.buf("ps_at")

            qin_o = [ab.take(16 * 512), ab.take(16 * 512)]
            qin_b = [sch.buf("qin0"), sch.buf("qin1")]
            kr_o = ab.take(6 * 4 * 512)
            kr_b = [sch.buf(f"kr{s_}") for s_ in range(6)]
            vr_o = ab.take(6 * 4 * 516)
            vr_b = [sch.buf(f"vr{s_}") for s_ in range(6)]
            tb_o = af.take(1536)
            tb_b = sch.buf("tb")
            eb_o = [ab.take(1536), ab.take(1536)]
            eb_b = [sch.buf("eb0"), sch.buf("eb1")]
            adiv_o = ab.take(4 * 128)
            adiv_b = sch.buf("adiv")
            rcd_o = af.take(4)
            rcd_b = sch.buf("rcd")
            aotb_o = [ab.take(512), ab.take(512)]
            aotb_b = [sch.buf("ao0"), sch.buf("ao1")]

            def load_kv_slot(cc):
                slot = cc % 6
                kd = _ap3(ab.ap(kr_o + slot * 4 * 512, 1), [(512, 4), (1, 512)])
                ks = _dram_ap(kt_s, [(CH, 128), (128 * CH, 4), (1, CH)],
                              cc * NKV * 128 * CH)
                sch.op(POOL, lambda g_, d=kd, s=ks: g_.dma_start(out=d, in_=s),
                       writes=[kr_b[slot]], dreads=[DB["kt"]], dma=True,
                       tag=f"kr_{cc}")
                vd = _ap3(ab.ap(vr_o + slot * 4 * 516, 1), [(516, 4), (1, 516)])
                vs2 = _dram_ap(v_s, [(516, 128), (128 * 516, 4), (1, 516)],
                               cc * 4 * 128 * 516)
                sch.op(POOL, lambda g_, d=vd, s=vs2: g_.dma_start(out=d, in_=s),
                       writes=[vr_b[slot]], dreads=[DB["v"]], dma=True,
                       tag=f"vr_{cc}")

            def load_q_chunk(cc):
                par = cc % 2
                qd = _ap3(ab.ap(qin_o[par], 1), [(512, 16), (1, 512)])
                qs2 = _dram_ap(qt_s, [(CH, 128), (128 * CH, 16), (1, CH)],
                               cc * NH * 128 * CH)
                sch.op(POOL, lambda g_, d=qd, s=qs2: g_.dma_start(out=d, in_=s),
                       writes=[qin_b[par]], dreads=[DB["qt"]], dma=True,
                       tag=f"q_{cc}")

            for cc in range(3):
                load_kv_slot(cc)
            load_q_chunk(0)

            for c in range(NCH):
                if c + 1 < NCH:
                    load_q_chunk(c + 1)
                if c + 3 < NCH:
                    load_kv_slot(c + 3)
                kb_lo = max(0, 4 * c - 8)
                kb_hi = min(NT - 1, 4 * c + 11)
                kbs = list(range(kb_lo, kb_hi + 1))
                for h in range(NH):
                    kv = h // 4
                    qrhs = ab.ap(qin_o[c % 2] + h * 512, 512)
                    started = [False] * 4
                    for w0 in range(0, len(kbs), 3):
                        wave = kbs[w0:w0 + 3]
                        for wi, kb in enumerate(wave):
                            slot = (kb // 4) % 6
                            lhs = ab.ap(kr_o + slot * 4 * 512 + kv * 512 +
                                        (kb % 4) * 128, 128)
                            psl = ps_sc[:, wi * 512:(wi + 1) * 512]
                            sch.op(PE, lambda t, l=lhs, r=qrhs, p=psl:
                                   t.matmul(p, l, r, start=True, stop=True),
                                   reads=[kr_b[slot], qin_b[c % 2]],
                                   writes=[pssc_b], tag="sc_mm")
                        nW = len(wave) * 512
                        tb = af.ap(tb_o, nW)
                        sch.op(ACT, lambda a, d=tb, n=nW: a.activation(
                            d, ps_sc[:, 0:n], AF.Tanh),
                            reads=[pssc_b], writes=[tb_b], tag="sc_tanh")
                        ebp = (w0 // 3) % 2
                        eb = ab.ap(eb_o[ebp], nW)
                        sch.op(ACT, lambda a, s=tb, d=eb: a.activation(
                            d, s, AF.Exp, scale=CAP),
                            reads=[tb_b], writes=[eb_b[ebp]], tag="sc_exp")
                        for wi, kb in enumerate(wave):
                            slot = (kb // 4) % 6
                            for qs in range(4):
                                qi = 4 * c + qs
                                dblk = kb - qi
                                if abs(dblk) > 8:
                                    continue
                                esl = ab.ap(eb_o[ebp] + wi * 512 + qs * 128, 128)
                                if dblk == 8:
                                    sch.op(DVE, lambda v, e=esl: v.tensor_mul(
                                        e, e, maskhi()),
                                        reads=[eb_b[ebp], CB["c_maskhi"]],
                                        writes=[eb_b[ebp]], tag="m_hi")
                                elif dblk == -8:
                                    sch.op(DVE, lambda v, e=esl: v.tensor_mul(
                                        e, e, masklo()),
                                        reads=[eb_b[ebp], CB["c_masklo"]],
                                        writes=[eb_b[ebp]], tag="m_lo")
                                vrhs = ab.ap(vr_o + slot * 4 * 516 +
                                             (kb % 4) * 516 + kv * 129, 129)
                                psl = ps_av[:, qs * 512:qs * 512 + 129]
                                st = not started[qs]
                                started[qs] = True
                                sp = (kb == min(NT - 1, qi + 8))
                                sch.op(PE, lambda t, l=esl, r=vrhs, p=psl, s=st,
                                       s2=sp: t.matmul(p, l, r, start=s, stop=s2),
                                       reads=[eb_b[ebp], vr_b[slot]],
                                       writes=[psav_b], tag="av_mm")
                    for qs in range(4):
                        rc = af.ap(rcd_o + qs, 1)
                        den = ps_av[:, qs * 512 + 128:qs * 512 + 129]
                        sch.op(DVE, lambda v, d=rc, s=den: v.reciprocal(d, s),
                               reads=[psav_b], writes=[rcd_b], tag="d_rcp")
                        num = ps_av[:, qs * 512:qs * 512 + 128]
                        ad = ab.ap(adiv_o + qs * 128, 128)
                        sch.op(DVE, lambda v, d=ad, s=num, r=rc:
                               v.tensor_scalar_mul(d, s, r),
                               reads=[psav_b, rcd_b], writes=[adiv_b], tag="d_mul")
                    for qs in range(4):
                        ad = ab.ap(adiv_o + qs * 128, 128)
                        sch.op(PE, lambda t, s=ad,
                               p=ps_at[:, qs * 128:(qs + 1) * 128]:
                               t.transpose(p, s, ident()),
                               reads=[adiv_b, CB["c_ident"]], writes=[psat_b],
                               tag="ao_tr")
                    hp = h % 2
                    ao = ab.ap(aotb_o[hp], 512)
                    sch.op(DVE, lambda v, d=ao: v.tensor_copy(d, ps_at[:]),
                           reads=[psat_b], writes=[aotb_b[hp]], tag="ao_ev")
                    adst = _dram_ap(aot_s, [(CH, 128), (1, CH)],
                                    (c * 16 + h) * 128 * CH)
                    sch.op(SP, lambda q, s=ao, d=adst: q.dma_start(out=d, in_=s),
                           reads=[aotb_b[hp]], dwrites=[DB["aot"]], dma=True,
                           tag="ao_st")

        sch.set_fence()
        ab.reset()
        af.reset()

        with ExitStack() as pctx:
            ps_tr = pctx.enter_context(nc.psum_tensor("p3_tr", [128, 2048], BF16))
            ps_tr2 = pctx.enter_context(nc.psum_tensor("p3_tr2", [128, 2048], BF16))
            ps_w = pctx.enter_context(nc.psum_tensor("p3_w", [128, 1024], F32))
            pstr_t = [ps_tr, ps_tr2]
            ps_trb = [sch.buf("p3tr0"), sch.buf("p3tr1")]
            psw_b = [sch.buf("p3w0"), sch.buf("p3w1")]

            wo_o = ab.take(16 * 2048)
            wo_b = sch.buf("wo")
            for hdc in range(16):
                dstb = ab.ap(wo_o + hdc * 2048, 2048)
                src = _dram_ap(wo_in, [(E, 128), (1, E)], hdc * 128 * E)
                sch.op(POOL, lambda g_, s=src, d=dstb: g_.dma_start(out=d, in_=s),
                       writes=[wo_b], dma=True, tag="wo_ld")

            aoin_o = [ab.take(16 * 512), ab.take(16 * 512)]
            aoin_b = [sch.buf("ai0"), sch.buf("ai1")]
            x4f_o = [af.take(4 * 2048)]
            x4f_b = [sch.buf("x40")]
            r_o = af.take(2048)
            r_b = sch.buf("r")
            st_o = af.take(28)
            st_b = sch.buf("stats")
            x1b_o = [ab.take(2048), ab.take(2048)]
            x1b_b = [sch.buf("x1b0"), sch.buf("x1b1")]
            ex1_o = [ab.take(2048), ab.take(2048)]
            ex1_b = [sch.buf("ex0"), sch.buf("ex1")]

            def layer_norm(src_ap, src_buf, out_ap, out_buf, tag):
                bn24 = af.ap(st_o, 24)
                bn2 = af.ap(st_o + 24, 2)
                for i in range(4):
                    bno = af.ap(st_o + i * 6, 6)
                    bni = _ap3(src_ap, [(1, 512)], i * 512)
                    sch.op(DVE, (lambda o2, i2: (lambda v: v.bn_stats(o2, i2)))(
                        bno, bni),
                        reads=[src_buf], writes=[st_b], tag=tag + "_bns")
                sch.op(DVE, lambda v: v.bn_aggr(bn2, bn24),
                       reads=[st_b], writes=[st_b], tag=tag + "_bna")
                mean = af.ap(st_o + 24, 1)
                var = af.ap(st_o + 25, 1)
                sd = af.ap(st_o + 26, 1)
                sch.op(ACT, lambda a: a.activation(sd, var, AF.Sqrt,
                                                   bias=epsln_ap()),
                       reads=[st_b, epsln_b], writes=[st_b], tag=tag + "_sd")
                rs = af.ap(st_o + 27, 1)
                sch.op(DVE, lambda v: v.reciprocal(rs, sd),
                       reads=[st_b], writes=[st_b], tag=tag + "_rs")
                sch.op(DVE, lambda v: v.tensor_scalar(
                    out_ap, src_ap, mean, rs, ALU.subtract, ALU.mult),
                    reads=[src_buf, st_b], writes=[out_buf], tag=tag + "_out")

            for c in range(NCH):
                cp = c % 2
                aod = _ap3(ab.ap(aoin_o[cp], 1), [(512, 16), (1, 512)])
                aos = _dram_ap(aot_s, [(CH, 128), (128 * CH, 16), (1, CH)],
                               c * 16 * 128 * CH)
                sch.op(POOL, lambda g_, d=aod, s=aos: g_.dma_start(out=d, in_=s),
                       writes=[aoin_b[cp]], dreads=[DB["aot"]], dma=True,
                       tag=f"ai_{c}")
                x4d = _ap3(af.ap(x4f_o[0], 1), [(2048, 4), (1, 2048)])
                xsrc = _dram_ap(x_in, [(E, 128), (128 * E, 4), (1, E)],
                                c * 4 * 128 * E)
                sch.op(POOL, lambda g_, d=x4d, s=xsrc: g_.dma_start(out=d, in_=s),
                       writes=[x4f_b[0]], dma=True, tag=f"x4_{c}")
                for ts in range(4):
                    for dmc in range(4):
                        par = dmc % 2
                        psl = ps_w[:, par * 512:(par + 1) * 512]
                        for hdc in range(16):
                            lhs = ab.ap(aoin_o[cp] + hdc * 512 + ts * 128, 128)
                            rhs = ab.ap(wo_o + hdc * 2048 + dmc * 512, 512)
                            sch.op(PE, lambda t, l=lhs, r2=rhs, p=psl,
                                   st=(hdc == 0), sp=(hdc == 15):
                                   t.matmul(p, l, r2, start=st, stop=sp),
                                   reads=[aoin_b[cp], wo_b], writes=[psw_b[par]],
                                   tag="wo_mm")
                        xsl = af.ap(x4f_o[0] + ts * 2048 + dmc * 512, 512)
                        rsl = af.ap(r_o + dmc * 512, 512)
                        sch.op(DVE, lambda v, p=psl, xs=xsl, d=rsl:
                               v.scalar_tensor_tensor(d, p, 1.0, xs, ALU.mult,
                                                      ALU.add),
                               reads=[psw_b[par], x4f_b[0]], writes=[r_b],
                               tag="wo_res")
                    if _DEBUG:
                        rdst = _dram_ap(dbg_r, [(E, 128), (1, E)],
                                        (c * 4 + ts) * 128 * E)
                        sch.op(SP, lambda q, d=rdst, s=af.ap(r_o, 2048):
                               q.dma_start(out=d, in_=s),
                               reads=[r_b], dma=True, tag="dbg_r")
                    x1t = ab.ap(x1b_o[ts % 2], 2048)
                    layer_norm(af.ap(r_o, 2048), r_b, x1t, x1b_b[ts % 2], "ln1")
                    if _DEBUG:
                        sdst = _dram_ap(dbg_st, [(28, 128), (1, 28)],
                                        (c * 4 + ts) * 128 * 28)
                        sch.op(SP, lambda q, d=sdst, s=af.ap(st_o, 28):
                               q.dma_start(out=d, in_=s),
                               reads=[st_b], dma=True, tag="dbg_st")
                    x1dst = _dram_ap(x1_s, [(E, 128), (1, E)],
                                     (c * 4 + ts) * 128 * E)
                    sch.op(SP, lambda q, s=x1t, d=x1dst: q.dma_start(out=d, in_=s),
                           reads=[x1b_b[ts % 2]], dwrites=[DB["x1"]], dma=True,
                           tag="x1_st")
                    pst = pstr_t[ts % 2]
                    for k in range(16):
                        sch.op(PE, (lambda kk, xv=x1t, p=pst: (lambda t:
                               t.transpose(p[:, kk * 128:(kk + 1) * 128],
                                           xv[:, kk * 128:(kk + 1) * 128],
                                           ident())))(k),
                            reads=[x1b_b[ts % 2], CB["c_ident"]],
                            writes=[ps_trb[ts % 2]], tag="x1_tr")
                    ex1 = ab.ap(ex1_o[ts % 2], 2048)
                    sch.op(DVE, lambda v, p=pst, d=ex1: v.tensor_copy(d, p[:]),
                           reads=[ps_trb[ts % 2]], writes=[ex1_b[ts % 2]],
                           tag="x1_ev")
                    xtd = _dram_ap(x1t_s, [(S, 128), (128 * S, 16), (1, 128)],
                                   c * 512 + ts * 128)
                    es = _ap3(ab.ap(ex1_o[ts % 2], 1), [(128, 16), (1, 128)])
                    sch.op(SP, lambda q, s=es, d=xtd: q.dma_start(out=d, in_=s),
                           reads=[ex1_b[ts % 2]], dwrites=[DB["x1t"]], dma=True,
                           tag="x1t_st")

        sch.set_fence()
        ab.reset()
        af.reset()

        with ExitStack() as pctx:
            ps_u1 = pctx.enter_context(nc.psum_tensor("p4_u1", [128, 1024], F32))
            ps_u3 = pctx.enter_context(nc.psum_tensor("p4_u3", [128, 1024], F32))
            ps_d = pctx.enter_context(nc.psum_tensor("p4_d", [128, 2048], F32))
            psu1_b = [sch.buf("u10"), sch.buf("u11")]
            psu3_b = [sch.buf("u30"), sch.buf("u31")]
            psd_b = [sch.buf(f"pd{i}") for i in range(4)]

            x1tin_o = ab.take(16 * 512)
            x1tin_b = sch.buf("x1tin")
            x1tok_o = ab.take(4 * 2048)
            x1tok_b = sch.buf("x1tok")
            w1g_o = [ab.take(4096), ab.take(4096)]
            w3g_o = [ab.take(4096), ab.take(4096)]
            w1g_b = [sch.buf("w1g0"), sch.buf("w1g1")]
            w3g_b = [sch.buf("w3g0"), sch.buf("w3g1")]
            w2g_o = [ab.take(4096), ab.take(4096)]
            w2g_b = [sch.buf("w2g0"), sch.buf("w2g1")]
            gt_o = ab.take(64 * 512)
            gt_b = sch.buf("gt")
            s1_o = [af.take(512), af.take(512)]
            s1_b = [sch.buf("s10"), sch.buf("s11")]
            r4_o = [af.take(2048) for _ in range(4)]
            r4_b = [sch.buf(f"r4_{i}") for i in range(4)]
            outf_o = af.take(2048)
            outf_b = sch.buf("outf")
            st_o = af.take(28)
            st_b = sch.buf("st4")

            for p in range(NCH):
                x1td = _ap3(ab.ap(x1tin_o, 1), [(512, 16), (1, 512)])
                x1src = _dram_ap(x1t_s, [(S, 128), (128 * S, 16), (1, 512)],
                                 p * 512)
                sch.op(POOL, lambda g_, d=x1td, s=x1src: g_.dma_start(out=d, in_=s),
                       writes=[x1tin_b], dreads=[DB["x1t"]], dma=True,
                       tag=f"fx1t_{p}")
                xtk = _ap3(ab.ap(x1tok_o, 1), [(2048, 4), (1, 2048)])
                xts = _dram_ap(x1_s, [(E, 128), (128 * E, 4), (1, E)],
                               p * 4 * 128 * E)
                sch.op(POOL, lambda g_, d=xtk, s=xts: g_.dma_start(out=d, in_=s),
                       writes=[x1tok_b], dreads=[DB["x1"]], dma=True,
                       tag=f"fx1_{p}")

                def load_wgroup(g, par):
                    for (wsrc, woff, wbuf) in [(w1t_s, w1g_o, w1g_b),
                                               (w3t_s, w3g_o, w3g_b)]:
                        d = _ap3(ab.ap(woff[par], 1), [(256, 16), (1, 256)])
                        s = _dram_ap(wsrc, [(HF, 128), (128 * HF, 16), (1, 256)],
                                     g * 256)
                        sch.op(SP, lambda q, d2=d, s2=s: q.dma_start(out=d2, in_=s2),
                               writes=[wbuf[par]],
                               dreads=[DB["w1t"] if wsrc is w1t_s else DB["w3t"]],
                               dma=True, tag=f"wg_{p}_{g}")

                load_wgroup(0, 0)
                for g in range(32):
                    par = g % 2
                    if g + 1 < 32:
                        load_wgroup(g + 1, (g + 1) % 2)
                    for hh in range(2):
                        hc = g * 2 + hh
                        up = hc % 2
                        psl1 = ps_u1[:, up * 512:(up + 1) * 512]
                        psl3 = ps_u3[:, up * 512:(up + 1) * 512]
                        for dc in range(16):
                            l1 = ab.ap(w1g_o[par] + dc * 256 + hh * 128, 128)
                            l3 = ab.ap(w3g_o[par] + dc * 256 + hh * 128, 128)
                            rr = ab.ap(x1tin_o + dc * 512, 512)
                            sch.op(PE, lambda t, l=l1, r=rr, p2=psl1, st=(dc == 0),
                                   sp=(dc == 15): t.matmul(p2, l, r, start=st,
                                                           stop=sp),
                                   reads=[w1g_b[par], x1tin_b],
                                   writes=[psu1_b[up]], tag="u1_mm")
                            sch.op(PE, lambda t, l=l3, r=rr, p2=psl3, st=(dc == 0),
                                   sp=(dc == 15): t.matmul(p2, l, r, start=st,
                                                           stop=sp),
                                   reads=[w3g_b[par], x1tin_b],
                                   writes=[psu3_b[up]], tag="u3_mm")
                        s1 = af.ap(s1_o[up], 512)
                        sch.op(ACT, lambda a, d=s1, p2=psl1: a.activation(
                            d, p2, AF.Silu),
                            reads=[psu1_b[up]], writes=[s1_b[up]], tag="silu")
                        gsl = ab.ap(gt_o + hc * 512, 512)
                        sch.op(DVE, lambda v, d=gsl, a2=s1, p2=psl3:
                               v.tensor_mul(d, a2, p2),
                               reads=[s1_b[up], psu3_b[up]], writes=[gt_b],
                               tag="g_mul")
                for dmc in range(4):
                    def load_w2g(hb, par):
                        d = _ap3(ab.ap(w2g_o[par], 1), [(512, 8), (1, 512)])
                        s = _dram_ap(w2f_s, [(E, 128), (128 * E, 8), (1, 512)],
                                     hb * 8 * 128 * E + dmc * 512)
                        sch.op(SP, lambda q, d2=d, s2=s: q.dma_start(out=d2,
                                                                     in_=s2),
                               writes=[w2g_b[par]], dreads=[DB["w2f"]], dma=True,
                               tag=f"w2g_{p}_{dmc}_{hb}")

                    load_w2g(0, 0)
                    for hb in range(8):
                        par = hb % 2
                        if hb + 1 < 8:
                            load_w2g(hb + 1, (hb + 1) % 2)
                        for hh in range(8):
                            hc = hb * 8 + hh
                            rhs = ab.ap(w2g_o[par] + hh * 512, 512)
                            for ts in range(4):
                                lhs = ab.ap(gt_o + hc * 512 + ts * 128, 128)
                                psl = ps_d[:, ts * 512:(ts + 1) * 512]
                                sch.op(PE, lambda t, l=lhs, r=rhs, p2=psl,
                                       st=(hc == 0), sp=(hc == 63):
                                       t.matmul(p2, l, r, start=st, stop=sp),
                                       reads=[gt_b, w2g_b[par]],
                                       writes=[psd_b[ts]], tag="dn_mm")
                    for ts in range(4):
                        psl = ps_d[:, ts * 512:(ts + 1) * 512]
                        xsl = ab.ap(x1tok_o + ts * 2048 + dmc * 512, 512)
                        rsl = af.ap(r4_o[ts] + dmc * 512, 512)
                        sch.op(DVE, lambda v, p2=psl, xs=xsl, d=rsl:
                               v.scalar_tensor_tensor(d, p2, 1.0, xs, ALU.mult,
                                                      ALU.add),
                               reads=[psd_b[ts], x1tok_b], writes=[r4_b[ts]],
                               tag="dn_res")
                for ts in range(4):
                    rr = af.ap(r4_o[ts], 2048)
                    of = af.ap(outf_o, 2048)
                    bn24 = af.ap(st_o, 24)
                    bn2 = af.ap(st_o + 24, 2)
                    for i in range(4):
                        bno = af.ap(st_o + i * 6, 6)
                        bni = _ap3(rr, [(1, 512)], i * 512)
                        sch.op(DVE, (lambda o2, i2: (lambda v: v.bn_stats(
                            o2, i2)))(bno, bni),
                            reads=[r4_b[ts]], writes=[st_b], tag="ln2_bns")
                    sch.op(DVE, lambda v: v.bn_aggr(bn2, bn24),
                           reads=[st_b], writes=[st_b], tag="ln2_bna")
                    mean = af.ap(st_o + 24, 1)
                    var = af.ap(st_o + 25, 1)
                    sd = af.ap(st_o + 26, 1)
                    sch.op(ACT, lambda a: a.activation(sd, var, AF.Sqrt,
                                                       bias=epsln_ap()),
                           reads=[st_b, epsln_b], writes=[st_b], tag="ln2_sd")
                    rs = af.ap(st_o + 27, 1)
                    sch.op(DVE, lambda v: v.reciprocal(rs, sd),
                           reads=[st_b], writes=[st_b], tag="ln2_rs")
                    sch.op(DVE, lambda v, s=rr, d=of: v.tensor_scalar(
                        d, s, mean, rs, ALU.subtract, ALU.mult),
                        reads=[r4_b[ts], st_b], writes=[outf_b], tag="ln2_out")
                    odst = _dram_ap(out_t, [(E, 128), (1, E)],
                                    (p * 4 + ts) * 128 * E)
                    sch.op(ACT, lambda a, s=of, d=odst: a.dma_start(out=d, in_=s),
                           reads=[outf_b], dma=True, tag="out_st")

        finals = dict(sch.counts)
        sems = {}
        for name in _ALL_SEMS:
            sems[name] = ctx.enter_context(nc.semaphore(name))
        block = ctx.enter_context(nc.Block())
        sch.emit(block, sems)

        @block.sync
        def _(sync):
            for qn in (Q_ACT, Q_SP, Q_POOL):
                if finals[qn]:
                    sync.wait_ge(sems[qn], finals[qn])

    return nc


_NC_CACHE = None


def _get_nc():
    global _NC_CACHE
    if _NC_CACHE is None:
        _NC_CACHE = build_nc()
    return _NC_CACHE


def kernel(L0_attn_hidden_states, L0_attn_wq, L0_attn_wk, L0_attn_wv, L0_attn_wo,
           L0_attn_query_scale, L0_attn_key_scale, L0_ffn_W1, L0_ffn_W2, L0_ffn_W3):
    consts = host_constants()
    nc = _get_nc()
    xs = np.ascontiguousarray(np.asarray(L0_attn_hidden_states, dtype=np.float32))
    shared = {
        "wq": np.ascontiguousarray(np.asarray(L0_attn_wq, np.float32)),
        "wk": np.ascontiguousarray(np.asarray(L0_attn_wk, np.float32)),
        "wv": np.ascontiguousarray(np.asarray(L0_attn_wv, np.float32)),
        "wo": np.ascontiguousarray(np.asarray(L0_attn_wo, np.float32)),
        "qscale": np.ascontiguousarray(np.asarray(L0_attn_query_scale, np.float32)),
        "kscale": np.ascontiguousarray(np.asarray(L0_attn_key_scale, np.float32)),
        "w1": np.ascontiguousarray(np.asarray(L0_ffn_W1, np.float32)),
        "w2": np.ascontiguousarray(np.asarray(L0_ffn_W2, np.float32)),
        "w3": np.ascontiguousarray(np.asarray(L0_ffn_W3, np.float32)),
    }
    shared.update(consts)
    in_maps = []
    for b in range(B):
        m = dict(shared)
        m["x"] = np.ascontiguousarray(xs[b])
        in_maps.append(m)
    res = run_bass_kernel_spmd(nc, in_maps, core_ids=list(range(B)))
    out = np.stack([r["out"] for r in res.results], axis=0)
    return out


# version 02
# speedup vs baseline: 1.0304x; 1.0304x over previous
import dataclasses
import numpy as np
import ml_dtypes
import concourse.bass as bass
import concourse.mybir as mybir
from concourse.bass_utils import run_bass_kernel_spmd
from contextlib import ExitStack

F32 = mybir.dt.float32
BF16 = mybir.dt.bfloat16
AF = mybir.ActivationFunctionType
ALU = mybir.AluOpType
AX = mybir.AxisListType

B = 8
S = 4096
E = 2048
NH = 16
NKV = 4
D = 128
CAP = 50.0
EPS_NRM = 1e-6
EPS_LN = 1e-5
HF = 8192
CH = 512
NCH = S // CH
NT = S // 128

_DEBUG = False


def _sylvester(n):
    H = np.array([[1.0]], dtype=np.float64)
    while H.shape[0] < n:
        H = np.block([[H, H], [H, -H]])
    return H


def host_constants():
    c = {}
    c["c_ident"] = np.eye(128, dtype=np.float32).astype(ml_dtypes.bfloat16)
    P = np.zeros((128, 128), np.float32)
    for i in range(64):
        P[i + 64, i] = -1.0
        P[i, i + 64] = 1.0
    c["c_p128"] = P.astype(ml_dtypes.bfloat16)
    c["c_ones"] = np.ones((128, 128), np.float32).astype(ml_dtypes.bfloat16)
    c["c_h64"] = (_sylvester(64) / 64.0).astype(ml_dtypes.bfloat16)
    c["c_h128"] = (_sylvester(128) / np.sqrt(2.0)).astype(ml_dtypes.bfloat16)
    kl = np.arange(128)[:, None]
    ql = np.arange(128)[None, :]
    c["c_maskhi"] = (kl <= ql).astype(np.float32).astype(ml_dtypes.bfloat16)
    c["c_masklo"] = (ql <= kl).astype(np.float32).astype(ml_dtypes.bfloat16)
    half = D // 2
    inv = 1.0 / 10000.0 ** (np.arange(half, dtype=np.float32) / half)
    ang = np.outer(np.arange(S, dtype=np.float32), inv)
    ang = np.concatenate([ang, ang], axis=-1)
    c["c_cosT"] = np.ascontiguousarray(np.cos(ang).T.astype(np.float32))
    c["c_sinT"] = np.ascontiguousarray(np.sin(ang).T.astype(np.float32))
    return c


PE, ACT, DVE, POOL, SP = "pe", "act", "dve", "pool", "sp"
Q_POOL, Q_SP, Q_ACT = "q_pool", "q_sp", "q_act"
_DMA_Q = {POOL: Q_POOL, SP: Q_SP, ACT: Q_ACT}
_ALL_SEMS = (PE, ACT, DVE, POOL, SP, Q_POOL, Q_SP, Q_ACT)


class Buf:
    __slots__ = ("writer", "readers", "name")

    def __init__(self, name, fence):
        self.writer = None
        self.readers = list(fence)
        self.name = name


class OpRec:
    __slots__ = ("build", "waits", "sem", "amount", "tag")

    def __init__(self, build, waits, sem, amount, tag):
        self.build = build
        self.waits = waits
        self.sem = sem
        self.amount = amount
        self.tag = tag


class DramBuf:
    __slots__ = ("writers", "name")

    def __init__(self, name):
        self.writers = []
        self.name = name


class Sched:
    def __init__(self):
        self.ops = {k: [] for k in (PE, ACT, DVE, POOL, SP)}
        self.counts = {k: 0 for k in _ALL_SEMS}
        self.fence = []

    def buf(self, name):
        return Buf(name, self.fence)

    def set_fence(self):
        self.fence = [(k, v) for k, v in self.counts.items() if v > 0]

    def op(self, eng, build, reads=(), writes=(), dreads=(), dwrites=(),
           dma=False, drain=False, tag=""):
        sem = _DMA_Q[eng] if dma else eng
        amount = 16 if dma else 1
        waits = {}

        def add(tok):
            if tok is None:
                return
            s, v = tok
            if (not dma) and eng == PE and s == PE:
                return
            if waits.get(s, 0) < v:
                waits[s] = v

        for b in reads:
            add(b.writer)
        for b in writes:
            add(b.writer)
            for t in b.readers:
                add(t)
        for db in dreads:
            for t in db.writers:
                add(t)
        self.counts[sem] += amount
        tok = (sem, self.counts[sem])
        self.ops[eng].append(OpRec(build, waits, sem, amount, tag))
        for b in reads:
            b.readers.append(tok)
        for b in writes:
            b.writer = tok
            b.readers = []
        for db in dwrites:
            db.writers.append(tok)
        if drain:
            assert dma
            self.ops[eng].append(OpRec(
                lambda e: e.nop(), {sem: self.counts[sem]}, eng, 1, tag + "_drain"))
            self.counts[eng] += 1
        return tok

    def emit(self, block, sems):
        eng_of = {PE: "tensor", ACT: "scalar", DVE: "vector", POOL: "gpsimd",
                  SP: "sync"}
        for eng_name, ops in self.ops.items():
            if not ops:
                continue
            deco = getattr(block, eng_of[eng_name])

            def make(ops_local):
                def _(eng):
                    waited = {}
                    for rec in ops_local:
                        for s, v in sorted(rec.waits.items()):
                            if waited.get(s, 0) >= v:
                                continue
                            eng.wait_ge(sems[s], v)
                            waited[s] = v
                        rec.build(eng).then_inc(sems[rec.sem], rec.amount)
                return _

            deco(make(ops))


class Arena:
    def __init__(self, nc, name, dtype, n_elems):
        self.t = nc.alloc_sbuf_tensor(name, [128, n_elems], dtype)
        self.off = 0
        self.n = n_elems
        self.mark = 0

    def take(self, elems):
        o = self.off
        self.off += elems
        assert self.off <= self.n, f"{self.t} arena overflow {self.off}>{self.n}"
        return o

    def set_mark(self):
        self.mark = self.off

    def reset(self):
        self.off = self.mark

    def ap(self, off, elems, parts=128):
        return self.t[0:parts, off:off + elems]


def _ap3(ap, dims, offset_elems=0):
    part_dim = list(ap.ap[0])
    new_dims = [part_dim] + [[s, c] for (s, c) in dims]
    return dataclasses.replace(ap, offset=ap.offset + offset_elems, ap=new_dims)


def _dram_ap(t, dims, offset_elems=0):
    full = t.ap()
    return dataclasses.replace(full, offset=full.offset + offset_elems,
                               ap=[[s, c] for (s, c) in dims])


def build_nc():
    nc = bass.Bass("TRN2", target_bir_lowering=False)

    x_in = nc.dram_tensor("x", [S, E], F32, kind="ExternalInput")
    wq_in = nc.dram_tensor("wq", [E, NH * D], F32, kind="ExternalInput")
    wk_in = nc.dram_tensor("wk", [E, NKV * D], F32, kind="ExternalInput")
    wv_in = nc.dram_tensor("wv", [E, NKV * D], F32, kind="ExternalInput")
    wo_in = nc.dram_tensor("wo", [NH * D, E], F32, kind="ExternalInput")
    qs_in = nc.dram_tensor("qscale", [D], F32, kind="ExternalInput")
    ks_in = nc.dram_tensor("kscale", [D], F32, kind="ExternalInput")
    w1_in = nc.dram_tensor("w1", [HF, E], F32, kind="ExternalInput")
    w2_in = nc.dram_tensor("w2", [E, HF], F32, kind="ExternalInput")
    w3_in = nc.dram_tensor("w3", [HF, E], F32, kind="ExternalInput")
    cin = {}
    for name, shape, dt in [
        ("c_ident", [128, 128], BF16), ("c_p128", [128, 128], BF16),
        ("c_ones", [128, 128], BF16), ("c_h64", [64, 64], BF16),
        ("c_h128", [128, 128], BF16), ("c_maskhi", [128, 128], BF16),
        ("c_masklo", [128, 128], BF16), ("c_cosT", [128, S], F32),
        ("c_sinT", [128, S], F32),
    ]:
        cin[name] = nc.dram_tensor(name, shape, dt, kind="ExternalInput")
    out_t = nc.dram_tensor("out", [S, E], F32, kind="ExternalOutput")

    skind = "ExternalOutput" if _DEBUG else "Internal"
    w1t_s = nc.dram_tensor("w1t_s", [16, 128, HF], BF16, kind="Internal")
    w3t_s = nc.dram_tensor("w3t_s", [16, 128, HF], BF16, kind="Internal")
    b_s = nc.dram_tensor("b_s", [64, 128, E], BF16, kind=skind)
    w2f_s = nc.dram_tensor("w2f_s", [64, 128, E], BF16, kind=skind)
    qt_s = nc.dram_tensor("qt_s", [NCH, NH, 128, CH], BF16, kind=skind)
    kt_s = nc.dram_tensor("kt_s", [NCH, NKV, 128, CH], BF16, kind=skind)
    v_s = nc.dram_tensor("v_s", [NCH, 4, 128, 516], BF16, kind=skind)
    aot_s = nc.dram_tensor("aot_s", [NCH, 16, 128, CH], BF16, kind=skind)
    x1_s = nc.dram_tensor("x1_s", [NT, 128, E], BF16, kind=skind)
    x1t_s = nc.dram_tensor("x1t_s", [16, 128, S], BF16, kind=skind)
    dbg_r = (nc.dram_tensor("dbg_r", [NT, 128, E], F32, kind="ExternalOutput")
             if _DEBUG else None)
    dbg_st = (nc.dram_tensor("dbg_st", [NT, 128, 28], F32, kind="ExternalOutput")
              if _DEBUG else None)

    sch = Sched()
    DB = {n: DramBuf(n) for n in ("w1t", "w3t", "b", "w2f", "qt", "kt", "v",
                                  "aot", "x1", "x1t")}

    with ExitStack() as ctx:
        ab = Arena(nc, "AB", BF16, 80000)
        af = Arena(nc, "AF", F32, 13000)

        co = {}
        CB = {}
        for name, elems, parts in [
            ("c_ident", 128, 128), ("c_p128", 128, 128), ("c_ones", 128, 128),
            ("c_h64", 64, 64), ("c_h128", 128, 128), ("c_maskhi", 128, 128),
            ("c_masklo", 128, 128),
        ]:
            co[name] = ab.take(elems)
            CB[name] = sch.buf(name)
            dst = ab.ap(co[name], elems, parts)
            src = cin[name].ap()
            sch.op(POOL, (lambda d, s: (lambda g: g.dma_start(out=d, in_=s)))(dst, src),
                   writes=[CB[name]], dma=True, drain=(parts != 128),
                   tag=f"ld_{name}")
        ident = lambda: ab.ap(co["c_ident"], 128)
        p128 = lambda: ab.ap(co["c_p128"], 128)
        ones = lambda: ab.ap(co["c_ones"], 128)
        h64 = lambda: ab.ap(co["c_h64"], 64, 64)
        h128 = lambda: ab.ap(co["c_h128"], 128)
        maskhi = lambda: ab.ap(co["c_maskhi"], 128)
        masklo = lambda: ab.ap(co["c_masklo"], 128)

        kc_o = af.take(2)
        kc_b = sch.buf("kc")
        qs_dst = af.ap(kc_o, 1)
        ks_dst = af.ap(kc_o + 1, 1)
        sch.op(POOL, lambda g: g.dma_start(
            out=qs_dst, in_=_dram_ap(qs_in, [(1, 128), (1, 1)]),
            allow_slow_non_contiguous=True), writes=[kc_b], dma=True, drain=True, tag="ld_qs")
        sch.op(POOL, lambda g: g.dma_start(
            out=ks_dst, in_=_dram_ap(ks_in, [(1, 128), (1, 1)]),
            allow_slow_non_contiguous=True), writes=[kc_b], dma=True, drain=True, tag="ld_ks")
        sch.op(DVE, lambda v: v.tensor_mul(qs_dst, qs_dst, ks_dst),
               reads=[kc_b], writes=[kc_b], tag="kc_mul")
        sch.op(DVE, lambda v: v.tensor_scalar_mul(
            qs_dst, qs_dst, float(1.0 / (CAP * np.sqrt(D)))),
            reads=[kc_b], writes=[kc_b], tag="kc_sc")
        kc_ap = lambda: af.ap(kc_o, 1)

        epsln_o = af.take(1)
        epsln_b = sch.buf("epsln")
        sch.op(DVE, lambda v: v.memset(af.ap(epsln_o, 1), EPS_LN),
               writes=[epsln_b], tag="eps_set")
        epsln_ap = lambda: af.ap(epsln_o, 1)

        ab.set_mark()
        af.set_mark()

        with ExitStack() as pctx:
            ps_tr = pctx.enter_context(nc.psum_tensor("p0_tr", [128, 2048], BF16))
            ps_tr2 = pctx.enter_context(nc.psum_tensor("p0_tr2", [128, 2048], BF16))
            ps_f = pctx.enter_context(nc.psum_tensor("p0_f", [64, 1024], F32))
            ps_g = pctx.enter_context(nc.psum_tensor("p0_g", [128, 1024], F32))
            pstr_t = [ps_tr, ps_tr2]
            ps_trb = [sch.buf("p0tr0"), sch.buf("p0tr1")]
            psf_b = [sch.buf("p0f0"), sch.buf("p0f1")]
            psg_b = [sch.buf("p0g0"), sch.buf("p0g1")]

            slabf_o = [af.take(2048), af.take(2048)]
            slabf_b = [sch.buf("sf0"), sch.buf("sf1")]
            slabb_o = [ab.take(2048), ab.take(2048)]
            evacb_o = [ab.take(2048), ab.take(2048)]
            slabb_b = [sch.buf("sb0"), sch.buf("sb1")]
            evacb_b = [sch.buf("ev0"), sch.buf("ev1")]

            def slab_load(w_dram, hc, par):
                sf = af.ap(slabf_o[par], 2048)
                src = _dram_ap(w_dram, [(2048, 128), (1, 2048)], hc * 128 * 2048)
                sch.op(POOL, lambda g, s=src, d=sf: g.dma_start(out=d, in_=s),
                       writes=[slabf_b[par]], dma=True, tag=f"w_ld_{hc}")

            for wi, (w_dram, wt_dram) in enumerate([(w1_in, w1t_s), (w3_in, w3t_s)]):
                for hc in range(64):
                    gidx = wi * 64 + hc
                    if gidx == 0:
                        slab_load(w1_in, 0, 0)
                    nxt = gidx + 1
                    if nxt < 128:
                        slab_load(w1_in if nxt < 64 else w3_in, nxt % 64, nxt % 2)
                    par = gidx % 2
                    sf = af.ap(slabf_o[par], 2048)
                    sb = ab.ap(slabb_o[par], 2048)
                    sch.op(DVE, lambda v, s=sf, d=sb: v.tensor_copy(d, s),
                           reads=[slabf_b[par]], writes=[slabb_b[par]],
                           tag="w_cvt")
                    ev = ab.ap(evacb_o[gidx % 2], 2048)
                    pst = pstr_t[gidx % 2]
                    for k in range(16):
                        sch.op(PE, (lambda kk, sbv=sb, p=pst: (lambda t: t.transpose(
                            p[:, kk * 128:(kk + 1) * 128],
                            sbv[:, kk * 128:(kk + 1) * 128], ident())))(k),
                            reads=[slabb_b[par], CB["c_ident"]],
                            writes=[ps_trb[gidx % 2]], tag="w_tr")
                    sch.op(DVE, lambda v, p=pst, d=ev: v.tensor_copy(d, p[:]),
                           reads=[ps_trb[gidx % 2]], writes=[evacb_b[gidx % 2]],
                           tag="w_ev")
                    dst = _dram_ap(wt_dram, [(HF, 128), (128 * HF, 16), (1, 128)],
                                   hc * 128)
                    esrc = _ap3(ev, [(128, 16), (1, 128)])
                    sch.op(SP, lambda q, s=esrc, d=dst: q.dma_start(out=d, in_=s),
                           reads=[evacb_b[gidx % 2]],
                           dwrites=[DB["w1t"] if wi == 0 else DB["w3t"]],
                           dma=True, tag="w_st")

            t2f_o = [af.take(4096), af.take(4096)]
            t2f_b = [sch.buf("t2f0"), sch.buf("t2f1")]
            t2b_o = [ab.take(4096), ab.take(4096)]
            bb_o = [ab.take(4096), ab.take(4096)]
            t2b_b = [sch.buf("t2b0"), sch.buf("t2b1")]
            bb_b = [sch.buf("bb0"), sch.buf("bb1")]

            def f1_load(g):
                tf2 = af.ap(t2f_o[g % 2], 4096, 64)
                s2 = _dram_ap(w2_in, [(128, 64), (8192, 32), (1, 128)],
                              g * 32 * 8192)
                sch.op(POOL, lambda g_, s=s2, d=tf2: g_.dma_start(out=d, in_=s),
                       writes=[t2f_b[g % 2]], dma=True, tag=f"f1_ld_{g}")

            f1_load(0)
            for g in range(64):
                par = g % 2
                if g + 1 < 64:
                    f1_load(g + 1)
                tf = af.ap(t2f_o[par], 4096, 64)
                tb = ab.ap(t2b_o[par], 4096, 64)
                sch.op(DVE, lambda v, s=tf, d=tb: v.tensor_copy(d, s),
                       reads=[t2f_b[par]], writes=[t2b_b[par]], tag="f1_cvt")
                bbv = ab.ap(bb_o[g % 2], 4096, 64)
                for m in range(8):
                    mp = m % 2
                    rhs = _ap3(tb, [(1, 16), (128, 32)], m * 16)
                    psl = ps_f[:, mp * 512:(mp + 1) * 512]
                    sch.op(PE, lambda t, r=rhs, p=psl: t.matmul(
                        p, h64(), r, start=True, stop=True),
                        reads=[t2b_b[par], CB["c_h64"]], writes=[psf_b[mp]],
                        tag="f1_mm")
                    dstb = _ap3(bbv, [(1, 512)], m * 512)
                    sch.op(DVE, lambda v, p=psl, d=dstb: v.tensor_copy(d, p),
                           reads=[psf_b[mp]], writes=[bb_b[g % 2]], tag="f1_ev")
                dst = _dram_ap(b_s, [(128 * E, 64), (E, 128), (1, 32)], g * 32)
                bsrc = _ap3(bbv, [(32, 128), (1, 32)])
                sch.op(SP, lambda q, s=bsrc, d=dst: q.dma_start(out=d, in_=s),
                       reads=[bb_b[g % 2]], dwrites=[DB["b"]], dma=True,
                       tag=f"f1_st_{g}")

            bslab_o = [ab.take(2048) for _ in range(4)]
            w2fe_o = [ab.take(2048), ab.take(2048)]
            bslab_b = [sch.buf(f"bs{i}") for i in range(4)]
            w2fe_b = [sch.buf("we0"), sch.buf("we1")]

            def f2_load(ip):
                bsl2 = ab.ap(bslab_o[ip % 4], 2048)
                s2 = _dram_ap(b_s, [(E, 128), (1, E)], ip * 128 * E)
                sch.op(POOL, lambda g_, s=s2, d=bsl2: g_.dma_start(out=d, in_=s),
                       writes=[bslab_b[ip % 4]], dreads=[DB["b"]], dma=True,
                       tag=f"f2_ld_{ip}")

            for j in range(3):
                f2_load(j)
            for ip in range(64):
                par = ip % 4
                if ip + 3 < 64:
                    f2_load(ip + 3)
                bsl = ab.ap(bslab_o[par], 2048)
                wfe = ab.ap(w2fe_o[ip % 2], 2048)
                for dc in range(4):
                    dp = dc % 2
                    psl = ps_g[:, dp * 512:(dp + 1) * 512]
                    sch.op(PE, lambda t, p=psl, r=bsl[:, dc * 512:(dc + 1) * 512]:
                           t.matmul(p, h128(), r, start=True, stop=True),
                           reads=[bslab_b[par], CB["c_h128"]], writes=[psg_b[dp]],
                           tag="f2_mm")
                    sch.op(DVE, lambda v, p=psl, d=wfe[:, dc * 512:(dc + 1) * 512]:
                           v.tensor_copy(d, p),
                           reads=[psg_b[dp]], writes=[w2fe_b[ip % 2]], tag="f2_ev")
                dst = _dram_ap(w2f_s, [(E, 128), (1, E)], ip * 128 * E)
                sch.op(SP, lambda q, s=wfe, d=dst: q.dma_start(out=d, in_=s),
                       reads=[w2fe_b[ip % 2]], dwrites=[DB["w2f"]], dma=True,
                       tag=f"f2_st_{ip}")

        sch.set_fence()
        ab.reset()
        af.reset()

        with ExitStack() as pctx:
            ps_tr = pctx.enter_context(nc.psum_tensor("p1_tr", [128, 2048], BF16))
            ps_tr2 = pctx.enter_context(nc.psum_tensor("p1_tr2", [128, 2048], BF16))
            ps_a = pctx.enter_context(nc.psum_tensor("p1_a", [128, 1024], F32))
            ps_r = pctx.enter_context(nc.psum_tensor("p1_r", [128, 512], F32))
            ps_s = pctx.enter_context(nc.psum_tensor("p1_s", [128, 512], F32))
            pstr_t = [ps_tr, ps_tr2]
            ps_trb = [sch.buf("p1tr0"), sch.buf("p1tr1")]
            psa_b = [sch.buf("p1a0"), sch.buf("p1a1")]
            psr_b = sch.buf("p1r")
            pss_b = sch.buf("p1s")

            wq_o = ab.take(16 * 2048)
            wk_o = ab.take(16 * 512)
            wv_o = ab.take(16 * 512)
            wq_b = sch.buf("wq"); wk_b = sch.buf("wk"); wv_b = sch.buf("wv")
            xf_o = [af.take(2048), af.take(2048)]
            xf_b = [sch.buf("xf0"), sch.buf("xf1")]
            wstg_o = xf_o
            wstg_b = xf_b
            idx = 0
            for (wt, off, width, buf) in [(wq_in, wq_o, 2048, wq_b),
                                          (wk_in, wk_o, 512, wk_b),
                                          (wv_in, wv_o, 512, wv_b)]:
                for dm in range(16):
                    par = idx % 2
                    idx += 1
                    stg = af.ap(wstg_o[par], width)
                    src = _dram_ap(wt, [(width, 128), (1, width)], dm * 128 * width)
                    sch.op(POOL, lambda g_, s=src, d=stg: g_.dma_start(out=d, in_=s),
                           writes=[wstg_b[par]], dma=True, tag="wqkv_ld")
                    dstb = ab.ap(off + dm * width, width)
                    sch.op(DVE, lambda v, s=stg, d=dstb: v.tensor_copy(d, s),
                           reads=[wstg_b[par]], writes=[buf], tag="wqkv_cvt")

            xb_o = [ab.take(2048), ab.take(2048)]
            xb_b = [sch.buf("xb0"), sch.buf("xb1")]
            xt_o = ab.take(16 * 512)
            xt_b = sch.buf("xt")
            cs_o = [af.take(512), af.take(512)]
            sn_o = [af.take(512), af.take(512)]
            cs_b = [sch.buf("cs0"), sch.buf("cs1")]
            sn_b = [sch.buf("sn0"), sch.buf("sn1")]
            qbf_o = [ab.take(512), ab.take(512)]
            qbf_b = [sch.buf("qb0"), sch.buf("qb1")]
            t1_o = af.take(512); t2_o = af.take(512); qp_o = af.take(512)
            t1_b = sch.buf("t1"); t2_b = sch.buf("t2"); qp_b = sch.buf("qp")
            sqb_o = ab.take(512); sqb_b = sch.buf("sqb")
            nrm_o = af.take(512); nrm_b = sch.buf("nrm")
            rcp_o = af.take(512); rcp_b = sch.buf("rcp")
            kf_o = af.take(512); kf_b = sch.buf("kf")
            qtc_o = ab.take(16 * 512); qtc_b = sch.buf("qtc")
            ktc_o = ab.take(4 * 512); ktc_b = sch.buf("ktc")
            vt_o = ab.take(4 * 516); vt_b = sch.buf("vt")

            def rope_norm(par, psl, psl_buf, out_off, out_buf, is_k, c):
                qb = ab.ap(qbf_o[par], 512)
                sch.op(DVE, lambda v, p=psl, d=qb: v.tensor_copy(d, p),
                       reads=[psl_buf], writes=[qbf_b[par]], tag="r_ev")
                sch.op(PE, lambda t, s=qb: t.matmul(ps_r[:], p128(), s,
                                                    start=True, stop=True),
                       reads=[qbf_b[par], CB["c_p128"]], writes=[psr_b], tag="r_rot")
                csl = af.ap(cs_o[c % 2], 512)
                snl = af.ap(sn_o[c % 2], 512)
                t1 = af.ap(t1_o, 512)
                t2 = af.ap(t2_o, 512)
                qp = af.ap(qp_o, 512)
                sch.op(DVE, lambda v, q=qb: v.tensor_mul(t1, q, csl),
                       reads=[qbf_b[par], cs_b[c % 2]], writes=[t1_b], tag="r_t1")
                sch.op(DVE, lambda v: v.tensor_mul(t2, ps_r[:], snl),
                       reads=[psr_b, sn_b[c % 2]], writes=[t2_b], tag="r_t2")
                sch.op(DVE, lambda v: v.tensor_add(qp, t1, t2),
                       reads=[t1_b, t2_b], writes=[qp_b], tag="r_qp")
                sq = ab.ap(sqb_o, 512)
                sch.op(DVE, lambda v: v.tensor_mul(sq, qp, qp),
                       reads=[qp_b], writes=[sqb_b], tag="r_sq")
                sch.op(PE, lambda t: t.matmul(ps_s[:], ones(), sq,
                                              start=True, stop=True),
                       reads=[sqb_b, CB["c_ones"]], writes=[pss_b], tag="r_ss")
                nr = af.ap(nrm_o, 512)
                sch.op(ACT, lambda a: a.activation(nr, ps_s[:], AF.Sqrt),
                       reads=[pss_b], writes=[nrm_b], tag="r_sqrt")
                sch.op(DVE, lambda v: v.tensor_scalar_add(nr, nr, EPS_NRM),
                       reads=[nrm_b], writes=[nrm_b], tag="r_eps")
                rc = af.ap(rcp_o, 512)
                sch.op(DVE, lambda v: v.reciprocal(rc, nr),
                       reads=[nrm_b], writes=[rcp_b], tag="r_rcp")
                outap = ab.ap(out_off, 512)
                if is_k:
                    kf = af.ap(kf_o, 512)
                    sch.op(DVE, lambda v: v.tensor_mul(kf, qp, rc),
                           reads=[qp_b, rcp_b], writes=[kf_b], tag="r_km")
                    sch.op(DVE, lambda v: v.tensor_scalar_mul(outap, kf, kc_ap()),
                           reads=[kf_b, kc_b], writes=[out_buf], tag="r_ks")
                else:
                    sch.op(DVE, lambda v: v.tensor_mul(outap, qp, rc),
                           reads=[qp_b, rcp_b], writes=[out_buf], tag="r_qm")

            for c in range(NCH):
                cp = c % 2
                csl = af.ap(cs_o[cp], 512)
                snl = af.ap(sn_o[cp], 512)
                sch.op(POOL, lambda g_, d=csl, s=cin["c_cosT"].ap()[:, c * CH:(c + 1) * CH]:
                       g_.dma_start(out=d, in_=s), writes=[cs_b[cp]], dma=True,
                       tag="cs_ld")
                sch.op(POOL, lambda g_, d=snl, s=cin["c_sinT"].ap()[:, c * CH:(c + 1) * CH]:
                       g_.dma_start(out=d, in_=s), writes=[sn_b[cp]], dma=True,
                       tag="sn_ld")
                for t4 in range(4):
                    par = t4 % 2
                    xf = af.ap(xf_o[par], 2048)
                    xb = ab.ap(xb_o[par], 2048)
                    row0 = (c * 4 + t4) * 128
                    src = _dram_ap(x_in, [(E, 128), (1, E)], row0 * E)
                    sch.op(POOL, lambda g_, s=src, d=xf: g_.dma_start(out=d, in_=s),
                           writes=[xf_b[par]], dma=True, tag=f"x_ld_{c}_{t4}")
                    sch.op(DVE, lambda v, s=xf, d=xb: v.tensor_copy(d, s),
                           reads=[xf_b[par]], writes=[xb_b[par]], tag="x_cvt")
                    pst = pstr_t[par]
                    for k in range(16):
                        sch.op(PE, (lambda kk, xbv=xb, p=pst: (lambda t: t.transpose(
                            p[:, kk * 128:(kk + 1) * 128],
                            xbv[:, kk * 128:(kk + 1) * 128], ident())))(k),
                            reads=[xb_b[par], CB["c_ident"]], writes=[ps_trb[par]],
                            tag="x_tr")
                    xt_dst = _ap3(ab.ap(xt_o, 1), [(512, 16), (1, 128)], t4 * 128)
                    sch.op(DVE, lambda v, p=pst, d=xt_dst: v.tensor_copy(d, p[:]),
                           reads=[ps_trb[par]], writes=[xt_b], tag="x_ev")
                for h in range(NH):
                    par = h % 2
                    psl = ps_a[:, par * 512:(par + 1) * 512]
                    for dc in range(16):
                        lhs = ab.ap(wq_o + dc * 2048 + h * 128, 128)
                        rhs = ab.ap(xt_o + dc * 512, 512)
                        sch.op(PE, lambda t, l=lhs, r=rhs, p=psl, st=(dc == 0),
                               sp=(dc == 15): t.matmul(p, l, r, start=st, stop=sp),
                               reads=[wq_b, xt_b], writes=[psa_b[par]], tag="q_mm")
                    rope_norm(par, psl, psa_b[par], qtc_o + h * 512, qtc_b, False, c)
                dst = _dram_ap(qt_s, [(CH, 128), (128 * CH, 16), (1, CH)],
                               c * NH * 128 * CH)
                qsrc = _ap3(ab.ap(qtc_o, 1), [(512, 16), (1, 512)])
                sch.op(SP, lambda q, s=qsrc, d=dst: q.dma_start(out=d, in_=s),
                       reads=[qtc_b], dwrites=[DB["qt"]], dma=True, tag="qt_st")
                for kv in range(NKV):
                    par = kv % 2
                    psl = ps_a[:, par * 512:(par + 1) * 512]
                    for dc in range(16):
                        lhs = ab.ap(wk_o + dc * 512 + kv * 128, 128)
                        rhs = ab.ap(xt_o + dc * 512, 512)
                        sch.op(PE, lambda t, l=lhs, r=rhs, p=psl, st=(dc == 0),
                               sp=(dc == 15): t.matmul(p, l, r, start=st, stop=sp),
                               reads=[wk_b, xt_b], writes=[psa_b[par]], tag="k_mm")
                    rope_norm(par, psl, psa_b[par], ktc_o + kv * 512, ktc_b, True, c)
                dst = _dram_ap(kt_s, [(CH, 128), (128 * CH, 4), (1, CH)],
                               c * NKV * 128 * CH)
                ksrc = _ap3(ab.ap(ktc_o, 1), [(512, 4), (1, 512)])
                sch.op(SP, lambda q, s=ksrc, d=dst: q.dma_start(out=d, in_=s),
                       reads=[ktc_b], dwrites=[DB["kt"]], dma=True, tag="kt_st")
                ones_dst = _ap3(ab.ap(vt_o, 1), [(516, 4), (129, 4), (1, 1)], 128)
                sch.op(DVE, lambda v, d=ones_dst: v.memset(d, 1.0),
                       writes=[vt_b], tag="v_ones")
                for t4 in range(4):
                    par = t4 % 2
                    psl = ps_a[:, par * 512:(par + 1) * 512]
                    for dc in range(16):
                        lhs = ab.ap(xt_o + dc * 512 + t4 * 128, 128)
                        rhs = ab.ap(wv_o + dc * 512, 512)
                        sch.op(PE, lambda t, l=lhs, r=rhs, p=psl, st=(dc == 0),
                               sp=(dc == 15): t.matmul(p, l, r, start=st, stop=sp),
                               reads=[wv_b, xt_b], writes=[psa_b[par]], tag="v_mm")
                    vdst = _ap3(ab.ap(vt_o, 1), [(129, 4), (1, 128)], t4 * 516)
                    sch.op(DVE, lambda v, p=psl, d=vdst: v.tensor_copy(d, p),
                           reads=[psa_b[par]], writes=[vt_b], tag="v_ev")
                    vst = _dram_ap(v_s, [(516, 128), (1, 516)],
                                   (c * 4 + t4) * 128 * 516)
                    vsr = _ap3(ab.ap(vt_o, 1), [(1, 516)], t4 * 516)
                    sch.op(SP, lambda q, s=vsr, d=vst: q.dma_start(out=d, in_=s),
                           reads=[vt_b], dwrites=[DB["v"]], dma=True, tag="v_st")

        sch.set_fence()
        ab.reset()
        af.reset()

        with ExitStack() as pctx:
            ps_sc = pctx.enter_context(nc.psum_tensor("p2_sc", [128, 1536], F32))
            ps_av = pctx.enter_context(nc.psum_tensor("p2_av", [128, 2048], F32))
            ps_at = pctx.enter_context(nc.psum_tensor("p2_at", [128, 512], BF16))
            pssc_b = sch.buf("ps_sc")
            psav_b = sch.buf("ps_av")
            psat_b = sch.buf("ps_at")

            qin_o = [ab.take(16 * 512), ab.take(16 * 512)]
            qin_b = [sch.buf("qin0"), sch.buf("qin1")]
            kr_o = ab.take(6 * 4 * 512)
            kr_b = [sch.buf(f"kr{s_}") for s_ in range(6)]
            vr_o = ab.take(6 * 4 * 516)
            vr_b = [sch.buf(f"vr{s_}") for s_ in range(6)]
            tb_o = af.take(1536)
            tb_b = sch.buf("tb")
            eb_o = [ab.take(1536), ab.take(1536)]
            eb_b = [sch.buf("eb0"), sch.buf("eb1")]
            adiv_o = ab.take(4 * 128)
            adiv_b = sch.buf("adiv")
            rcd_o = af.take(4)
            rcd_b = sch.buf("rcd")
            aotb_o = [ab.take(512), ab.take(512)]
            aotb_b = [sch.buf("ao0"), sch.buf("ao1")]

            def load_kv_slot(cc):
                slot = cc % 6
                kd = _ap3(ab.ap(kr_o + slot * 4 * 512, 1), [(512, 4), (1, 512)])
                ks = _dram_ap(kt_s, [(CH, 128), (128 * CH, 4), (1, CH)],
                              cc * NKV * 128 * CH)
                sch.op(POOL, lambda g_, d=kd, s=ks: g_.dma_start(out=d, in_=s),
                       writes=[kr_b[slot]], dreads=[DB["kt"]], dma=True,
                       tag=f"kr_{cc}")
                vd = _ap3(ab.ap(vr_o + slot * 4 * 516, 1), [(516, 4), (1, 516)])
                vs2 = _dram_ap(v_s, [(516, 128), (128 * 516, 4), (1, 516)],
                               cc * 4 * 128 * 516)
                sch.op(POOL, lambda g_, d=vd, s=vs2: g_.dma_start(out=d, in_=s),
                       writes=[vr_b[slot]], dreads=[DB["v"]], dma=True,
                       tag=f"vr_{cc}")

            def load_q_chunk(cc):
                par = cc % 2
                qd = _ap3(ab.ap(qin_o[par], 1), [(512, 16), (1, 512)])
                qs2 = _dram_ap(qt_s, [(CH, 128), (128 * CH, 16), (1, CH)],
                               cc * NH * 128 * CH)
                sch.op(POOL, lambda g_, d=qd, s=qs2: g_.dma_start(out=d, in_=s),
                       writes=[qin_b[par]], dreads=[DB["qt"]], dma=True,
                       tag=f"q_{cc}")

            for cc in range(3):
                load_kv_slot(cc)
            load_q_chunk(0)

            for c in range(NCH):
                if c + 1 < NCH:
                    load_q_chunk(c + 1)
                if c + 3 < NCH:
                    load_kv_slot(c + 3)
                kb_lo = max(0, 4 * c - 8)
                kb_hi = min(NT - 1, 4 * c + 11)
                kbs = list(range(kb_lo, kb_hi + 1))
                for h in range(NH):
                    kv = h // 4
                    qrhs = ab.ap(qin_o[c % 2] + h * 512, 512)
                    started = [False] * 4
                    for w0 in range(0, len(kbs), 3):
                        wave = kbs[w0:w0 + 3]
                        for wi, kb in enumerate(wave):
                            slot = (kb // 4) % 6
                            lhs = ab.ap(kr_o + slot * 4 * 512 + kv * 512 +
                                        (kb % 4) * 128, 128)
                            psl = ps_sc[:, wi * 512:(wi + 1) * 512]
                            sch.op(PE, lambda t, l=lhs, r=qrhs, p=psl:
                                   t.matmul(p, l, r, start=True, stop=True),
                                   reads=[kr_b[slot], qin_b[c % 2]],
                                   writes=[pssc_b], tag="sc_mm")
                        nW = len(wave) * 512
                        tb = af.ap(tb_o, nW)
                        sch.op(ACT, lambda a, d=tb, n=nW: a.activation(
                            d, ps_sc[:, 0:n], AF.Tanh),
                            reads=[pssc_b], writes=[tb_b], tag="sc_tanh")
                        ebp = (w0 // 3) % 2
                        eb = ab.ap(eb_o[ebp], nW)
                        sch.op(ACT, lambda a, s=tb, d=eb: a.activation(
                            d, s, AF.Exp, scale=CAP),
                            reads=[tb_b], writes=[eb_b[ebp]], tag="sc_exp")
                        for wi, kb in enumerate(wave):
                            slot = (kb // 4) % 6
                            for qs in range(4):
                                qi = 4 * c + qs
                                dblk = kb - qi
                                if abs(dblk) > 8:
                                    continue
                                esl = ab.ap(eb_o[ebp] + wi * 512 + qs * 128, 128)
                                if dblk == 8:
                                    sch.op(DVE, lambda v, e=esl: v.tensor_mul(
                                        e, e, maskhi()),
                                        reads=[eb_b[ebp], CB["c_maskhi"]],
                                        writes=[eb_b[ebp]], tag="m_hi")
                                elif dblk == -8:
                                    sch.op(DVE, lambda v, e=esl: v.tensor_mul(
                                        e, e, masklo()),
                                        reads=[eb_b[ebp], CB["c_masklo"]],
                                        writes=[eb_b[ebp]], tag="m_lo")
                                vrhs = ab.ap(vr_o + slot * 4 * 516 +
                                             (kb % 4) * 516 + kv * 129, 129)
                                psl = ps_av[:, qs * 512:qs * 512 + 129]
                                st = not started[qs]
                                started[qs] = True
                                sp = (kb == min(NT - 1, qi + 8))
                                sch.op(PE, lambda t, l=esl, r=vrhs, p=psl, s=st,
                                       s2=sp: t.matmul(p, l, r, start=s, stop=s2),
                                       reads=[eb_b[ebp], vr_b[slot]],
                                       writes=[psav_b], tag="av_mm")
                    for qs in range(4):
                        rc = af.ap(rcd_o + qs, 1)
                        den = ps_av[:, qs * 512 + 128:qs * 512 + 129]
                        sch.op(DVE, lambda v, d=rc, s=den: v.reciprocal(d, s),
                               reads=[psav_b], writes=[rcd_b], tag="d_rcp")
                        num = ps_av[:, qs * 512:qs * 512 + 128]
                        ad = ab.ap(adiv_o + qs * 128, 128)
                        sch.op(DVE, lambda v, d=ad, s=num, r=rc:
                               v.tensor_scalar_mul(d, s, r),
                               reads=[psav_b, rcd_b], writes=[adiv_b], tag="d_mul")
                    for qs in range(4):
                        ad = ab.ap(adiv_o + qs * 128, 128)
                        sch.op(PE, lambda t, s=ad,
                               p=ps_at[:, qs * 128:(qs + 1) * 128]:
                               t.transpose(p, s, ident()),
                               reads=[adiv_b, CB["c_ident"]], writes=[psat_b],
                               tag="ao_tr")
                    hp = h % 2
                    ao = ab.ap(aotb_o[hp], 512)
                    sch.op(DVE, lambda v, d=ao: v.tensor_copy(d, ps_at[:]),
                           reads=[psat_b], writes=[aotb_b[hp]], tag="ao_ev")
                    adst = _dram_ap(aot_s, [(CH, 128), (1, CH)],
                                    (c * 16 + h) * 128 * CH)
                    sch.op(SP, lambda q, s=ao, d=adst: q.dma_start(out=d, in_=s),
                           reads=[aotb_b[hp]], dwrites=[DB["aot"]], dma=True,
                           tag="ao_st")

        sch.set_fence()
        ab.reset()
        af.reset()

        with ExitStack() as pctx:
            ps_tr = pctx.enter_context(nc.psum_tensor("p3_tr", [128, 2048], BF16))
            ps_tr2 = pctx.enter_context(nc.psum_tensor("p3_tr2", [128, 2048], BF16))
            ps_w = pctx.enter_context(nc.psum_tensor("p3_w", [128, 1024], F32))
            pstr_t = [ps_tr, ps_tr2]
            ps_trb = [sch.buf("p3tr0"), sch.buf("p3tr1")]
            psw_b = [sch.buf("p3w0"), sch.buf("p3w1")]

            wo_o = ab.take(16 * 2048)
            wo_b = sch.buf("wo")
            wof_o = [af.take(2048)]
            wof_b = [sch.buf("wof0")]
            for hdc in range(16):
                par = 0
                stg = af.ap(wof_o[par], 2048)
                src = _dram_ap(wo_in, [(E, 128), (1, E)], hdc * 128 * E)
                sch.op(POOL, lambda g_, s=src, d=stg: g_.dma_start(out=d, in_=s),
                       writes=[wof_b[par]], dma=True, tag="wo_ld")
                dstb = ab.ap(wo_o + hdc * 2048, 2048)
                sch.op(DVE, lambda v, s=stg, d=dstb: v.tensor_copy(d, s),
                       reads=[wof_b[par]], writes=[wo_b], tag="wo_cvt")

            aoin_o = [ab.take(16 * 512), ab.take(16 * 512)]
            aoin_b = [sch.buf("ai0"), sch.buf("ai1")]
            x4f_o = [af.take(4 * 2048)]
            x4f_b = [sch.buf("x40")]
            r_o = af.take(2048)
            r_b = sch.buf("r")
            st_o = af.take(28)
            st_b = sch.buf("stats")
            x1b_o = [ab.take(2048), ab.take(2048)]
            x1b_b = [sch.buf("x1b0"), sch.buf("x1b1")]
            ex1_o = [ab.take(2048), ab.take(2048)]
            ex1_b = [sch.buf("ex0"), sch.buf("ex1")]

            def layer_norm(src_ap, src_buf, out_ap, out_buf, tag):
                bn24 = af.ap(st_o, 24)
                bn2 = af.ap(st_o + 24, 2)
                for i in range(4):
                    bno = af.ap(st_o + i * 6, 6)
                    bni = _ap3(src_ap, [(1, 512)], i * 512)
                    sch.op(DVE, (lambda o2, i2: (lambda v: v.bn_stats(o2, i2)))(
                        bno, bni),
                        reads=[src_buf], writes=[st_b], tag=tag + "_bns")
                sch.op(DVE, lambda v: v.bn_aggr(bn2, bn24),
                       reads=[st_b], writes=[st_b], tag=tag + "_bna")
                mean = af.ap(st_o + 24, 1)
                var = af.ap(st_o + 25, 1)
                sd = af.ap(st_o + 26, 1)
                sch.op(ACT, lambda a: a.activation(sd, var, AF.Sqrt,
                                                   bias=epsln_ap()),
                       reads=[st_b, epsln_b], writes=[st_b], tag=tag + "_sd")
                rs = af.ap(st_o + 27, 1)
                sch.op(DVE, lambda v: v.reciprocal(rs, sd),
                       reads=[st_b], writes=[st_b], tag=tag + "_rs")
                sch.op(DVE, lambda v: v.tensor_scalar(
                    out_ap, src_ap, mean, rs, ALU.subtract, ALU.mult),
                    reads=[src_buf, st_b], writes=[out_buf], tag=tag + "_out")

            def aoin_load(cc):
                aod = _ap3(ab.ap(aoin_o[cc % 2], 1), [(512, 16), (1, 512)])
                aos = _dram_ap(aot_s, [(CH, 128), (128 * CH, 16), (1, CH)],
                               cc * 16 * 128 * CH)
                sch.op(POOL, lambda g_, d=aod, s=aos: g_.dma_start(out=d, in_=s),
                       writes=[aoin_b[cc % 2]], dreads=[DB["aot"]], dma=True,
                       tag=f"ai_{cc}")

            aoin_load(0)
            for c in range(NCH):
                cp = c % 2
                if c + 1 < NCH:
                    aoin_load(c + 1)
                x4d = _ap3(af.ap(x4f_o[0], 1), [(2048, 4), (1, 2048)])
                xsrc = _dram_ap(x_in, [(E, 128), (128 * E, 4), (1, E)],
                                c * 4 * 128 * E)
                sch.op(POOL, lambda g_, d=x4d, s=xsrc: g_.dma_start(out=d, in_=s),
                       writes=[x4f_b[0]], dma=True, tag=f"x4_{c}")
                for ts in range(4):
                    for dmc in range(4):
                        par = dmc % 2
                        psl = ps_w[:, par * 512:(par + 1) * 512]
                        for hdc in range(16):
                            lhs = ab.ap(aoin_o[cp] + hdc * 512 + ts * 128, 128)
                            rhs = ab.ap(wo_o + hdc * 2048 + dmc * 512, 512)
                            sch.op(PE, lambda t, l=lhs, r2=rhs, p=psl,
                                   st=(hdc == 0), sp=(hdc == 15):
                                   t.matmul(p, l, r2, start=st, stop=sp),
                                   reads=[aoin_b[cp], wo_b], writes=[psw_b[par]],
                                   tag="wo_mm")
                        xsl = af.ap(x4f_o[0] + ts * 2048 + dmc * 512, 512)
                        rsl = af.ap(r_o + dmc * 512, 512)
                        sch.op(DVE, lambda v, p=psl, xs=xsl, d=rsl:
                               v.scalar_tensor_tensor(d, p, 1.0, xs, ALU.mult,
                                                      ALU.add),
                               reads=[psw_b[par], x4f_b[0]], writes=[r_b],
                               tag="wo_res")
                    if _DEBUG:
                        rdst = _dram_ap(dbg_r, [(E, 128), (1, E)],
                                        (c * 4 + ts) * 128 * E)
                        sch.op(SP, lambda q, d=rdst, s=af.ap(r_o, 2048):
                               q.dma_start(out=d, in_=s),
                               reads=[r_b], dma=True, tag="dbg_r")
                    x1t = ab.ap(x1b_o[ts % 2], 2048)
                    layer_norm(af.ap(r_o, 2048), r_b, x1t, x1b_b[ts % 2], "ln1")
                    if _DEBUG:
                        sdst = _dram_ap(dbg_st, [(28, 128), (1, 28)],
                                        (c * 4 + ts) * 128 * 28)
                        sch.op(SP, lambda q, d=sdst, s=af.ap(st_o, 28):
                               q.dma_start(out=d, in_=s),
                               reads=[st_b], dma=True, tag="dbg_st")
                    x1dst = _dram_ap(x1_s, [(E, 128), (1, E)],
                                     (c * 4 + ts) * 128 * E)
                    sch.op(SP, lambda q, s=x1t, d=x1dst: q.dma_start(out=d, in_=s),
                           reads=[x1b_b[ts % 2]], dwrites=[DB["x1"]], dma=True,
                           tag="x1_st")
                    pst = pstr_t[ts % 2]
                    for k in range(16):
                        sch.op(PE, (lambda kk, xv=x1t, p=pst: (lambda t:
                               t.transpose(p[:, kk * 128:(kk + 1) * 128],
                                           xv[:, kk * 128:(kk + 1) * 128],
                                           ident())))(k),
                            reads=[x1b_b[ts % 2], CB["c_ident"]],
                            writes=[ps_trb[ts % 2]], tag="x1_tr")
                    ex1 = ab.ap(ex1_o[ts % 2], 2048)
                    sch.op(DVE, lambda v, p=pst, d=ex1: v.tensor_copy(d, p[:]),
                           reads=[ps_trb[ts % 2]], writes=[ex1_b[ts % 2]],
                           tag="x1_ev")
                    xtd = _dram_ap(x1t_s, [(S, 128), (128 * S, 16), (1, 128)],
                                   c * 512 + ts * 128)
                    es = _ap3(ab.ap(ex1_o[ts % 2], 1), [(128, 16), (1, 128)])
                    sch.op(SP, lambda q, s=es, d=xtd: q.dma_start(out=d, in_=s),
                           reads=[ex1_b[ts % 2]], dwrites=[DB["x1t"]], dma=True,
                           tag="x1t_st")

        sch.set_fence()
        ab.reset()
        af.reset()

        with ExitStack() as pctx:
            ps_u1 = pctx.enter_context(nc.psum_tensor("p4_u1", [128, 1024], F32))
            ps_u3 = pctx.enter_context(nc.psum_tensor("p4_u3", [128, 1024], F32))
            ps_d = pctx.enter_context(nc.psum_tensor("p4_d", [128, 2048], F32))
            psu1_b = [sch.buf("u10"), sch.buf("u11")]
            psu3_b = [sch.buf("u30"), sch.buf("u31")]
            psd_b = [sch.buf(f"pd{i}") for i in range(4)]

            x1tin_o = ab.take(16 * 512)
            x1tin_b = sch.buf("x1tin")
            x1tok_o = ab.take(4 * 2048)
            x1tok_b = sch.buf("x1tok")
            w1g_o = [ab.take(4096), ab.take(4096)]
            w3g_o = [ab.take(4096), ab.take(4096)]
            w1g_b = [sch.buf("w1g0"), sch.buf("w1g1")]
            w3g_b = [sch.buf("w3g0"), sch.buf("w3g1")]
            w2g_o = [ab.take(4096), ab.take(4096)]
            w2g_b = [sch.buf("w2g0"), sch.buf("w2g1")]
            gt_o = ab.take(64 * 512)
            gt_b = sch.buf("gt")
            s1_o = [af.take(512), af.take(512)]
            s1_b = [sch.buf("s10"), sch.buf("s11")]
            r4_o = [af.take(2048) for _ in range(4)]
            r4_b = [sch.buf(f"r4_{i}") for i in range(4)]
            outf_o = af.take(2048)
            outf_b = sch.buf("outf")
            st_o = af.take(28)
            st_b = sch.buf("st4")

            for p in range(NCH):
                if p == 0:
                    x1td = _ap3(ab.ap(x1tin_o, 1), [(512, 16), (1, 512)])
                    x1src = _dram_ap(x1t_s, [(S, 128), (128 * S, 16), (1, 512)], 0)
                    sch.op(POOL, lambda g_, d=x1td, s=x1src:
                           g_.dma_start(out=d, in_=s),
                           writes=[x1tin_b], dreads=[DB["x1t"]], dma=True,
                           tag="fx1t_0")
                xtk = _ap3(ab.ap(x1tok_o, 1), [(2048, 4), (1, 2048)])
                xts = _dram_ap(x1_s, [(E, 128), (128 * E, 4), (1, E)],
                               p * 4 * 128 * E)
                sch.op(POOL, lambda g_, d=xtk, s=xts: g_.dma_start(out=d, in_=s),
                       writes=[x1tok_b], dreads=[DB["x1"]], dma=True,
                       tag=f"fx1_{p}")

                def load_wgroup(g, par):
                    for (wsrc, woff, wbuf) in [(w1t_s, w1g_o, w1g_b),
                                               (w3t_s, w3g_o, w3g_b)]:
                        d = _ap3(ab.ap(woff[par], 1), [(256, 16), (1, 256)])
                        s = _dram_ap(wsrc, [(HF, 128), (128 * HF, 16), (1, 256)],
                                     g * 256)
                        sch.op(SP, lambda q, d2=d, s2=s: q.dma_start(out=d2, in_=s2),
                               writes=[wbuf[par]],
                               dreads=[DB["w1t"] if wsrc is w1t_s else DB["w3t"]],
                               dma=True, tag=f"wg_{p}_{g}")

                load_wgroup(0, 0)
                for g in range(32):
                    par = g % 2
                    if g + 1 < 32:
                        load_wgroup(g + 1, (g + 1) % 2)
                    for hh in range(2):
                        hc = g * 2 + hh
                        up = hc % 2
                        psl1 = ps_u1[:, up * 512:(up + 1) * 512]
                        psl3 = ps_u3[:, up * 512:(up + 1) * 512]
                        for dc in range(16):
                            l1 = ab.ap(w1g_o[par] + dc * 256 + hh * 128, 128)
                            l3 = ab.ap(w3g_o[par] + dc * 256 + hh * 128, 128)
                            rr = ab.ap(x1tin_o + dc * 512, 512)
                            sch.op(PE, lambda t, l=l1, r=rr, p2=psl1, st=(dc == 0),
                                   sp=(dc == 15): t.matmul(p2, l, r, start=st,
                                                           stop=sp),
                                   reads=[w1g_b[par], x1tin_b],
                                   writes=[psu1_b[up]], tag="u1_mm")
                            sch.op(PE, lambda t, l=l3, r=rr, p2=psl3, st=(dc == 0),
                                   sp=(dc == 15): t.matmul(p2, l, r, start=st,
                                                           stop=sp),
                                   reads=[w3g_b[par], x1tin_b],
                                   writes=[psu3_b[up]], tag="u3_mm")
                        s1 = af.ap(s1_o[up], 512)
                        sch.op(ACT, lambda a, d=s1, p2=psl1: a.activation(
                            d, p2, AF.Silu),
                            reads=[psu1_b[up]], writes=[s1_b[up]], tag="silu")
                        gsl = ab.ap(gt_o + hc * 512, 512)
                        sch.op(DVE, lambda v, d=gsl, a2=s1, p2=psl3:
                               v.tensor_mul(d, a2, p2),
                               reads=[s1_b[up], psu3_b[up]], writes=[gt_b],
                               tag="g_mul")
                if p + 1 < NCH:
                    x1td2 = _ap3(ab.ap(x1tin_o, 1), [(512, 16), (1, 512)])
                    x1src2 = _dram_ap(x1t_s, [(S, 128), (128 * S, 16), (1, 512)],
                                      (p + 1) * 512)
                    sch.op(POOL, lambda g_, d=x1td2, s=x1src2:
                           g_.dma_start(out=d, in_=s),
                           writes=[x1tin_b], dreads=[DB["x1t"]], dma=True,
                           tag=f"fx1t_{p + 1}")
                for dmc in range(4):
                    def load_w2g(hb, par):
                        d = _ap3(ab.ap(w2g_o[par], 1), [(512, 8), (1, 512)])
                        s = _dram_ap(w2f_s, [(E, 128), (128 * E, 8), (1, 512)],
                                     hb * 8 * 128 * E + dmc * 512)
                        sch.op(SP, lambda q, d2=d, s2=s: q.dma_start(out=d2,
                                                                     in_=s2),
                               writes=[w2g_b[par]], dreads=[DB["w2f"]], dma=True,
                               tag=f"w2g_{p}_{dmc}_{hb}")

                    load_w2g(0, 0)
                    for hb in range(8):
                        par = hb % 2
                        if hb + 1 < 8:
                            load_w2g(hb + 1, (hb + 1) % 2)
                        for hh in range(8):
                            hc = hb * 8 + hh
                            rhs = ab.ap(w2g_o[par] + hh * 512, 512)
                            for ts in range(4):
                                lhs = ab.ap(gt_o + hc * 512 + ts * 128, 128)
                                psl = ps_d[:, ts * 512:(ts + 1) * 512]
                                sch.op(PE, lambda t, l=lhs, r=rhs, p2=psl,
                                       st=(hc == 0), sp=(hc == 63):
                                       t.matmul(p2, l, r, start=st, stop=sp),
                                       reads=[gt_b, w2g_b[par]],
                                       writes=[psd_b[ts]], tag="dn_mm")
                    for ts in range(4):
                        psl = ps_d[:, ts * 512:(ts + 1) * 512]
                        xsl = ab.ap(x1tok_o + ts * 2048 + dmc * 512, 512)
                        rsl = af.ap(r4_o[ts] + dmc * 512, 512)
                        sch.op(DVE, lambda v, p2=psl, xs=xsl, d=rsl:
                               v.scalar_tensor_tensor(d, p2, 1.0, xs, ALU.mult,
                                                      ALU.add),
                               reads=[psd_b[ts], x1tok_b], writes=[r4_b[ts]],
                               tag="dn_res")
                for ts in range(4):
                    rr = af.ap(r4_o[ts], 2048)
                    of = af.ap(outf_o, 2048)
                    bn24 = af.ap(st_o, 24)
                    bn2 = af.ap(st_o + 24, 2)
                    for i in range(4):
                        bno = af.ap(st_o + i * 6, 6)
                        bni = _ap3(rr, [(1, 512)], i * 512)
                        sch.op(DVE, (lambda o2, i2: (lambda v: v.bn_stats(
                            o2, i2)))(bno, bni),
                            reads=[r4_b[ts]], writes=[st_b], tag="ln2_bns")
                    sch.op(DVE, lambda v: v.bn_aggr(bn2, bn24),
                           reads=[st_b], writes=[st_b], tag="ln2_bna")
                    mean = af.ap(st_o + 24, 1)
                    var = af.ap(st_o + 25, 1)
                    sd = af.ap(st_o + 26, 1)
                    sch.op(ACT, lambda a: a.activation(sd, var, AF.Sqrt,
                                                       bias=epsln_ap()),
                           reads=[st_b, epsln_b], writes=[st_b], tag="ln2_sd")
                    rs = af.ap(st_o + 27, 1)
                    sch.op(DVE, lambda v: v.reciprocal(rs, sd),
                           reads=[st_b], writes=[st_b], tag="ln2_rs")
                    sch.op(DVE, lambda v, s=rr, d=of: v.tensor_scalar(
                        d, s, mean, rs, ALU.subtract, ALU.mult),
                        reads=[r4_b[ts], st_b], writes=[outf_b], tag="ln2_out")
                    odst = _dram_ap(out_t, [(E, 128), (1, E)],
                                    (p * 4 + ts) * 128 * E)
                    sch.op(ACT, lambda a, s=of, d=odst: a.dma_start(out=d, in_=s),
                           reads=[outf_b], dma=True, tag="out_st")

        finals = dict(sch.counts)
        sems = {}
        for name in _ALL_SEMS:
            sems[name] = ctx.enter_context(nc.semaphore(name))
        block = ctx.enter_context(nc.Block())
        sch.emit(block, sems)

        @block.sync
        def _(sync):
            for qn in (Q_ACT, Q_SP, Q_POOL):
                if finals[qn]:
                    sync.wait_ge(sems[qn], finals[qn])

    return nc


_NC_CACHE = None


def _get_nc():
    global _NC_CACHE
    if _NC_CACHE is None:
        _NC_CACHE = build_nc()
    return _NC_CACHE


def kernel(L0_attn_hidden_states, L0_attn_wq, L0_attn_wk, L0_attn_wv, L0_attn_wo,
           L0_attn_query_scale, L0_attn_key_scale, L0_ffn_W1, L0_ffn_W2, L0_ffn_W3):
    consts = host_constants()
    nc = _get_nc()
    xs = np.ascontiguousarray(np.asarray(L0_attn_hidden_states, dtype=np.float32))
    shared = {
        "wq": np.ascontiguousarray(np.asarray(L0_attn_wq, np.float32)),
        "wk": np.ascontiguousarray(np.asarray(L0_attn_wk, np.float32)),
        "wv": np.ascontiguousarray(np.asarray(L0_attn_wv, np.float32)),
        "wo": np.ascontiguousarray(np.asarray(L0_attn_wo, np.float32)),
        "qscale": np.ascontiguousarray(np.asarray(L0_attn_query_scale, np.float32)),
        "kscale": np.ascontiguousarray(np.asarray(L0_attn_key_scale, np.float32)),
        "w1": np.ascontiguousarray(np.asarray(L0_ffn_W1, np.float32)),
        "w2": np.ascontiguousarray(np.asarray(L0_ffn_W2, np.float32)),
        "w3": np.ascontiguousarray(np.asarray(L0_ffn_W3, np.float32)),
    }
    shared.update(consts)
    in_maps = []
    for b in range(B):
        m = dict(shared)
        m["x"] = np.ascontiguousarray(xs[b])
        in_maps.append(m)
    res = run_bass_kernel_spmd(nc, in_maps, core_ids=list(range(B)))
    out = np.stack([r["out"] for r in res.results], axis=0)
    return out


# version 04
# speedup vs baseline: 1.1729x; 1.1383x over previous
import dataclasses
import numpy as np
import ml_dtypes
import concourse.bass as bass
import concourse.mybir as mybir
from concourse.bass_utils import run_bass_kernel_spmd
from contextlib import ExitStack

F32 = mybir.dt.float32
BF16 = mybir.dt.bfloat16
AF = mybir.ActivationFunctionType
ALU = mybir.AluOpType
AX = mybir.AxisListType

B = 8
S = 4096
E = 2048
NH = 16
NKV = 4
D = 128
CAP = 50.0
EPS_NRM = 1e-6
EPS_LN = 1e-5
HF = 8192
CH = 512
NCH = S // CH
NT = S // 128

_DEBUG = False


def _sylvester(n):
    H = np.array([[1.0]], dtype=np.float64)
    while H.shape[0] < n:
        H = np.block([[H, H], [H, -H]])
    return H


def host_constants():
    c = {}
    c["c_ident"] = np.eye(128, dtype=np.float32).astype(ml_dtypes.bfloat16)
    P = np.zeros((128, 128), np.float32)
    for i in range(64):
        P[i + 64, i] = -1.0
        P[i, i + 64] = 1.0
    c["c_p128"] = P.astype(ml_dtypes.bfloat16)
    c["c_ones"] = np.ones((128, 128), np.float32).astype(ml_dtypes.bfloat16)
    c["c_h64"] = (_sylvester(64) / 64.0).astype(ml_dtypes.bfloat16)
    c["c_h128"] = (_sylvester(128) / np.sqrt(2.0)).astype(ml_dtypes.bfloat16)
    kl = np.arange(128)[:, None]
    ql = np.arange(128)[None, :]
    c["c_maskhi"] = (kl <= ql).astype(np.float32).astype(ml_dtypes.bfloat16)
    c["c_masklo"] = (ql <= kl).astype(np.float32).astype(ml_dtypes.bfloat16)
    half = D // 2
    inv = 1.0 / 10000.0 ** (np.arange(half, dtype=np.float32) / half)
    ang = np.outer(np.arange(S, dtype=np.float32), inv)
    ang = np.concatenate([ang, ang], axis=-1)
    c["c_cosT"] = np.ascontiguousarray(np.cos(ang).T.astype(np.float32))
    c["c_sinT"] = np.ascontiguousarray(np.sin(ang).T.astype(np.float32))
    return c


PE, ACT, DVE, POOL, SP = "pe", "act", "dve", "pool", "sp"
Q_POOL, Q_SP, Q_ACT = "q_pool", "q_sp", "q_act"
_DMA_Q = {POOL: Q_POOL, SP: Q_SP, ACT: Q_ACT}
_ALL_SEMS = (PE, ACT, DVE, POOL, SP, Q_POOL, Q_SP, Q_ACT)


class Buf:
    __slots__ = ("writer", "readers", "name")

    def __init__(self, name, fence):
        self.writer = None
        self.readers = list(fence)
        self.name = name


class OpRec:
    __slots__ = ("build", "waits", "sem", "amount", "tag", "phase")

    def __init__(self, build, waits, sem, amount, tag, phase):
        self.build = build
        self.waits = waits
        self.sem = sem
        self.amount = amount
        self.tag = tag
        self.phase = phase


class DramBuf:
    __slots__ = ("writers", "name")

    def __init__(self, name):
        self.writers = []
        self.name = name


class Sched:
    def __init__(self):
        self.ops = {k: [] for k in (PE, ACT, DVE, POOL, SP)}
        self.counts = {k: 0 for k in _ALL_SEMS}
        self.fence = []
        self.phase = "init"

    def buf(self, name):
        return Buf(name, self.fence)

    def set_fence(self):
        self.fence = [(k, v) for k, v in self.counts.items() if v > 0]

    def op(self, eng, build, reads=(), writes=(), dreads=(), dwrites=(),
           dma=False, drain=False, tag=""):
        sem = _DMA_Q[eng] if dma else eng
        amount = 16 if dma else 1
        waits = {}

        def add(tok):
            if tok is None:
                return
            s, v = tok
            if (not dma) and eng == PE and s == PE:
                return
            if waits.get(s, 0) < v:
                waits[s] = v

        for b in reads:
            add(b.writer)
        for b in writes:
            add(b.writer)
            for t in b.readers:
                add(t)
        for db in dreads:
            for t in db.writers:
                add(t)
        self.counts[sem] += amount
        tok = (sem, self.counts[sem])
        self.ops[eng].append(OpRec(build, waits, sem, amount, tag, self.phase))
        for b in reads:
            b.readers.append(tok)
        for b in writes:
            b.writer = tok
            b.readers = []
        for db in dwrites:
            db.writers.append(tok)
        if drain:
            assert dma
            self.ops[eng].append(OpRec(
                lambda e: e.nop(), {sem: self.counts[sem]}, eng, 1,
                tag + "_drain", self.phase))
            self.counts[eng] += 1
        return tok

    def emit(self, block, sems, nc=None):
        eng_of = {PE: "tensor", ACT: "scalar", DVE: "vector", POOL: "gpsimd",
                  SP: "sync"}
        for eng_name, ops in self.ops.items():
            if not ops:
                continue
            deco = getattr(block, eng_of[eng_name])

            def make(ops_local):
                def _(eng):
                    from contextlib import ExitStack as _ES
                    waited = {}
                    cur_phase = [None]
                    stk = _ES()
                    for rec in ops_local:
                        if nc is not None and rec.phase != cur_phase[0]:
                            stk.close()
                            stk = _ES()
                            stk.enter_context(nc.named_scope(rec.phase))
                            cur_phase[0] = rec.phase
                        for s, v in sorted(rec.waits.items()):
                            if waited.get(s, 0) >= v:
                                continue
                            eng.wait_ge(sems[s], v)
                            waited[s] = v
                        rec.build(eng).then_inc(sems[rec.sem], rec.amount)
                    stk.close()
                return _

            deco(make(ops))


class Arena:
    def __init__(self, nc, name, dtype, n_elems):
        self.t = nc.alloc_sbuf_tensor(name, [128, n_elems], dtype)
        self.off = 0
        self.n = n_elems
        self.mark = 0

    def take(self, elems):
        o = self.off
        self.off += elems
        assert self.off <= self.n, f"{self.t} arena overflow {self.off}>{self.n}"
        return o

    def set_mark(self):
        self.mark = self.off

    def reset(self):
        self.off = self.mark

    def ap(self, off, elems, parts=128):
        return self.t[0:parts, off:off + elems]


def _ap3(ap, dims, offset_elems=0):
    part_dim = list(ap.ap[0])
    new_dims = [part_dim] + [[s, c] for (s, c) in dims]
    return dataclasses.replace(ap, offset=ap.offset + offset_elems, ap=new_dims)


def _dram_ap(t, dims, offset_elems=0):
    full = t.ap()
    return dataclasses.replace(full, offset=full.offset + offset_elems,
                               ap=[[s, c] for (s, c) in dims])


def build_nc():
    nc = bass.Bass("TRN2", target_bir_lowering=False)

    x_in = nc.dram_tensor("x", [S, E], F32, kind="ExternalInput")
    wq_in = nc.dram_tensor("wq", [E, NH * D], F32, kind="ExternalInput")
    wk_in = nc.dram_tensor("wk", [E, NKV * D], F32, kind="ExternalInput")
    wv_in = nc.dram_tensor("wv", [E, NKV * D], F32, kind="ExternalInput")
    wo_in = nc.dram_tensor("wo", [NH * D, E], F32, kind="ExternalInput")
    qs_in = nc.dram_tensor("qscale", [D], F32, kind="ExternalInput")
    ks_in = nc.dram_tensor("kscale", [D], F32, kind="ExternalInput")
    w1_in = nc.dram_tensor("w1", [HF, E], F32, kind="ExternalInput")
    w2_in = nc.dram_tensor("w2", [E, HF], F32, kind="ExternalInput")
    w3_in = nc.dram_tensor("w3", [HF, E], F32, kind="ExternalInput")
    cin = {}
    for name, shape, dt in [
        ("c_ident", [128, 128], BF16), ("c_p128", [128, 128], BF16),
        ("c_ones", [128, 128], BF16), ("c_h64", [64, 64], BF16),
        ("c_h128", [128, 128], BF16), ("c_maskhi", [128, 128], BF16),
        ("c_masklo", [128, 128], BF16), ("c_cosT", [128, S], F32),
        ("c_sinT", [128, S], F32),
    ]:
        cin[name] = nc.dram_tensor(name, shape, dt, kind="ExternalInput")
    out_t = nc.dram_tensor("out", [S, E], F32, kind="ExternalOutput")

    skind = "ExternalOutput" if _DEBUG else "Internal"
    w1t_s = nc.dram_tensor("w1t_s", [16, 128, HF], BF16, kind="Internal")
    w3t_s = nc.dram_tensor("w3t_s", [16, 128, HF], BF16, kind="Internal")
    b_s = nc.dram_tensor("b_s", [64, 128, E], BF16, kind=skind)
    w2f_s = nc.dram_tensor("w2f_s", [64, 128, E], BF16, kind=skind)
    qt_s = nc.dram_tensor("qt_s", [NCH, NH, 128, CH], BF16, kind=skind)
    kt_s = nc.dram_tensor("kt_s", [NCH, NKV, 128, CH], BF16, kind=skind)
    v_s = nc.dram_tensor("v_s", [NCH, 4, 128, 516], BF16, kind=skind)
    aot_s = nc.dram_tensor("aot_s", [NCH, 16, 128, CH], BF16, kind=skind)
    x1_s = nc.dram_tensor("x1_s", [NT, 128, E], BF16, kind=skind)
    x1t_s = nc.dram_tensor("x1t_s", [16, 128, S], BF16, kind=skind)
    dbg_r = (nc.dram_tensor("dbg_r", [NT, 128, E], F32, kind="ExternalOutput")
             if _DEBUG else None)
    dbg_st = (nc.dram_tensor("dbg_st", [NT, 128, 28], F32, kind="ExternalOutput")
              if _DEBUG else None)

    sch = Sched()
    DB = {n: DramBuf(n) for n in ("w1t", "w3t", "b", "w2f", "qt", "kt", "v",
                                  "aot", "x1", "x1t")}

    with ExitStack() as ctx:
        ab = Arena(nc, "AB", BF16, 77000)
        af = Arena(nc, "AF", F32, 14500)

        co = {}
        CB = {}
        for name, elems, parts in [
            ("c_ident", 128, 128), ("c_p128", 128, 128), ("c_ones", 128, 128),
            ("c_h64", 64, 64), ("c_h128", 128, 128), ("c_maskhi", 128, 128),
            ("c_masklo", 128, 128),
        ]:
            co[name] = ab.take(elems)
            CB[name] = sch.buf(name)
            dst = ab.ap(co[name], elems, parts)
            src = cin[name].ap()
            sch.op(POOL, (lambda d, s: (lambda g: g.dma_start(out=d, in_=s)))(dst, src),
                   writes=[CB[name]], dma=True, drain=(parts != 128),
                   tag=f"ld_{name}")
        ident = lambda: ab.ap(co["c_ident"], 128)
        p128 = lambda: ab.ap(co["c_p128"], 128)
        ones = lambda: ab.ap(co["c_ones"], 128)
        h64 = lambda: ab.ap(co["c_h64"], 64, 64)
        h128 = lambda: ab.ap(co["c_h128"], 128)
        maskhi = lambda: ab.ap(co["c_maskhi"], 128)
        masklo = lambda: ab.ap(co["c_masklo"], 128)

        kc_o = af.take(2)
        kc_b = sch.buf("kc")
        qs_dst = af.ap(kc_o, 1)
        ks_dst = af.ap(kc_o + 1, 1)
        sch.op(POOL, lambda g: g.dma_start(
            out=qs_dst, in_=_dram_ap(qs_in, [(1, 128), (1, 1)]),
            allow_slow_non_contiguous=True), writes=[kc_b], dma=True, drain=True, tag="ld_qs")
        sch.op(POOL, lambda g: g.dma_start(
            out=ks_dst, in_=_dram_ap(ks_in, [(1, 128), (1, 1)]),
            allow_slow_non_contiguous=True), writes=[kc_b], dma=True, drain=True, tag="ld_ks")
        sch.op(DVE, lambda v: v.tensor_mul(qs_dst, qs_dst, ks_dst),
               reads=[kc_b], writes=[kc_b], tag="kc_mul")
        sch.op(DVE, lambda v: v.tensor_scalar_mul(
            qs_dst, qs_dst, float(1.0 / (CAP * np.sqrt(D)))),
            reads=[kc_b], writes=[kc_b], tag="kc_sc")
        kc_ap = lambda: af.ap(kc_o, 1)

        epsln_o = af.take(1)
        epsln_b = sch.buf("epsln")
        sch.op(DVE, lambda v: v.memset(af.ap(epsln_o, 1), EPS_LN),
               writes=[epsln_b], tag="eps_set")
        epsln_ap = lambda: af.ap(epsln_o, 1)
        epsnr_o = af.take(1)
        epsnr_b = sch.buf("epsnr")
        sch.op(DVE, lambda v: v.memset(af.ap(epsnr_o, 1), EPS_NRM),
               writes=[epsnr_b], tag="epsn_set")
        epsnr_ap = lambda: af.ap(epsnr_o, 1)

        ab.set_mark()
        af.set_mark()

        sch.phase = "P0wprep"
        with ExitStack() as pctx:
            ps_tr = pctx.enter_context(nc.psum_tensor("p0_tr", [128, 2048], BF16))
            ps_tr2 = pctx.enter_context(nc.psum_tensor("p0_tr2", [128, 2048], BF16))
            ps_f = pctx.enter_context(nc.psum_tensor("p0_f", [64, 1024], F32))
            ps_g = pctx.enter_context(nc.psum_tensor("p0_g", [128, 1024], F32))
            pstr_t = [ps_tr, ps_tr2]
            ps_trb = [sch.buf("p0tr0"), sch.buf("p0tr1")]
            psf_b = [sch.buf("p0f0"), sch.buf("p0f1")]
            psg_b = [sch.buf("p0g0"), sch.buf("p0g1")]

            slabf_o = [af.take(2048), af.take(2048), af.take(2048)]
            slabf_b = [sch.buf("sf0"), sch.buf("sf1"), sch.buf("sf2")]
            slabb_o = [ab.take(2048), ab.take(2048)]
            evacb_o = [ab.take(2048), ab.take(2048)]
            slabb_b = [sch.buf("sb0"), sch.buf("sb1")]
            evacb_b = [sch.buf("ev0"), sch.buf("ev1")]

            def slab_load(w_dram, hc, par):
                sf = af.ap(slabf_o[par], 2048)
                src = _dram_ap(w_dram, [(2048, 128), (1, 2048)], hc * 128 * 2048)
                sch.op(POOL, lambda g, s=src, d=sf: g.dma_start(out=d, in_=s),
                       writes=[slabf_b[par]], dma=True, tag=f"w_ld_{hc}")

            for wi, (w_dram, wt_dram) in enumerate([(w1_in, w1t_s), (w3_in, w3t_s)]):
                for hc in range(64):
                    gidx = wi * 64 + hc
                    if gidx == 0:
                        slab_load(w1_in, 0, 0)
                        slab_load(w1_in, 1, 1)
                    nxt = gidx + 2
                    if nxt < 128:
                        slab_load(w1_in if nxt < 64 else w3_in, nxt % 64, nxt % 3)
                    par = gidx % 3
                    bpar = gidx % 2
                    sf = af.ap(slabf_o[par], 2048)
                    sb = ab.ap(slabb_o[bpar], 2048)
                    sch.op(ACT, lambda a, s=sf, d=sb: a.copy(d, s),
                           reads=[slabf_b[par]], writes=[slabb_b[bpar]],
                           tag="w_cvt")
                    ev = ab.ap(evacb_o[gidx % 2], 2048)
                    pst = pstr_t[gidx % 2]
                    for k in range(16):
                        sch.op(PE, (lambda kk, sbv=sb, p=pst: (lambda t: t.transpose(
                            p[:, kk * 128:(kk + 1) * 128],
                            sbv[:, kk * 128:(kk + 1) * 128], ident())))(k),
                            reads=[slabb_b[bpar], CB["c_ident"]],
                            writes=[ps_trb[gidx % 2]], tag="w_tr")
                    sch.op(DVE, lambda v, p=pst, d=ev: v.tensor_copy(d, p[:]),
                           reads=[ps_trb[gidx % 2]], writes=[evacb_b[gidx % 2]],
                           tag="w_ev")
                    dst = _dram_ap(wt_dram, [(HF, 128), (128 * HF, 16), (1, 128)],
                                   hc * 128)
                    esrc = _ap3(ev, [(128, 16), (1, 128)])
                    sch.op(SP, lambda q, s=esrc, d=dst: q.dma_start(out=d, in_=s),
                           reads=[evacb_b[gidx % 2]],
                           dwrites=[DB["w1t"] if wi == 0 else DB["w3t"]],
                           dma=True, tag="w_st")

            t2f_o = [af.take(4096), af.take(4096)]
            t2f_b = [sch.buf("t2f0"), sch.buf("t2f1")]
            t2b_o = [ab.take(4096), ab.take(4096)]
            bb_o = [ab.take(4096), ab.take(4096)]
            t2b_b = [sch.buf("t2b0"), sch.buf("t2b1")]
            bb_b = [sch.buf("bb0"), sch.buf("bb1")]

            def f1_load(g):
                tf2 = af.ap(t2f_o[g % 2], 4096, 64)
                s2 = _dram_ap(w2_in, [(128, 64), (8192, 32), (1, 128)],
                              g * 32 * 8192)
                sch.op(POOL, lambda g_, s=s2, d=tf2: g_.dma_start(out=d, in_=s),
                       writes=[t2f_b[g % 2]], dma=True, tag=f"f1_ld_{g}")

            f1_load(0)
            for g in range(64):
                par = g % 2
                if g + 1 < 64:
                    f1_load(g + 1)
                tf = af.ap(t2f_o[par], 4096, 64)
                tb = ab.ap(t2b_o[par], 4096, 64)
                sch.op(ACT, lambda a, s=tf, d=tb: a.copy(d, s),
                       reads=[t2f_b[par]], writes=[t2b_b[par]], tag="f1_cvt")
                bbv = ab.ap(bb_o[g % 2], 4096, 64)
                for m in range(8):
                    mp = m % 2
                    rhs = _ap3(tb, [(1, 16), (128, 32)], m * 16)
                    psl = ps_f[:, mp * 512:(mp + 1) * 512]
                    sch.op(PE, lambda t, r=rhs, p=psl: t.matmul(
                        p, h64(), r, start=True, stop=True),
                        reads=[t2b_b[par], CB["c_h64"]], writes=[psf_b[mp]],
                        tag="f1_mm")
                    dstb = _ap3(bbv, [(1, 512)], m * 512)
                    sch.op(DVE, lambda v, p=psl, d=dstb: v.tensor_copy(d, p),
                           reads=[psf_b[mp]], writes=[bb_b[g % 2]], tag="f1_ev")
                dst = _dram_ap(b_s, [(128 * E, 64), (E, 128), (1, 32)], g * 32)
                bsrc = _ap3(bbv, [(32, 128), (1, 32)])
                sch.op(SP, lambda q, s=bsrc, d=dst: q.dma_start(out=d, in_=s),
                       reads=[bb_b[g % 2]], dwrites=[DB["b"]], dma=True,
                       tag=f"f1_st_{g}")

            bslab_o = [ab.take(2048) for _ in range(4)]
            w2fe_o = [ab.take(2048), ab.take(2048)]
            bslab_b = [sch.buf(f"bs{i}") for i in range(4)]
            w2fe_b = [sch.buf("we0"), sch.buf("we1")]

            def f2_load(ip):
                bsl2 = ab.ap(bslab_o[ip % 4], 2048)
                s2 = _dram_ap(b_s, [(E, 128), (1, E)], ip * 128 * E)
                sch.op(POOL, lambda g_, s=s2, d=bsl2: g_.dma_start(out=d, in_=s),
                       writes=[bslab_b[ip % 4]], dreads=[DB["b"]], dma=True,
                       tag=f"f2_ld_{ip}")

            for j in range(3):
                f2_load(j)
            for ip in range(64):
                par = ip % 4
                if ip + 3 < 64:
                    f2_load(ip + 3)
                bsl = ab.ap(bslab_o[par], 2048)
                wfe = ab.ap(w2fe_o[ip % 2], 2048)
                for dc in range(4):
                    dp = dc % 2
                    psl = ps_g[:, dp * 512:(dp + 1) * 512]
                    sch.op(PE, lambda t, p=psl, r=bsl[:, dc * 512:(dc + 1) * 512]:
                           t.matmul(p, h128(), r, start=True, stop=True),
                           reads=[bslab_b[par], CB["c_h128"]], writes=[psg_b[dp]],
                           tag="f2_mm")
                    sch.op(DVE, lambda v, p=psl, d=wfe[:, dc * 512:(dc + 1) * 512]:
                           v.tensor_copy(d, p),
                           reads=[psg_b[dp]], writes=[w2fe_b[ip % 2]], tag="f2_ev")
                dst = _dram_ap(w2f_s, [(E, 128), (1, E)], ip * 128 * E)
                sch.op(SP, lambda q, s=wfe, d=dst: q.dma_start(out=d, in_=s),
                       reads=[w2fe_b[ip % 2]], dwrites=[DB["w2f"]], dma=True,
                       tag=f"f2_st_{ip}")

        sch.set_fence()
        ab.reset()
        af.reset()

        sch.phase = "P1qkv"
        with ExitStack() as pctx:
            ps_tr = pctx.enter_context(nc.psum_tensor("p1_tr", [128, 2048], BF16))
            ps_a = pctx.enter_context(nc.psum_tensor("p1_a", [128, 1024], F32))
            ps_r = pctx.enter_context(nc.psum_tensor("p1_r", [128, 1024], F32))
            ps_s = pctx.enter_context(nc.psum_tensor("p1_s", [128, 1024], F32))
            pstr_t = [ps_tr, ps_tr]
            ps_trb0 = sch.buf("p1tr0")
            ps_trb = [ps_trb0, ps_trb0]
            psa_b = [sch.buf("p1a0"), sch.buf("p1a1")]
            psr_b = [sch.buf("p1r0"), sch.buf("p1r1")]
            pss_b = [sch.buf("p1s0"), sch.buf("p1s1")]

            wq_o = ab.take(16 * 2048)
            wk_o = ab.take(16 * 512)
            wv_o = ab.take(16 * 512)
            wq_b = sch.buf("wq"); wk_b = sch.buf("wk"); wv_b = sch.buf("wv")
            xf_o = [af.take(2048), af.take(2048)]
            xf_b = [sch.buf("xf0"), sch.buf("xf1")]
            wstg_o = xf_o
            wstg_b = xf_b
            idx = 0
            for (wt, off, width, buf) in [(wq_in, wq_o, 2048, wq_b),
                                          (wk_in, wk_o, 512, wk_b),
                                          (wv_in, wv_o, 512, wv_b)]:
                for dm in range(16):
                    par = idx % 2
                    idx += 1
                    stg = af.ap(wstg_o[par], width)
                    src = _dram_ap(wt, [(width, 128), (1, width)], dm * 128 * width)
                    sch.op(POOL, lambda g_, s=src, d=stg: g_.dma_start(out=d, in_=s),
                           writes=[wstg_b[par]], dma=True, tag="wqkv_ld")
                    dstb = ab.ap(off + dm * width, width)
                    sch.op(ACT, lambda a, s=stg, d=dstb: a.copy(d, s),
                           reads=[wstg_b[par]], writes=[buf], tag="wqkv_cvt")

            xb_o = [ab.take(2048), ab.take(2048)]
            xb_b = [sch.buf("xb0"), sch.buf("xb1")]
            xt_o = ab.take(16 * 512)
            xt_b = sch.buf("xt")
            cs_o = [af.take(512), af.take(512)]
            sn_o = [af.take(512), af.take(512)]
            cs_b = [sch.buf("cs0"), sch.buf("cs1")]
            sn_b = [sch.buf("sn0"), sch.buf("sn1")]
            qbf_o = [ab.take(512), ab.take(512)]
            qbf_b = [sch.buf("qb0"), sch.buf("qb1")]
            t1_o = [af.take(512), af.take(512)]
            t2_o = [af.take(512), af.take(512)]
            qp_o = [af.take(512), af.take(512)]
            t1_b = [sch.buf("t1a"), sch.buf("t1b")]
            t2_b = [sch.buf("t2a"), sch.buf("t2b")]
            qp_b = [sch.buf("qpa"), sch.buf("qpb")]
            sqb_o = [ab.take(512), ab.take(512)]
            sqb_b = [sch.buf("sqa"), sch.buf("sqb")]
            nrm_o = [af.take(512), af.take(512)]
            nrm_b = [sch.buf("nra"), sch.buf("nrb")]
            rcp_o = [af.take(512), af.take(512)]
            rcp_b = [sch.buf("rca"), sch.buf("rcb")]
            kf_o = [af.take(512), af.take(512)]
            kf_b = [sch.buf("kfa"), sch.buf("kfb")]
            qtc_o = ab.take(16 * 512); qtc_b = sch.buf("qtc")
            ktc_o = ab.take(4 * 512); ktc_b = sch.buf("ktc")
            vt_o = ab.take(4 * 516); vt_b = sch.buf("vt")

            def rope_norm(par, psl, psl_buf, out_off, out_buf, is_k, c):
                qb = ab.ap(qbf_o[par], 512)
                psr = ps_r[:, par * 512:(par + 1) * 512]
                pss = ps_s[:, par * 512:(par + 1) * 512]
                sch.op(ACT, lambda a, p=psl, d=qb: a.copy(d, p),
                       reads=[psl_buf], writes=[qbf_b[par]], tag="r_ev")
                sch.op(PE, lambda t, s=qb, p=psr: t.matmul(p, p128(), s,
                                                           start=True, stop=True),
                       reads=[qbf_b[par], CB["c_p128"]], writes=[psr_b[par]],
                       tag="r_rot")
                csl = af.ap(cs_o[c % 2], 512)
                snl = af.ap(sn_o[c % 2], 512)
                t1 = af.ap(t1_o[par], 512)
                t2 = af.ap(t2_o[par], 512)
                qp = af.ap(qp_o[par], 512)
                sch.op(DVE, lambda v, q=qb: v.tensor_mul(t1, q, csl),
                       reads=[qbf_b[par], cs_b[c % 2]], writes=[t1_b[par]],
                       tag="r_t1")
                sch.op(DVE, lambda v, p=psr: v.tensor_mul(t2, p, snl),
                       reads=[psr_b[par], sn_b[c % 2]], writes=[t2_b[par]],
                       tag="r_t2")
                sch.op(DVE, lambda v: v.tensor_add(qp, t1, t2),
                       reads=[t1_b[par], t2_b[par]], writes=[qp_b[par]], tag="r_qp")
                sq = ab.ap(sqb_o[par], 512)
                sch.op(ACT, lambda a: a.activation(sq, qp, AF.Square),
                       reads=[qp_b[par]], writes=[sqb_b[par]], tag="r_sq")
                sch.op(PE, lambda t, p=pss: t.matmul(p, ones(), sq,
                                                     start=True, stop=True),
                       reads=[sqb_b[par], CB["c_ones"]], writes=[pss_b[par]],
                       tag="r_ss")
                nr = af.ap(nrm_o[par], 512)
                sch.op(ACT, lambda a, p=pss: a.activation(nr, p, AF.Sqrt),
                       reads=[pss_b[par]], writes=[nrm_b[par]], tag="r_sqrt")
                sch.op(ACT, lambda a: a.add(nr, nr, epsnr_ap()),
                       reads=[nrm_b[par], epsnr_b], writes=[nrm_b[par]],
                       tag="r_eps")
                rc = af.ap(rcp_o[par], 512)
                sch.op(DVE, lambda v: v.reciprocal(rc, nr),
                       reads=[nrm_b[par]], writes=[rcp_b[par]], tag="r_rcp")
                outap = ab.ap(out_off, 512)
                if is_k:
                    kf = af.ap(kf_o[par], 512)
                    sch.op(DVE, lambda v: v.tensor_mul(kf, qp, rc),
                           reads=[qp_b[par], rcp_b[par]], writes=[kf_b[par]],
                           tag="r_km")
                    sch.op(DVE, lambda v: v.tensor_scalar_mul(outap, kf, kc_ap()),
                           reads=[kf_b[par], kc_b], writes=[out_buf], tag="r_ks")
                else:
                    sch.op(DVE, lambda v: v.tensor_mul(outap, qp, rc),
                           reads=[qp_b[par], rcp_b[par]], writes=[out_buf],
                           tag="r_qm")

            for c in range(NCH):
                cp = c % 2
                csl = af.ap(cs_o[cp], 512)
                snl = af.ap(sn_o[cp], 512)
                sch.op(POOL, lambda g_, d=csl, s=cin["c_cosT"].ap()[:, c * CH:(c + 1) * CH]:
                       g_.dma_start(out=d, in_=s), writes=[cs_b[cp]], dma=True,
                       tag="cs_ld")
                sch.op(POOL, lambda g_, d=snl, s=cin["c_sinT"].ap()[:, c * CH:(c + 1) * CH]:
                       g_.dma_start(out=d, in_=s), writes=[sn_b[cp]], dma=True,
                       tag="sn_ld")
                for t4 in range(4):
                    par = t4 % 2
                    xf = af.ap(xf_o[par], 2048)
                    xb = ab.ap(xb_o[par], 2048)
                    row0 = (c * 4 + t4) * 128
                    src = _dram_ap(x_in, [(E, 128), (1, E)], row0 * E)
                    sch.op(POOL, lambda g_, s=src, d=xf: g_.dma_start(out=d, in_=s),
                           writes=[xf_b[par]], dma=True, tag=f"x_ld_{c}_{t4}")
                    sch.op(ACT, lambda a, s=xf, d=xb: a.copy(d, s),
                           reads=[xf_b[par]], writes=[xb_b[par]], tag="x_cvt")
                    pst = pstr_t[par]
                    for k in range(16):
                        sch.op(PE, (lambda kk, xbv=xb, p=pst: (lambda t: t.transpose(
                            p[:, kk * 128:(kk + 1) * 128],
                            xbv[:, kk * 128:(kk + 1) * 128], ident())))(k),
                            reads=[xb_b[par], CB["c_ident"]], writes=[ps_trb[par]],
                            tag="x_tr")
                    xt_dst = _ap3(ab.ap(xt_o, 1), [(512, 16), (1, 128)], t4 * 128)
                    sch.op(DVE, lambda v, p=pst, d=xt_dst: v.tensor_copy(d, p[:]),
                           reads=[ps_trb[par]], writes=[xt_b], tag="x_ev")
                for h in range(NH):
                    par = h % 2
                    psl = ps_a[:, par * 512:(par + 1) * 512]
                    for dc in range(16):
                        lhs = ab.ap(wq_o + dc * 2048 + h * 128, 128)
                        rhs = ab.ap(xt_o + dc * 512, 512)
                        sch.op(PE, lambda t, l=lhs, r=rhs, p=psl, st=(dc == 0),
                               sp=(dc == 15): t.matmul(p, l, r, start=st, stop=sp),
                               reads=[wq_b, xt_b], writes=[psa_b[par]], tag="q_mm")
                    rope_norm(par, psl, psa_b[par], qtc_o + h * 512, qtc_b, False, c)
                dst = _dram_ap(qt_s, [(CH, 128), (128 * CH, 16), (1, CH)],
                               c * NH * 128 * CH)
                qsrc = _ap3(ab.ap(qtc_o, 1), [(512, 16), (1, 512)])
                sch.op(SP, lambda q, s=qsrc, d=dst: q.dma_start(out=d, in_=s),
                       reads=[qtc_b], dwrites=[DB["qt"]], dma=True, tag="qt_st")
                for kv in range(NKV):
                    par = kv % 2
                    psl = ps_a[:, par * 512:(par + 1) * 512]
                    for dc in range(16):
                        lhs = ab.ap(wk_o + dc * 512 + kv * 128, 128)
                        rhs = ab.ap(xt_o + dc * 512, 512)
                        sch.op(PE, lambda t, l=lhs, r=rhs, p=psl, st=(dc == 0),
                               sp=(dc == 15): t.matmul(p, l, r, start=st, stop=sp),
                               reads=[wk_b, xt_b], writes=[psa_b[par]], tag="k_mm")
                    rope_norm(par, psl, psa_b[par], ktc_o + kv * 512, ktc_b, True, c)
                dst = _dram_ap(kt_s, [(CH, 128), (128 * CH, 4), (1, CH)],
                               c * NKV * 128 * CH)
                ksrc = _ap3(ab.ap(ktc_o, 1), [(512, 4), (1, 512)])
                sch.op(SP, lambda q, s=ksrc, d=dst: q.dma_start(out=d, in_=s),
                       reads=[ktc_b], dwrites=[DB["kt"]], dma=True, tag="kt_st")
                ones_dst = _ap3(ab.ap(vt_o, 1), [(516, 4), (129, 4), (1, 1)], 128)
                sch.op(DVE, lambda v, d=ones_dst: v.memset(d, 1.0),
                       writes=[vt_b], tag="v_ones")
                for t4 in range(4):
                    par = t4 % 2
                    psl = ps_a[:, par * 512:(par + 1) * 512]
                    for dc in range(16):
                        lhs = ab.ap(xt_o + dc * 512 + t4 * 128, 128)
                        rhs = ab.ap(wv_o + dc * 512, 512)
                        sch.op(PE, lambda t, l=lhs, r=rhs, p=psl, st=(dc == 0),
                               sp=(dc == 15): t.matmul(p, l, r, start=st, stop=sp),
                               reads=[wv_b, xt_b], writes=[psa_b[par]], tag="v_mm")
                    vdst = _ap3(ab.ap(vt_o, 1), [(129, 4), (1, 128)], t4 * 516)
                    sch.op(DVE, lambda v, p=psl, d=vdst: v.tensor_copy(d, p),
                           reads=[psa_b[par]], writes=[vt_b], tag="v_ev")
                    vst = _dram_ap(v_s, [(516, 128), (1, 516)],
                                   (c * 4 + t4) * 128 * 516)
                    vsr = _ap3(ab.ap(vt_o, 1), [(1, 516)], t4 * 516)
                    sch.op(SP, lambda q, s=vsr, d=vst: q.dma_start(out=d, in_=s),
                           reads=[vt_b], dwrites=[DB["v"]], dma=True, tag="v_st")

        sch.set_fence()
        ab.reset()
        af.reset()

        sch.phase = "P2scores"
        with ExitStack() as pctx:
            ps_sc = pctx.enter_context(nc.psum_tensor("p2_sc", [128, 1536], F32))
            ps_av = pctx.enter_context(nc.psum_tensor("p2_av", [128, 2048], F32))
            ps_at = pctx.enter_context(nc.psum_tensor("p2_at", [128, 512], BF16))
            pssc_b = sch.buf("ps_sc")
            psav_b = sch.buf("ps_av")
            psat_b = sch.buf("ps_at")

            qin_o = [ab.take(16 * 512), ab.take(16 * 512)]
            qin_b = [sch.buf("qin0"), sch.buf("qin1")]
            kr_o = ab.take(6 * 4 * 512)
            kr_b = [sch.buf(f"kr{s_}") for s_ in range(6)]
            vr_o = ab.take(6 * 4 * 516)
            vr_b = [sch.buf(f"vr{s_}") for s_ in range(6)]
            tb_o = af.take(1536)
            tb_b = sch.buf("tb")
            eb_o = [ab.take(1536), ab.take(1536)]
            eb_b = [sch.buf("eb0"), sch.buf("eb1")]
            adiv_o = ab.take(4 * 128)
            adiv_b = sch.buf("adiv")
            rcd_o = af.take(4)
            rcd_b = sch.buf("rcd")
            aotb_o = [ab.take(512), ab.take(512)]
            aotb_b = [sch.buf("ao0"), sch.buf("ao1")]

            def load_kv_slot(cc):
                slot = cc % 6
                kd = _ap3(ab.ap(kr_o + slot * 4 * 512, 1), [(512, 4), (1, 512)])
                ks = _dram_ap(kt_s, [(CH, 128), (128 * CH, 4), (1, CH)],
                              cc * NKV * 128 * CH)
                sch.op(POOL, lambda g_, d=kd, s=ks: g_.dma_start(out=d, in_=s),
                       writes=[kr_b[slot]], dreads=[DB["kt"]], dma=True,
                       tag=f"kr_{cc}")
                vd = _ap3(ab.ap(vr_o + slot * 4 * 516, 1), [(516, 4), (1, 516)])
                vs2 = _dram_ap(v_s, [(516, 128), (128 * 516, 4), (1, 516)],
                               cc * 4 * 128 * 516)
                sch.op(POOL, lambda g_, d=vd, s=vs2: g_.dma_start(out=d, in_=s),
                       writes=[vr_b[slot]], dreads=[DB["v"]], dma=True,
                       tag=f"vr_{cc}")

            def load_q_chunk(cc):
                par = cc % 2
                qd = _ap3(ab.ap(qin_o[par], 1), [(512, 16), (1, 512)])
                qs2 = _dram_ap(qt_s, [(CH, 128), (128 * CH, 16), (1, CH)],
                               cc * NH * 128 * CH)
                sch.op(POOL, lambda g_, d=qd, s=qs2: g_.dma_start(out=d, in_=s),
                       writes=[qin_b[par]], dreads=[DB["qt"]], dma=True,
                       tag=f"q_{cc}")

            for cc in range(3):
                load_kv_slot(cc)
            load_q_chunk(0)
            pending_tail = [None]

            for c in range(NCH):
                if c + 1 < NCH:
                    load_q_chunk(c + 1)
                if c + 3 < NCH:
                    load_kv_slot(c + 3)
                kb_lo = max(0, 4 * c - 8)
                kb_hi = min(NT - 1, 4 * c + 11)
                kbs = list(range(kb_lo, kb_hi + 1))
                for h in range(NH):
                    kv = h // 4
                    qrhs = ab.ap(qin_o[c % 2] + h * 512, 512)
                    started = [False] * 4
                    waves = [kbs[w0:w0 + 3] for w0 in range(0, len(kbs), 3)]

                    def emit_scores(wave):
                        for wi, kb in enumerate(wave):
                            slot = (kb // 4) % 6
                            lhs = ab.ap(kr_o + slot * 4 * 512 + kv * 512 +
                                        (kb % 4) * 128, 128)
                            psl = ps_sc[:, wi * 512:(wi + 1) * 512]
                            sch.op(PE, lambda t, l=lhs, r=qrhs, p=psl:
                                   t.matmul(p, l, r, start=True, stop=True),
                                   reads=[kr_b[slot], qin_b[c % 2]],
                                   writes=[pssc_b], tag="sc_mm")

                    def emit_acts(wave, ebp):
                        nW = len(wave) * 512
                        tb = af.ap(tb_o, nW)
                        sch.op(ACT, lambda a, d=tb, n=nW: a.activation(
                            d, ps_sc[:, 0:n], AF.Tanh),
                            reads=[pssc_b], writes=[tb_b], tag="sc_tanh")
                        eb = ab.ap(eb_o[ebp], nW)
                        sch.op(ACT, lambda a, s=tb, d=eb: a.activation(
                            d, s, AF.Exp, scale=CAP),
                            reads=[tb_b], writes=[eb_b[ebp]], tag="sc_exp")

                    def emit_av(wave, ebp):
                        for wi, kb in enumerate(wave):
                            slot = (kb // 4) % 6
                            for qs in range(4):
                                qi = 4 * c + qs
                                dblk = kb - qi
                                if abs(dblk) > 8:
                                    continue
                                esl = ab.ap(eb_o[ebp] + wi * 512 + qs * 128, 128)
                                if dblk == 8:
                                    sch.op(DVE, lambda v, e=esl: v.tensor_mul(
                                        e, e, maskhi()),
                                        reads=[eb_b[ebp], CB["c_maskhi"]],
                                        writes=[eb_b[ebp]], tag="m_hi")
                                elif dblk == -8:
                                    sch.op(DVE, lambda v, e=esl: v.tensor_mul(
                                        e, e, masklo()),
                                        reads=[eb_b[ebp], CB["c_masklo"]],
                                        writes=[eb_b[ebp]], tag="m_lo")
                                vrhs = ab.ap(vr_o + slot * 4 * 516 +
                                             (kb % 4) * 516 + kv * 129, 129)
                                psl = ps_av[:, qs * 512:qs * 512 + 129]
                                st = not started[qs]
                                started[qs] = True
                                sp = (kb == min(NT - 1, qi + 8))
                                sch.op(PE, lambda t, l=esl, r=vrhs, p=psl, s=st,
                                       s2=sp: t.matmul(p, l, r, start=s, stop=s2),
                                       reads=[eb_b[ebp], vr_b[slot]],
                                       writes=[psav_b], tag="av_mm")

                    if pending_tail[0] is not None:
                        pass
                    emit_scores(waves[0])
                    if pending_tail[0] is not None:
                        pending_tail[0]()
                        pending_tail[0] = None
                    for w in range(len(waves)):
                        emit_acts(waves[w], w % 2)
                        if w + 1 < len(waves):
                            emit_scores(waves[w + 1])
                        emit_av(waves[w], w % 2)

                    def unit_tail(c=c, h=h):
                        for qs in range(4):
                            rc = af.ap(rcd_o + qs, 1)
                            den = ps_av[:, qs * 512 + 128:qs * 512 + 129]
                            sch.op(DVE, lambda v, d=rc, s=den: v.reciprocal(d, s),
                                   reads=[psav_b], writes=[rcd_b], tag="d_rcp")
                            num = ps_av[:, qs * 512:qs * 512 + 128]
                            ad = ab.ap(adiv_o + qs * 128, 128)
                            sch.op(DVE, lambda v, d=ad, s=num, r=rc:
                                   v.tensor_scalar_mul(d, s, r),
                                   reads=[psav_b, rcd_b], writes=[adiv_b],
                                   tag="d_mul")
                        for qs in range(4):
                            ad = ab.ap(adiv_o + qs * 128, 128)
                            sch.op(PE, lambda t, s=ad,
                                   p=ps_at[:, qs * 128:(qs + 1) * 128]:
                                   t.transpose(p, s, ident()),
                                   reads=[adiv_b, CB["c_ident"]], writes=[psat_b],
                                   tag="ao_tr")
                        hp = h % 2
                        ao = ab.ap(aotb_o[hp], 512)
                        sch.op(DVE, lambda v, d=ao: v.tensor_copy(d, ps_at[:]),
                               reads=[psat_b], writes=[aotb_b[hp]], tag="ao_ev")
                        adst = _dram_ap(aot_s, [(CH, 128), (1, CH)],
                                        (c * 16 + h) * 128 * CH)
                        sch.op(SP, lambda q, s=ao, d=adst: q.dma_start(out=d,
                                                                       in_=s),
                               reads=[aotb_b[hp]], dwrites=[DB["aot"]], dma=True,
                               tag="ao_st")

                    pending_tail[0] = unit_tail

            if pending_tail[0] is not None:
                pending_tail[0]()
                pending_tail[0] = None

        sch.set_fence()
        ab.reset()
        af.reset()

        sch.phase = "P3wo"
        with ExitStack() as pctx:
            ps_tr = pctx.enter_context(nc.psum_tensor("p3_tr", [128, 2048], BF16))
            ps_tr2 = pctx.enter_context(nc.psum_tensor("p3_tr2", [128, 2048], BF16))
            ps_w = pctx.enter_context(nc.psum_tensor("p3_w", [128, 1024], F32))
            pstr_t = [ps_tr, ps_tr2]
            ps_trb = [sch.buf("p3tr0"), sch.buf("p3tr1")]
            psw_b = [sch.buf("p3w0"), sch.buf("p3w1")]

            wo_o = ab.take(16 * 2048)
            wo_b = sch.buf("wo")
            wof_o = [af.take(2048)]
            wof_b = [sch.buf("wof0")]
            for hdc in range(16):
                par = 0
                stg = af.ap(wof_o[par], 2048)
                src = _dram_ap(wo_in, [(E, 128), (1, E)], hdc * 128 * E)
                sch.op(POOL, lambda g_, s=src, d=stg: g_.dma_start(out=d, in_=s),
                       writes=[wof_b[par]], dma=True, tag="wo_ld")
                dstb = ab.ap(wo_o + hdc * 2048, 2048)
                sch.op(ACT, lambda a, s=stg, d=dstb: a.copy(d, s),
                       reads=[wof_b[par]], writes=[wo_b], tag="wo_cvt")

            aoin_o = [ab.take(16 * 512), ab.take(16 * 512)]
            aoin_b = [sch.buf("ai0"), sch.buf("ai1")]
            x4f_o = [af.take(4 * 2048)]
            x4f_b = [sch.buf("x40")]
            r_o = af.take(2048)
            r_b = sch.buf("r")
            st_o = af.take(28)
            st_b = sch.buf("stats")
            x1b_o = [ab.take(2048), ab.take(2048)]
            x1b_b = [sch.buf("x1b0"), sch.buf("x1b1")]
            ex1_o = [ab.take(2048), ab.take(2048)]
            ex1_b = [sch.buf("ex0"), sch.buf("ex1")]

            def layer_norm(src_ap, src_buf, out_ap, out_buf, tag):
                bn24 = af.ap(st_o, 24)
                bn2 = af.ap(st_o + 24, 2)
                for i in range(4):
                    bno = af.ap(st_o + i * 6, 6)
                    bni = _ap3(src_ap, [(1, 512)], i * 512)
                    sch.op(DVE, (lambda o2, i2: (lambda v: v.bn_stats(o2, i2)))(
                        bno, bni),
                        reads=[src_buf], writes=[st_b], tag=tag + "_bns")
                sch.op(DVE, lambda v: v.bn_aggr(bn2, bn24),
                       reads=[st_b], writes=[st_b], tag=tag + "_bna")
                mean = af.ap(st_o + 24, 1)
                var = af.ap(st_o + 25, 1)
                sd = af.ap(st_o + 26, 1)
                sch.op(ACT, lambda a: a.activation(sd, var, AF.Sqrt,
                                                   bias=epsln_ap()),
                       reads=[st_b, epsln_b], writes=[st_b], tag=tag + "_sd")
                rs = af.ap(st_o + 27, 1)
                sch.op(DVE, lambda v: v.reciprocal(rs, sd),
                       reads=[st_b], writes=[st_b], tag=tag + "_rs")
                sch.op(DVE, lambda v: v.tensor_scalar(
                    out_ap, src_ap, mean, rs, ALU.subtract, ALU.mult),
                    reads=[src_buf, st_b], writes=[out_buf], tag=tag + "_out")

            def aoin_load(cc):
                aod = _ap3(ab.ap(aoin_o[cc % 2], 1), [(512, 16), (1, 512)])
                aos = _dram_ap(aot_s, [(CH, 128), (128 * CH, 16), (1, CH)],
                               cc * 16 * 128 * CH)
                sch.op(POOL, lambda g_, d=aod, s=aos: g_.dma_start(out=d, in_=s),
                       writes=[aoin_b[cc % 2]], dreads=[DB["aot"]], dma=True,
                       tag=f"ai_{cc}")

            aoin_load(0)
            for c in range(NCH):
                cp = c % 2
                if c + 1 < NCH:
                    aoin_load(c + 1)
                x4d = _ap3(af.ap(x4f_o[0], 1), [(2048, 4), (1, 2048)])
                xsrc = _dram_ap(x_in, [(E, 128), (128 * E, 4), (1, E)],
                                c * 4 * 128 * E)
                sch.op(POOL, lambda g_, d=x4d, s=xsrc: g_.dma_start(out=d, in_=s),
                       writes=[x4f_b[0]], dma=True, tag=f"x4_{c}")
                for ts in range(4):
                    for dmc in range(4):
                        par = dmc % 2
                        psl = ps_w[:, par * 512:(par + 1) * 512]
                        for hdc in range(16):
                            lhs = ab.ap(aoin_o[cp] + hdc * 512 + ts * 128, 128)
                            rhs = ab.ap(wo_o + hdc * 2048 + dmc * 512, 512)
                            sch.op(PE, lambda t, l=lhs, r2=rhs, p=psl,
                                   st=(hdc == 0), sp=(hdc == 15):
                                   t.matmul(p, l, r2, start=st, stop=sp),
                                   reads=[aoin_b[cp], wo_b], writes=[psw_b[par]],
                                   tag="wo_mm")
                        xsl = af.ap(x4f_o[0] + ts * 2048 + dmc * 512, 512)
                        rsl = af.ap(r_o + dmc * 512, 512)
                        sch.op(DVE, lambda v, p=psl, xs=xsl, d=rsl:
                               v.scalar_tensor_tensor(d, p, 1.0, xs, ALU.mult,
                                                      ALU.add),
                               reads=[psw_b[par], x4f_b[0]], writes=[r_b],
                               tag="wo_res")
                    if _DEBUG:
                        rdst = _dram_ap(dbg_r, [(E, 128), (1, E)],
                                        (c * 4 + ts) * 128 * E)
                        sch.op(SP, lambda q, d=rdst, s=af.ap(r_o, 2048):
                               q.dma_start(out=d, in_=s),
                               reads=[r_b], dma=True, tag="dbg_r")
                    x1t = ab.ap(x1b_o[ts % 2], 2048)
                    layer_norm(af.ap(r_o, 2048), r_b, x1t, x1b_b[ts % 2], "ln1")
                    if _DEBUG:
                        sdst = _dram_ap(dbg_st, [(28, 128), (1, 28)],
                                        (c * 4 + ts) * 128 * 28)
                        sch.op(SP, lambda q, d=sdst, s=af.ap(st_o, 28):
                               q.dma_start(out=d, in_=s),
                               reads=[st_b], dma=True, tag="dbg_st")
                    x1dst = _dram_ap(x1_s, [(E, 128), (1, E)],
                                     (c * 4 + ts) * 128 * E)
                    sch.op(SP, lambda q, s=x1t, d=x1dst: q.dma_start(out=d, in_=s),
                           reads=[x1b_b[ts % 2]], dwrites=[DB["x1"]], dma=True,
                           tag="x1_st")
                    pst = pstr_t[ts % 2]
                    for k in range(16):
                        sch.op(PE, (lambda kk, xv=x1t, p=pst: (lambda t:
                               t.transpose(p[:, kk * 128:(kk + 1) * 128],
                                           xv[:, kk * 128:(kk + 1) * 128],
                                           ident())))(k),
                            reads=[x1b_b[ts % 2], CB["c_ident"]],
                            writes=[ps_trb[ts % 2]], tag="x1_tr")
                    ex1 = ab.ap(ex1_o[ts % 2], 2048)
                    sch.op(DVE, lambda v, p=pst, d=ex1: v.tensor_copy(d, p[:]),
                           reads=[ps_trb[ts % 2]], writes=[ex1_b[ts % 2]],
                           tag="x1_ev")
                    xtd = _dram_ap(x1t_s, [(S, 128), (128 * S, 16), (1, 128)],
                                   c * 512 + ts * 128)
                    es = _ap3(ab.ap(ex1_o[ts % 2], 1), [(128, 16), (1, 128)])
                    sch.op(SP, lambda q, s=es, d=xtd: q.dma_start(out=d, in_=s),
                           reads=[ex1_b[ts % 2]], dwrites=[DB["x1t"]], dma=True,
                           tag="x1t_st")

        sch.set_fence()
        ab.reset()
        af.reset()

        sch.phase = "P4ffn"
        with ExitStack() as pctx:
            ps_u1 = pctx.enter_context(nc.psum_tensor("p4_u1", [128, 1024], F32))
            ps_u3 = pctx.enter_context(nc.psum_tensor("p4_u3", [128, 1024], F32))
            ps_d = pctx.enter_context(nc.psum_tensor("p4_d", [128, 2048], F32))
            psu1_b = [sch.buf("u10"), sch.buf("u11")]
            psu3_b = [sch.buf("u30"), sch.buf("u31")]
            psd_b = [sch.buf(f"pd{i}") for i in range(4)]

            x1tin_o = ab.take(16 * 512)
            x1tin_b = sch.buf("x1tin")
            x1tok_o = ab.take(4 * 2048)
            x1tok_b = sch.buf("x1tok")
            w1g_o = [ab.take(4096), ab.take(4096)]
            w3g_o = [ab.take(4096), ab.take(4096)]
            w1g_b = [sch.buf("w1g0"), sch.buf("w1g1")]
            w3g_b = [sch.buf("w3g0"), sch.buf("w3g1")]
            w2g_o = [ab.take(4096), ab.take(4096)]
            w2g_b = [sch.buf("w2g0"), sch.buf("w2g1")]
            gt_o = ab.take(64 * 512)
            gt_b = sch.buf("gt")
            s1_o = [af.take(512), af.take(512)]
            s1_b = [sch.buf("s10"), sch.buf("s11")]
            r4_o = [af.take(2048) for _ in range(4)]
            r4_b = [sch.buf(f"r4_{i}") for i in range(4)]
            outf_o = af.take(2048)
            outf_b = sch.buf("outf")
            st_o = af.take(28)
            st_b = sch.buf("st4")

            for p in range(NCH):
                if p == 0:
                    x1td = _ap3(ab.ap(x1tin_o, 1), [(512, 16), (1, 512)])
                    x1src = _dram_ap(x1t_s, [(S, 128), (128 * S, 16), (1, 512)], 0)
                    sch.op(POOL, lambda g_, d=x1td, s=x1src:
                           g_.dma_start(out=d, in_=s),
                           writes=[x1tin_b], dreads=[DB["x1t"]], dma=True,
                           tag="fx1t_0")
                xtk = _ap3(ab.ap(x1tok_o, 1), [(2048, 4), (1, 2048)])
                xts = _dram_ap(x1_s, [(E, 128), (128 * E, 4), (1, E)],
                               p * 4 * 128 * E)
                sch.op(POOL, lambda g_, d=xtk, s=xts: g_.dma_start(out=d, in_=s),
                       writes=[x1tok_b], dreads=[DB["x1"]], dma=True,
                       tag=f"fx1_{p}")

                def load_wgroup(g, par):
                    for (wsrc, woff, wbuf) in [(w1t_s, w1g_o, w1g_b),
                                               (w3t_s, w3g_o, w3g_b)]:
                        d = _ap3(ab.ap(woff[par], 1), [(256, 16), (1, 256)])
                        s = _dram_ap(wsrc, [(HF, 128), (128 * HF, 16), (1, 256)],
                                     g * 256)
                        sch.op(SP, lambda q, d2=d, s2=s: q.dma_start(out=d2, in_=s2),
                               writes=[wbuf[par]],
                               dreads=[DB["w1t"] if wsrc is w1t_s else DB["w3t"]],
                               dma=True, tag=f"wg_{p}_{g}")

                load_wgroup(0, 0)
                for g in range(32):
                    par = g % 2
                    if g + 1 < 32:
                        load_wgroup(g + 1, (g + 1) % 2)
                    for hh in range(2):
                        hc = g * 2 + hh
                        up = hc % 2
                        psl1 = ps_u1[:, up * 512:(up + 1) * 512]
                        psl3 = ps_u3[:, up * 512:(up + 1) * 512]
                        for dc in range(16):
                            l1 = ab.ap(w1g_o[par] + dc * 256 + hh * 128, 128)
                            l3 = ab.ap(w3g_o[par] + dc * 256 + hh * 128, 128)
                            rr = ab.ap(x1tin_o + dc * 512, 512)
                            sch.op(PE, lambda t, l=l1, r=rr, p2=psl1, st=(dc == 0),
                                   sp=(dc == 15): t.matmul(p2, l, r, start=st,
                                                           stop=sp),
                                   reads=[w1g_b[par], x1tin_b],
                                   writes=[psu1_b[up]], tag="u1_mm")
                            sch.op(PE, lambda t, l=l3, r=rr, p2=psl3, st=(dc == 0),
                                   sp=(dc == 15): t.matmul(p2, l, r, start=st,
                                                           stop=sp),
                                   reads=[w3g_b[par], x1tin_b],
                                   writes=[psu3_b[up]], tag="u3_mm")
                        s1 = af.ap(s1_o[up], 512)
                        sch.op(ACT, lambda a, d=s1, p2=psl1: a.activation(
                            d, p2, AF.Silu),
                            reads=[psu1_b[up]], writes=[s1_b[up]], tag="silu")
                        gsl = ab.ap(gt_o + hc * 512, 512)
                        sch.op(DVE, lambda v, d=gsl, a2=s1, p2=psl3:
                               v.tensor_mul(d, a2, p2),
                               reads=[s1_b[up], psu3_b[up]], writes=[gt_b],
                               tag="g_mul")
                if p + 1 < NCH:
                    x1td2 = _ap3(ab.ap(x1tin_o, 1), [(512, 16), (1, 512)])
                    x1src2 = _dram_ap(x1t_s, [(S, 128), (128 * S, 16), (1, 512)],
                                      (p + 1) * 512)
                    sch.op(POOL, lambda g_, d=x1td2, s=x1src2:
                           g_.dma_start(out=d, in_=s),
                           writes=[x1tin_b], dreads=[DB["x1t"]], dma=True,
                           tag=f"fx1t_{p + 1}")
                for dmc in range(4):
                    def load_w2g(hb, par):
                        d = _ap3(ab.ap(w2g_o[par], 1), [(512, 8), (1, 512)])
                        s = _dram_ap(w2f_s, [(E, 128), (128 * E, 8), (1, 512)],
                                     hb * 8 * 128 * E + dmc * 512)
                        sch.op(SP, lambda q, d2=d, s2=s: q.dma_start(out=d2,
                                                                     in_=s2),
                               writes=[w2g_b[par]], dreads=[DB["w2f"]], dma=True,
                               tag=f"w2g_{p}_{dmc}_{hb}")

                    load_w2g(0, 0)
                    for hb in range(8):
                        par = hb % 2
                        if hb + 1 < 8:
                            load_w2g(hb + 1, (hb + 1) % 2)
                        for hh in range(8):
                            hc = hb * 8 + hh
                            rhs = ab.ap(w2g_o[par] + hh * 512, 512)
                            for ts in range(4):
                                lhs = ab.ap(gt_o + hc * 512 + ts * 128, 128)
                                psl = ps_d[:, ts * 512:(ts + 1) * 512]
                                sch.op(PE, lambda t, l=lhs, r=rhs, p2=psl,
                                       st=(hc == 0), sp=(hc == 63):
                                       t.matmul(p2, l, r, start=st, stop=sp),
                                       reads=[gt_b, w2g_b[par]],
                                       writes=[psd_b[ts]], tag="dn_mm")
                    for ts in range(4):
                        psl = ps_d[:, ts * 512:(ts + 1) * 512]
                        xsl = ab.ap(x1tok_o + ts * 2048 + dmc * 512, 512)
                        rsl = af.ap(r4_o[ts] + dmc * 512, 512)
                        sch.op(DVE, lambda v, p2=psl, xs=xsl, d=rsl:
                               v.scalar_tensor_tensor(d, p2, 1.0, xs, ALU.mult,
                                                      ALU.add),
                               reads=[psd_b[ts], x1tok_b], writes=[r4_b[ts]],
                               tag="dn_res")
                for ts in range(4):
                    rr = af.ap(r4_o[ts], 2048)
                    of = af.ap(outf_o, 2048)
                    bn24 = af.ap(st_o, 24)
                    bn2 = af.ap(st_o + 24, 2)
                    for i in range(4):
                        bno = af.ap(st_o + i * 6, 6)
                        bni = _ap3(rr, [(1, 512)], i * 512)
                        sch.op(DVE, (lambda o2, i2: (lambda v: v.bn_stats(
                            o2, i2)))(bno, bni),
                            reads=[r4_b[ts]], writes=[st_b], tag="ln2_bns")
                    sch.op(DVE, lambda v: v.bn_aggr(bn2, bn24),
                           reads=[st_b], writes=[st_b], tag="ln2_bna")
                    mean = af.ap(st_o + 24, 1)
                    var = af.ap(st_o + 25, 1)
                    sd = af.ap(st_o + 26, 1)
                    sch.op(ACT, lambda a: a.activation(sd, var, AF.Sqrt,
                                                       bias=epsln_ap()),
                           reads=[st_b, epsln_b], writes=[st_b], tag="ln2_sd")
                    rs = af.ap(st_o + 27, 1)
                    sch.op(DVE, lambda v: v.reciprocal(rs, sd),
                           reads=[st_b], writes=[st_b], tag="ln2_rs")
                    sch.op(DVE, lambda v, s=rr, d=of: v.tensor_scalar(
                        d, s, mean, rs, ALU.subtract, ALU.mult),
                        reads=[r4_b[ts], st_b], writes=[outf_b], tag="ln2_out")
                    odst = _dram_ap(out_t, [(E, 128), (1, E)],
                                    (p * 4 + ts) * 128 * E)
                    sch.op(ACT, lambda a, s=of, d=odst: a.dma_start(out=d, in_=s),
                           reads=[outf_b], dma=True, tag="out_st")

        finals = dict(sch.counts)
        sems = {}
        for name in _ALL_SEMS:
            sems[name] = ctx.enter_context(nc.semaphore(name))
        block = ctx.enter_context(nc.Block())
        sch.emit(block, sems, nc)

        @block.sync
        def _(sync):
            for qn in (Q_ACT, Q_SP, Q_POOL):
                if finals[qn]:
                    sync.wait_ge(sems[qn], finals[qn])

    return nc


_NC_CACHE = None


def _get_nc():
    global _NC_CACHE
    if _NC_CACHE is None:
        _NC_CACHE = build_nc()
    return _NC_CACHE


def kernel(L0_attn_hidden_states, L0_attn_wq, L0_attn_wk, L0_attn_wv, L0_attn_wo,
           L0_attn_query_scale, L0_attn_key_scale, L0_ffn_W1, L0_ffn_W2, L0_ffn_W3):
    consts = host_constants()
    nc = _get_nc()
    xs = np.ascontiguousarray(np.asarray(L0_attn_hidden_states, dtype=np.float32))
    shared = {
        "wq": np.ascontiguousarray(np.asarray(L0_attn_wq, np.float32)),
        "wk": np.ascontiguousarray(np.asarray(L0_attn_wk, np.float32)),
        "wv": np.ascontiguousarray(np.asarray(L0_attn_wv, np.float32)),
        "wo": np.ascontiguousarray(np.asarray(L0_attn_wo, np.float32)),
        "qscale": np.ascontiguousarray(np.asarray(L0_attn_query_scale, np.float32)),
        "kscale": np.ascontiguousarray(np.asarray(L0_attn_key_scale, np.float32)),
        "w1": np.ascontiguousarray(np.asarray(L0_ffn_W1, np.float32)),
        "w2": np.ascontiguousarray(np.asarray(L0_ffn_W2, np.float32)),
        "w3": np.ascontiguousarray(np.asarray(L0_ffn_W3, np.float32)),
    }
    shared.update(consts)
    in_maps = []
    for b in range(B):
        m = dict(shared)
        m["x"] = np.ascontiguousarray(xs[b])
        in_maps.append(m)
    res = run_bass_kernel_spmd(nc, in_maps, core_ids=list(range(B)))
    out = np.stack([r["out"] for r in res.results], axis=0)
    return out


# version 07
# speedup vs baseline: 1.1835x; 1.0090x over previous
import dataclasses
import numpy as np
import ml_dtypes
import concourse.bass as bass
import concourse.mybir as mybir
from concourse.bass_utils import run_bass_kernel_spmd
from contextlib import ExitStack

F32 = mybir.dt.float32
BF16 = mybir.dt.bfloat16
AF = mybir.ActivationFunctionType
ALU = mybir.AluOpType
AX = mybir.AxisListType

B = 8
S = 4096
E = 2048
NH = 16
NKV = 4
D = 128
CAP = 50.0
EPS_NRM = 1e-6
EPS_LN = 1e-5
HF = 8192
CH = 512
NCH = S // CH
NT = S // 128

_DEBUG = False


def _sylvester(n):
    H = np.array([[1.0]], dtype=np.float64)
    while H.shape[0] < n:
        H = np.block([[H, H], [H, -H]])
    return H


def host_constants():
    c = {}
    c["c_ident"] = np.eye(128, dtype=np.float32).astype(ml_dtypes.bfloat16)
    P = np.zeros((128, 128), np.float32)
    for i in range(64):
        P[i + 64, i] = -1.0
        P[i, i + 64] = 1.0
    c["c_p128"] = P.astype(ml_dtypes.bfloat16)
    c["c_ones"] = np.ones((128, 128), np.float32).astype(ml_dtypes.bfloat16)
    c["c_h64"] = (_sylvester(64) / 64.0).astype(ml_dtypes.bfloat16)
    c["c_h128"] = (_sylvester(128) / np.sqrt(2.0)).astype(ml_dtypes.bfloat16)
    kl = np.arange(128)[:, None]
    ql = np.arange(128)[None, :]
    c["c_maskhi"] = (kl <= ql).astype(np.float32).astype(ml_dtypes.bfloat16)
    c["c_masklo"] = (ql <= kl).astype(np.float32).astype(ml_dtypes.bfloat16)
    half = D // 2
    inv = 1.0 / 10000.0 ** (np.arange(half, dtype=np.float32) / half)
    ang = np.outer(np.arange(S, dtype=np.float32), inv)
    ang = np.concatenate([ang, ang], axis=-1)
    c["c_cosT"] = np.ascontiguousarray(np.cos(ang).T.astype(np.float32))
    c["c_sinT"] = np.ascontiguousarray(np.sin(ang).T.astype(np.float32))
    return c


PE, ACT, DVE, POOL, SP = "pe", "act", "dve", "pool", "sp"
Q_POOL, Q_SP, Q_ACT = "q_pool", "q_sp", "q_act"
_DMA_Q = {POOL: Q_POOL, SP: Q_SP, ACT: Q_ACT}
_ALL_SEMS = (PE, ACT, DVE, POOL, SP, Q_POOL, Q_SP, Q_ACT)


class Buf:
    __slots__ = ("writer", "readers", "name")

    def __init__(self, name, fence):
        self.writer = None
        self.readers = list(fence)
        self.name = name


class OpRec:
    __slots__ = ("build", "waits", "sem", "amount", "tag", "phase")

    def __init__(self, build, waits, sem, amount, tag, phase):
        self.build = build
        self.waits = waits
        self.sem = sem
        self.amount = amount
        self.tag = tag
        self.phase = phase


class DramBuf:
    __slots__ = ("writers", "name")

    def __init__(self, name):
        self.writers = []
        self.name = name


class Sched:
    def __init__(self):
        self.ops = {k: [] for k in (PE, ACT, DVE, POOL, SP)}
        self.counts = {k: 0 for k in _ALL_SEMS}
        self.fence = []
        self.phase = "init"

    def buf(self, name):
        return Buf(name, self.fence)

    def set_fence(self):
        self.fence = [(k, v) for k, v in self.counts.items() if v > 0]

    def op(self, eng, build, reads=(), writes=(), dreads=(), dwrites=(),
           dma=False, drain=False, tag=""):
        sem = _DMA_Q[eng] if dma else eng
        amount = 16 if dma else 1
        waits = {}

        def add(tok):
            if tok is None:
                return
            s, v = tok
            if (not dma) and eng == PE and s == PE:
                return
            if waits.get(s, 0) < v:
                waits[s] = v

        for b in reads:
            add(b.writer)
        for b in writes:
            add(b.writer)
            for t in b.readers:
                add(t)
        for db in dreads:
            for t in db.writers:
                add(t)
        self.counts[sem] += amount
        tok = (sem, self.counts[sem])
        self.ops[eng].append(OpRec(build, waits, sem, amount, tag, self.phase))
        for b in reads:
            b.readers.append(tok)
        for b in writes:
            b.writer = tok
            b.readers = []
        for db in dwrites:
            db.writers.append(tok)
        if drain:
            assert dma
            self.ops[eng].append(OpRec(
                lambda e: e.nop(), {sem: self.counts[sem]}, eng, 1,
                tag + "_drain", self.phase))
            self.counts[eng] += 1
        return tok

    def emit(self, block, sems, nc=None):
        eng_of = {PE: "tensor", ACT: "scalar", DVE: "vector", POOL: "gpsimd",
                  SP: "sync"}
        for eng_name, ops in self.ops.items():
            if not ops:
                continue
            deco = getattr(block, eng_of[eng_name])

            def make(ops_local):
                def _(eng):
                    from contextlib import ExitStack as _ES
                    waited = {}
                    cur_phase = [None]
                    stk = _ES()
                    for rec in ops_local:
                        if nc is not None and rec.phase != cur_phase[0]:
                            stk.close()
                            stk = _ES()
                            stk.enter_context(nc.named_scope(rec.phase))
                            cur_phase[0] = rec.phase
                        for s, v in sorted(rec.waits.items()):
                            if waited.get(s, 0) >= v:
                                continue
                            eng.wait_ge(sems[s], v)
                            waited[s] = v
                        rec.build(eng).then_inc(sems[rec.sem], rec.amount)
                    stk.close()
                return _

            deco(make(ops))


class Arena:
    def __init__(self, nc, name, dtype, n_elems):
        self.t = nc.alloc_sbuf_tensor(name, [128, n_elems], dtype)
        self.off = 0
        self.n = n_elems
        self.mark = 0

    def take(self, elems):
        o = self.off
        self.off += elems
        assert self.off <= self.n, f"{self.t} arena overflow {self.off}>{self.n}"
        return o

    def set_mark(self):
        self.mark = self.off

    def reset(self):
        self.off = self.mark

    def ap(self, off, elems, parts=128):
        return self.t[0:parts, off:off + elems]


def _ap3(ap, dims, offset_elems=0):
    part_dim = list(ap.ap[0])
    new_dims = [part_dim] + [[s, c] for (s, c) in dims]
    return dataclasses.replace(ap, offset=ap.offset + offset_elems, ap=new_dims)


def _dram_ap(t, dims, offset_elems=0):
    full = t.ap()
    return dataclasses.replace(full, offset=full.offset + offset_elems,
                               ap=[[s, c] for (s, c) in dims])


def build_nc():
    nc = bass.Bass("TRN2", target_bir_lowering=False)

    x_in = nc.dram_tensor("x", [S, E], F32, kind="ExternalInput")
    wq_in = nc.dram_tensor("wq", [E, NH * D], F32, kind="ExternalInput")
    wk_in = nc.dram_tensor("wk", [E, NKV * D], F32, kind="ExternalInput")
    wv_in = nc.dram_tensor("wv", [E, NKV * D], F32, kind="ExternalInput")
    wo_in = nc.dram_tensor("wo", [NH * D, E], F32, kind="ExternalInput")
    qs_in = nc.dram_tensor("qscale", [D], F32, kind="ExternalInput")
    ks_in = nc.dram_tensor("kscale", [D], F32, kind="ExternalInput")
    w1_in = nc.dram_tensor("w1", [HF, E], F32, kind="ExternalInput")
    w2_in = nc.dram_tensor("w2", [E, HF], F32, kind="ExternalInput")
    w3_in = nc.dram_tensor("w3", [HF, E], F32, kind="ExternalInput")
    cin = {}
    for name, shape, dt in [
        ("c_ident", [128, 128], BF16), ("c_p128", [128, 128], BF16),
        ("c_ones", [128, 128], BF16), ("c_h64", [64, 64], BF16),
        ("c_h128", [128, 128], BF16), ("c_maskhi", [128, 128], BF16),
        ("c_masklo", [128, 128], BF16), ("c_cosT", [128, S], F32),
        ("c_sinT", [128, S], F32),
    ]:
        cin[name] = nc.dram_tensor(name, shape, dt, kind="ExternalInput")
    out_t = nc.dram_tensor("out", [S, E], F32, kind="ExternalOutput")

    skind = "ExternalOutput" if _DEBUG else "Internal"
    w1t_s = nc.dram_tensor("w1t_s", [16, 128, HF], BF16, kind="Internal")
    w3t_s = nc.dram_tensor("w3t_s", [16, 128, HF], BF16, kind="Internal")
    b_s = nc.dram_tensor("b_s", [64, 128, E], BF16, kind=skind)
    w2f_s = nc.dram_tensor("w2f_s", [64, 128, E], BF16, kind=skind)
    qt_s = nc.dram_tensor("qt_s", [NCH, NH, 128, CH], BF16, kind=skind)
    kt_s = nc.dram_tensor("kt_s", [NCH, NKV, 128, CH], BF16, kind=skind)
    v_s = nc.dram_tensor("v_s", [NCH, 4, 128, 516], BF16, kind=skind)
    aot_s = nc.dram_tensor("aot_s", [NCH, 16, 128, CH], BF16, kind=skind)
    x1_s = nc.dram_tensor("x1_s", [NT, 128, E], BF16, kind=skind)
    x1t_s = nc.dram_tensor("x1t_s", [16, 128, S], BF16, kind=skind)
    dbg_r = (nc.dram_tensor("dbg_r", [NT, 128, E], F32, kind="ExternalOutput")
             if _DEBUG else None)
    dbg_st = (nc.dram_tensor("dbg_st", [NT, 128, 28], F32, kind="ExternalOutput")
              if _DEBUG else None)

    sch = Sched()
    DB = {n: DramBuf(n) for n in ("w1t", "w3t", "b", "w2f", "qt", "kt", "v",
                                  "aot", "x1", "x1t")}

    with ExitStack() as ctx:
        ab = Arena(nc, "AB", BF16, 75700)
        af = Arena(nc, "AF", F32, 14700)

        co = {}
        CB = {}
        for name, elems, parts in [
            ("c_ident", 128, 128), ("c_p128", 128, 128), ("c_ones", 128, 128),
            ("c_h64", 64, 64), ("c_h128", 128, 128), ("c_maskhi", 128, 128),
            ("c_masklo", 128, 128),
        ]:
            co[name] = ab.take(elems)
            CB[name] = sch.buf(name)
            dst = ab.ap(co[name], elems, parts)
            src = cin[name].ap()
            sch.op(POOL, (lambda d, s: (lambda g: g.dma_start(out=d, in_=s)))(dst, src),
                   writes=[CB[name]], dma=True, drain=(parts != 128),
                   tag=f"ld_{name}")
        ident = lambda: ab.ap(co["c_ident"], 128)
        p128 = lambda: ab.ap(co["c_p128"], 128)
        ones = lambda: ab.ap(co["c_ones"], 128)
        h64 = lambda: ab.ap(co["c_h64"], 64, 64)
        h128 = lambda: ab.ap(co["c_h128"], 128)
        maskhi = lambda: ab.ap(co["c_maskhi"], 128)
        masklo = lambda: ab.ap(co["c_masklo"], 128)

        kc_o = af.take(2)
        kc_b = sch.buf("kc")
        qs_dst = af.ap(kc_o, 1)
        ks_dst = af.ap(kc_o + 1, 1)
        sch.op(POOL, lambda g: g.dma_start(
            out=qs_dst, in_=_dram_ap(qs_in, [(1, 128), (1, 1)]),
            allow_slow_non_contiguous=True), writes=[kc_b], dma=True, drain=True, tag="ld_qs")
        sch.op(POOL, lambda g: g.dma_start(
            out=ks_dst, in_=_dram_ap(ks_in, [(1, 128), (1, 1)]),
            allow_slow_non_contiguous=True), writes=[kc_b], dma=True, drain=True, tag="ld_ks")
        sch.op(DVE, lambda v: v.tensor_mul(qs_dst, qs_dst, ks_dst),
               reads=[kc_b], writes=[kc_b], tag="kc_mul")
        sch.op(DVE, lambda v: v.tensor_scalar_mul(
            qs_dst, qs_dst, float(1.0 / (CAP * np.sqrt(D)))),
            reads=[kc_b], writes=[kc_b], tag="kc_sc")
        kc_ap = lambda: af.ap(kc_o, 1)

        epsln_o = af.take(1)
        epsln_b = sch.buf("epsln")
        sch.op(DVE, lambda v: v.memset(af.ap(epsln_o, 1), EPS_LN),
               writes=[epsln_b], tag="eps_set")
        epsln_ap = lambda: af.ap(epsln_o, 1)
        epsnr_o = af.take(1)
        epsnr_b = sch.buf("epsnr")
        sch.op(DVE, lambda v: v.memset(af.ap(epsnr_o, 1), EPS_NRM),
               writes=[epsnr_b], tag="epsn_set")
        epsnr_ap = lambda: af.ap(epsnr_o, 1)

        ab.set_mark()
        af.set_mark()

        sch.phase = "P0wprep"
        with ExitStack() as pctx:
            ps_f = pctx.enter_context(nc.psum_tensor("p0_f", [64, 1024], F32))
            ps_g = pctx.enter_context(nc.psum_tensor("p0_g", [128, 1024], F32))
            psf_b = [sch.buf("p0f0"), sch.buf("p0f1")]
            psg_b = [sch.buf("p0g0"), sch.buf("p0g1")]

            t2f_o = [af.take(4096)]
            t2f_b = [sch.buf("t2f0")]
            t2b_o = [ab.take(4096), ab.take(4096)]
            bb_o = [ab.take(4096), ab.take(4096)]
            t2b_b = [sch.buf("t2b0"), sch.buf("t2b1")]
            bb_b = [sch.buf("bb0"), sch.buf("bb1")]

            def f1_load(g):
                tf2 = af.ap(t2f_o[0], 4096, 64)
                s2 = _dram_ap(w2_in, [(128, 64), (8192, 32), (1, 128)],
                              g * 32 * 8192)
                sch.op(POOL, lambda g_, s=s2, d=tf2: g_.dma_start(out=d, in_=s),
                       writes=[t2f_b[0]], dma=True, tag=f"f1_ld_{g}")

            f1_load(0)
            for g in range(64):
                par = g % 2
                tf = af.ap(t2f_o[0], 4096, 64)
                tb = ab.ap(t2b_o[par], 4096, 64)
                sch.op(ACT, lambda a, s=tf, d=tb: a.copy(d, s),
                       reads=[t2f_b[0]], writes=[t2b_b[par]], tag="f1_cvt")
                if g + 1 < 64:
                    f1_load(g + 1)
                bbv = ab.ap(bb_o[g % 2], 4096, 64)
                for m in range(8):
                    mp = m % 2
                    rhs = _ap3(tb, [(1, 16), (128, 32)], m * 16)
                    psl = ps_f[:, mp * 512:(mp + 1) * 512]
                    sch.op(PE, lambda t, r=rhs, p=psl: t.matmul(
                        p, h64(), r, start=True, stop=True),
                        reads=[t2b_b[par], CB["c_h64"]], writes=[psf_b[mp]],
                        tag="f1_mm")
                    dstb = _ap3(bbv, [(1, 512)], m * 512)
                    sch.op(DVE, lambda v, p=psl, d=dstb: v.tensor_copy(d, p),
                           reads=[psf_b[mp]], writes=[bb_b[g % 2]], tag="f1_ev")
                dst = _dram_ap(b_s, [(128 * E, 64), (E, 128), (1, 32)], g * 32)
                bsrc = _ap3(bbv, [(32, 128), (1, 32)])
                sch.op(SP, lambda q, s=bsrc, d=dst: q.dma_start(out=d, in_=s),
                       reads=[bb_b[g % 2]], dwrites=[DB["b"]], dma=True,
                       tag=f"f1_st_{g}")

            bslab_o = [ab.take(2048) for _ in range(4)]
            w2fe_o = [ab.take(2048), ab.take(2048)]
            bslab_b = [sch.buf(f"bs{i}") for i in range(4)]
            w2fe_b = [sch.buf("we0"), sch.buf("we1")]

            def f2_load(ip):
                bsl2 = ab.ap(bslab_o[ip % 4], 2048)
                s2 = _dram_ap(b_s, [(E, 128), (1, E)], ip * 128 * E)
                sch.op(POOL, lambda g_, s=s2, d=bsl2: g_.dma_start(out=d, in_=s),
                       writes=[bslab_b[ip % 4]], dreads=[DB["b"]], dma=True,
                       tag=f"f2_ld_{ip}")

            for j in range(3):
                f2_load(j)
            for ip in range(64):
                par = ip % 4
                if ip + 3 < 64:
                    f2_load(ip + 3)
                bsl = ab.ap(bslab_o[par], 2048)
                wfe = ab.ap(w2fe_o[ip % 2], 2048)
                for dc in range(4):
                    dp = dc % 2
                    psl = ps_g[:, dp * 512:(dp + 1) * 512]
                    sch.op(PE, lambda t, p=psl, r=bsl[:, dc * 512:(dc + 1) * 512]:
                           t.matmul(p, h128(), r, start=True, stop=True),
                           reads=[bslab_b[par], CB["c_h128"]], writes=[psg_b[dp]],
                           tag="f2_mm")
                    sch.op(DVE, lambda v, p=psl, d=wfe[:, dc * 512:(dc + 1) * 512]:
                           v.tensor_copy(d, p),
                           reads=[psg_b[dp]], writes=[w2fe_b[ip % 2]], tag="f2_ev")
                dst = _dram_ap(w2f_s, [(E, 128), (1, E)], ip * 128 * E)
                sch.op(SP, lambda q, s=wfe, d=dst: q.dma_start(out=d, in_=s),
                       reads=[w2fe_b[ip % 2]], dwrites=[DB["w2f"]], dma=True,
                       tag=f"f2_st_{ip}")

        sch.set_fence()
        ab.reset()
        af.reset()

        sch.phase = "P1qkv"
        with ExitStack() as pctx:
            ps_tr = pctx.enter_context(nc.psum_tensor("p1_tr", [128, 2048], BF16))
            ps_a = pctx.enter_context(nc.psum_tensor("p1_a", [128, 1024], F32))
            ps_r = pctx.enter_context(nc.psum_tensor("p1_r", [128, 1024], F32))
            ps_s = pctx.enter_context(nc.psum_tensor("p1_s", [128, 1024], F32))
            pstr_t = [ps_tr, ps_tr]
            ps_trb0 = sch.buf("p1tr0")
            ps_trb = [ps_trb0, ps_trb0]
            psa_b = [sch.buf("p1a0"), sch.buf("p1a1")]
            psr_b = [sch.buf("p1r0"), sch.buf("p1r1")]
            pss_b = [sch.buf("p1s0"), sch.buf("p1s1")]

            wq_o = ab.take(16 * 2048)
            wk_o = ab.take(16 * 512)
            wv_o = ab.take(16 * 512)
            wq_b = sch.buf("wq"); wk_b = sch.buf("wk"); wv_b = sch.buf("wv")
            xf_o = [af.take(2048), af.take(2048)]
            xf_b = [sch.buf("xf0"), sch.buf("xf1")]
            wstg_o = xf_o
            wstg_b = xf_b
            idx = 0
            for (wt, off, width, buf) in [(wq_in, wq_o, 2048, wq_b),
                                          (wk_in, wk_o, 512, wk_b),
                                          (wv_in, wv_o, 512, wv_b)]:
                for dm in range(16):
                    par = idx % 2
                    idx += 1
                    stg = af.ap(wstg_o[par], width)
                    src = _dram_ap(wt, [(width, 128), (1, width)], dm * 128 * width)
                    sch.op(POOL, lambda g_, s=src, d=stg: g_.dma_start(out=d, in_=s),
                           writes=[wstg_b[par]], dma=True, tag="wqkv_ld")
                    dstb = ab.ap(off + dm * width, width)
                    sch.op(ACT, lambda a, s=stg, d=dstb: a.copy(d, s),
                           reads=[wstg_b[par]], writes=[buf], tag="wqkv_cvt")

            xb_o = [ab.take(2048), ab.take(2048)]
            xb_b = [sch.buf("xb0"), sch.buf("xb1")]
            xt_o = ab.take(16 * 512)
            xt_b = sch.buf("xt")
            cs_o = [af.take(512), af.take(512)]
            sn_o = [af.take(512), af.take(512)]
            cs_b = [sch.buf("cs0"), sch.buf("cs1")]
            sn_b = [sch.buf("sn0"), sch.buf("sn1")]
            qbf_o = [ab.take(512) for _ in range(2)]
            qbf_b = [sch.buf(f"qb{i}") for i in range(2)]
            t1_o = [af.take(512) for _ in range(2)]
            t2_o = [af.take(512) for _ in range(2)]
            qp_o = [af.take(512) for _ in range(2)]
            t1_b = [sch.buf(f"t1{i}") for i in range(2)]
            t2_b = [sch.buf(f"t2{i}") for i in range(2)]
            qp_b = [sch.buf(f"qp{i}") for i in range(2)]
            sqb_o = [ab.take(512) for _ in range(2)]
            sqb_b = [sch.buf(f"sq{i}") for i in range(2)]
            nrm_o = [af.take(512) for _ in range(2)]
            nrm_b = [sch.buf(f"nr{i}") for i in range(2)]
            rcp_o = t2_o
            rcp_b = t2_b
            kf_o = t1_o
            kf_b = t1_b
            qout_o = [ab.take(512), ab.take(512)]
            qout_b = [sch.buf("qo0"), sch.buf("qo1")]
            vt_o = ab.take(4 * 516); vt_b = sch.buf("vt")
            slabf_o = [af.take(2048), af.take(2048)]
            slabf_b = [sch.buf(f"sf{i}") for i in range(2)]
            slabb_o = [ab.take(2048), ab.take(2048)]
            slabb_b = [sch.buf("sb0"), sch.buf("sb1")]
            evacb_o = [ab.take(2048), ab.take(2048)]
            evacb_b = [sch.buf("we0"), sch.buf("we1")]

            def slab_load(gidx):
                wd = w1_in if gidx < 64 else w3_in
                hcs = gidx % 64
                sf = af.ap(slabf_o[gidx % 2], 2048)
                s2 = _dram_ap(wd, [(2048, 128), (1, 2048)], hcs * 128 * 2048)
                sch.op(POOL, lambda g, s=s2, d=sf: g.dma_start(out=d, in_=s),
                       writes=[slabf_b[gidx % 2]], dma=True, tag=f"w_ld_{gidx}")

            slab_ctr = [0]

            def emit_slab_step():
                gidx = slab_ctr[0]
                if gidx >= 128:
                    return
                slab_ctr[0] += 1
                if gidx == 0:
                    slab_load(0)
                if gidx + 1 < 128:
                    slab_load(gidx + 1)
                par = gidx % 2
                bpar = gidx % 2
                sf = af.ap(slabf_o[par], 2048)
                sb = ab.ap(slabb_o[bpar], 2048)
                sch.op(ACT, lambda a, s=sf, d=sb: a.copy(d, s),
                       reads=[slabf_b[par]], writes=[slabb_b[bpar]], tag="w_cvt")
                ev = ab.ap(evacb_o[bpar], 2048)
                pst = pstr_t[0]
                for k in range(16):
                    sch.op(PE, (lambda kk, sbv=sb, p=pst: (lambda t: t.transpose(
                        p[:, kk * 128:(kk + 1) * 128],
                        sbv[:, kk * 128:(kk + 1) * 128], ident())))(k),
                        reads=[slabb_b[bpar], CB["c_ident"]], writes=[ps_trb[0]],
                        tag="w_tr")
                sch.op(DVE, lambda v, p=pst, d=ev: v.tensor_copy(d, p[:]),
                       reads=[ps_trb[0]], writes=[evacb_b[bpar]], tag="w_ev")
                wt_dram = w1t_s if gidx < 64 else w3t_s
                hcs = gidx % 64
                dst = _dram_ap(wt_dram, [(HF, 128), (128 * HF, 16), (1, 128)],
                               hcs * 128)
                esrc = _ap3(ev, [(128, 16), (1, 128)])
                sch.op(SP, lambda q, s=esrc, d=dst: q.dma_start(out=d, in_=s),
                       reads=[evacb_b[bpar]],
                       dwrites=[DB["w1t"] if gidx < 64 else DB["w3t"]],
                       dma=True, tag="w_st")

            def rope_norm(par, ci, psl, psl_buf, out_off, out_buf, is_k, c):
                qb = ab.ap(qbf_o[ci], 512)
                psr = ps_r[:, par * 512:(par + 1) * 512]
                pss = ps_s[:, par * 512:(par + 1) * 512]
                sch.op(DVE, lambda v, p=psl, d=qb: v.tensor_copy(d, p),
                       reads=[psl_buf], writes=[qbf_b[ci]], tag="r_ev")
                sch.op(PE, lambda t, s=qb, p=psr: t.matmul(p, p128(), s,
                                                           start=True, stop=True),
                       reads=[qbf_b[ci], CB["c_p128"]], writes=[psr_b[par]],
                       tag="r_rot")
                csl = af.ap(cs_o[c % 2], 512)
                snl = af.ap(sn_o[c % 2], 512)
                t1 = af.ap(t1_o[ci], 512)
                t2 = af.ap(t2_o[ci], 512)
                qp = af.ap(qp_o[ci], 512)
                sch.op(DVE, lambda v, q=qb: v.tensor_mul(t1, q, csl),
                       reads=[qbf_b[ci], cs_b[c % 2]], writes=[t1_b[ci]],
                       tag="r_t1")
                sch.op(DVE, lambda v, p=psr: v.tensor_mul(t2, p, snl),
                       reads=[psr_b[par], sn_b[c % 2]], writes=[t2_b[ci]],
                       tag="r_t2")
                sch.op(DVE, lambda v: v.tensor_add(qp, t1, t2),
                       reads=[t1_b[ci], t2_b[ci]], writes=[qp_b[ci]], tag="r_qp")
                sq = ab.ap(sqb_o[ci], 512)
                sch.op(DVE, lambda v: v.tensor_mul(sq, qp, qp),
                       reads=[qp_b[ci]], writes=[sqb_b[ci]], tag="r_sq")
                sch.op(PE, lambda t, p=pss: t.matmul(p, ones(), sq,
                                                     start=True, stop=True),
                       reads=[sqb_b[ci], CB["c_ones"]], writes=[pss_b[par]],
                       tag="r_ss")
                nr = af.ap(nrm_o[ci], 512)
                sch.op(ACT, lambda a, p=pss: a.activation(nr, p, AF.Sqrt),
                       reads=[pss_b[par]], writes=[nrm_b[ci]], tag="r_sqrt")
                sch.op(DVE, lambda v: v.tensor_scalar_add(nr, nr, EPS_NRM),
                       reads=[nrm_b[ci]], writes=[nrm_b[ci]], tag="r_eps")
                rc = af.ap(rcp_o[ci], 512)
                sch.op(DVE, lambda v: v.reciprocal(rc, nr),
                       reads=[nrm_b[ci]], writes=[rcp_b[ci]], tag="r_rcp")
                outap = ab.ap(out_off, 512)
                if is_k:
                    kf = af.ap(kf_o[ci], 512)
                    sch.op(DVE, lambda v: v.tensor_mul(kf, qp, rc),
                           reads=[qp_b[ci], rcp_b[ci]], writes=[kf_b[ci]],
                           tag="r_km")
                    sch.op(DVE, lambda v: v.tensor_scalar_mul(outap, kf, kc_ap()),
                           reads=[kf_b[ci], kc_b], writes=[out_buf], tag="r_ks")
                else:
                    sch.op(DVE, lambda v: v.tensor_mul(outap, qp, rc),
                           reads=[qp_b[ci], rcp_b[ci]], writes=[out_buf],
                           tag="r_qm")

            for c in range(NCH):
                cp = c % 2
                csl = af.ap(cs_o[cp], 512)
                snl = af.ap(sn_o[cp], 512)
                sch.op(POOL, lambda g_, d=csl, s=cin["c_cosT"].ap()[:, c * CH:(c + 1) * CH]:
                       g_.dma_start(out=d, in_=s), writes=[cs_b[cp]], dma=True,
                       tag="cs_ld")
                sch.op(POOL, lambda g_, d=snl, s=cin["c_sinT"].ap()[:, c * CH:(c + 1) * CH]:
                       g_.dma_start(out=d, in_=s), writes=[sn_b[cp]], dma=True,
                       tag="sn_ld")
                for t4 in range(4):
                    par = t4 % 2
                    xf = af.ap(xf_o[par], 2048)
                    xb = ab.ap(xb_o[par], 2048)
                    row0 = (c * 4 + t4) * 128
                    src = _dram_ap(x_in, [(E, 128), (1, E)], row0 * E)
                    sch.op(POOL, lambda g_, s=src, d=xf: g_.dma_start(out=d, in_=s),
                           writes=[xf_b[par]], dma=True, tag=f"x_ld_{c}_{t4}")
                    sch.op(ACT, lambda a, s=xf, d=xb: a.copy(d, s),
                           reads=[xf_b[par]], writes=[xb_b[par]], tag="x_cvt")
                    pst = pstr_t[par]
                    for k in range(16):
                        sch.op(PE, (lambda kk, xbv=xb, p=pst: (lambda t: t.transpose(
                            p[:, kk * 128:(kk + 1) * 128],
                            xbv[:, kk * 128:(kk + 1) * 128], ident())))(k),
                            reads=[xb_b[par], CB["c_ident"]], writes=[ps_trb[par]],
                            tag="x_tr")
                    xt_dst = _ap3(ab.ap(xt_o, 1), [(512, 16), (1, 128)], t4 * 128)
                    sch.op(DVE, lambda v, p=pst, d=xt_dst: v.tensor_copy(d, p[:]),
                           reads=[ps_trb[par]], writes=[xt_b], tag="x_ev")
                for h in range(NH):
                    par = h % 2
                    psl = ps_a[:, par * 512:(par + 1) * 512]
                    for dc in range(16):
                        lhs = ab.ap(wq_o + dc * 2048 + h * 128, 128)
                        rhs = ab.ap(xt_o + dc * 512, 512)
                        sch.op(PE, lambda t, l=lhs, r=rhs, p=psl, st=(dc == 0),
                               sp=(dc == 15): t.matmul(p, l, r, start=st, stop=sp),
                               reads=[wq_b, xt_b], writes=[psa_b[par]], tag="q_mm")
                    rope_norm(par, h % 2, psl, psa_b[par], qout_o[h % 2],
                              qout_b[h % 2], False, c)
                    dst = _dram_ap(qt_s, [(CH, 128), (1, CH)],
                                   (c * NH + h) * 128 * CH)
                    qsrc = ab.ap(qout_o[h % 2], 512)
                    sch.op(SP, lambda q, s=qsrc, d=dst: q.dma_start(out=d, in_=s),
                           reads=[qout_b[h % 2]], dwrites=[DB["qt"]], dma=True,
                           tag="qt_st")
                    emit_slab_step()
                for kv in range(NKV):
                    par = kv % 2
                    psl = ps_a[:, par * 512:(par + 1) * 512]
                    for dc in range(16):
                        lhs = ab.ap(wk_o + dc * 512 + kv * 128, 128)
                        rhs = ab.ap(xt_o + dc * 512, 512)
                        sch.op(PE, lambda t, l=lhs, r=rhs, p=psl, st=(dc == 0),
                               sp=(dc == 15): t.matmul(p, l, r, start=st, stop=sp),
                               reads=[wk_b, xt_b], writes=[psa_b[par]], tag="k_mm")
                    rope_norm(par, kv % 2, psl, psa_b[par], qout_o[kv % 2],
                              qout_b[kv % 2], True, c)
                    dst = _dram_ap(kt_s, [(CH, 128), (1, CH)],
                                   (c * NKV + kv) * 128 * CH)
                    ksrc = ab.ap(qout_o[kv % 2], 512)
                    sch.op(SP, lambda q, s=ksrc, d=dst: q.dma_start(out=d, in_=s),
                           reads=[qout_b[kv % 2]], dwrites=[DB["kt"]], dma=True,
                           tag="kt_st")
                ones_dst = _ap3(ab.ap(vt_o, 1), [(516, 4), (129, 4), (1, 1)], 128)
                sch.op(DVE, lambda v, d=ones_dst: v.memset(d, 1.0),
                       writes=[vt_b], tag="v_ones")
                for t4 in range(4):
                    par = t4 % 2
                    psl = ps_a[:, par * 512:(par + 1) * 512]
                    for dc in range(16):
                        lhs = ab.ap(xt_o + dc * 512 + t4 * 128, 128)
                        rhs = ab.ap(wv_o + dc * 512, 512)
                        sch.op(PE, lambda t, l=lhs, r=rhs, p=psl, st=(dc == 0),
                               sp=(dc == 15): t.matmul(p, l, r, start=st, stop=sp),
                               reads=[wv_b, xt_b], writes=[psa_b[par]], tag="v_mm")
                    vdst = _ap3(ab.ap(vt_o, 1), [(129, 4), (1, 128)], t4 * 516)
                    sch.op(DVE, lambda v, p=psl, d=vdst: v.tensor_copy(d, p),
                           reads=[psa_b[par]], writes=[vt_b], tag="v_ev")
                    vst = _dram_ap(v_s, [(516, 128), (1, 516)],
                                   (c * 4 + t4) * 128 * 516)
                    vsr = _ap3(ab.ap(vt_o, 1), [(1, 516)], t4 * 516)
                    sch.op(SP, lambda q, s=vsr, d=vst: q.dma_start(out=d, in_=s),
                           reads=[vt_b], dwrites=[DB["v"]], dma=True, tag="v_st")

        sch.set_fence()
        ab.reset()
        af.reset()

        sch.phase = "P2scores"
        with ExitStack() as pctx:
            ps_sc = pctx.enter_context(nc.psum_tensor("p2_sc", [128, 1536], F32))
            ps_av = pctx.enter_context(nc.psum_tensor("p2_av", [128, 2048], F32))
            ps_at = pctx.enter_context(nc.psum_tensor("p2_at", [128, 512], BF16))
            pssc_b = sch.buf("ps_sc")
            psav_b = sch.buf("ps_av")
            psat_b = sch.buf("ps_at")

            qin_o = [ab.take(16 * 512), ab.take(16 * 512)]
            qin_b = [sch.buf("qin0"), sch.buf("qin1")]
            kr_o = ab.take(6 * 4 * 512)
            kr_b = [sch.buf(f"kr{s_}") for s_ in range(6)]
            vr_o = ab.take(6 * 4 * 516)
            vr_b = [sch.buf(f"vr{s_}") for s_ in range(6)]
            tb_o = af.take(1536)
            tb_b = sch.buf("tb")
            eb_o = [ab.take(1536), ab.take(1536)]
            eb_b = [sch.buf("eb0"), sch.buf("eb1")]
            adiv_o = ab.take(4 * 128)
            adiv_b = sch.buf("adiv")
            rcd_o = af.take(4)
            rcd_b = sch.buf("rcd")
            aotb_o = [ab.take(512), ab.take(512)]
            aotb_b = [sch.buf("ao0"), sch.buf("ao1")]

            def load_kv_slot(cc):
                slot = cc % 6
                kd = _ap3(ab.ap(kr_o + slot * 4 * 512, 1), [(512, 4), (1, 512)])
                ks = _dram_ap(kt_s, [(CH, 128), (128 * CH, 4), (1, CH)],
                              cc * NKV * 128 * CH)
                sch.op(POOL, lambda g_, d=kd, s=ks: g_.dma_start(out=d, in_=s),
                       writes=[kr_b[slot]], dreads=[DB["kt"]], dma=True,
                       tag=f"kr_{cc}")
                vd = _ap3(ab.ap(vr_o + slot * 4 * 516, 1), [(516, 4), (1, 516)])
                vs2 = _dram_ap(v_s, [(516, 128), (128 * 516, 4), (1, 516)],
                               cc * 4 * 128 * 516)
                sch.op(POOL, lambda g_, d=vd, s=vs2: g_.dma_start(out=d, in_=s),
                       writes=[vr_b[slot]], dreads=[DB["v"]], dma=True,
                       tag=f"vr_{cc}")

            def load_q_chunk(cc):
                par = cc % 2
                qd = _ap3(ab.ap(qin_o[par], 1), [(512, 16), (1, 512)])
                qs2 = _dram_ap(qt_s, [(CH, 128), (128 * CH, 16), (1, CH)],
                               cc * NH * 128 * CH)
                sch.op(POOL, lambda g_, d=qd, s=qs2: g_.dma_start(out=d, in_=s),
                       writes=[qin_b[par]], dreads=[DB["qt"]], dma=True,
                       tag=f"q_{cc}")

            for cc in range(3):
                load_kv_slot(cc)
            load_q_chunk(0)
            pending_tail = [None]

            for c in range(NCH):
                if c + 1 < NCH:
                    load_q_chunk(c + 1)
                if c + 3 < NCH:
                    load_kv_slot(c + 3)
                kb_lo = max(0, 4 * c - 8)
                kb_hi = min(NT - 1, 4 * c + 11)
                kbs = list(range(kb_lo, kb_hi + 1))
                for h in range(NH):
                    kv = h // 4
                    qrhs = ab.ap(qin_o[c % 2] + h * 512, 512)
                    started = [False] * 4
                    waves = [kbs[w0:w0 + 3] for w0 in range(0, len(kbs), 3)]

                    def emit_scores(wave):
                        for wi, kb in enumerate(wave):
                            slot = (kb // 4) % 6
                            lhs = ab.ap(kr_o + slot * 4 * 512 + kv * 512 +
                                        (kb % 4) * 128, 128)
                            psl = ps_sc[:, wi * 512:(wi + 1) * 512]
                            sch.op(PE, lambda t, l=lhs, r=qrhs, p=psl:
                                   t.matmul(p, l, r, start=True, stop=True),
                                   reads=[kr_b[slot], qin_b[c % 2]],
                                   writes=[pssc_b], tag="sc_mm")

                    def emit_acts(wave, ebp):
                        nW = len(wave) * 512
                        tb = af.ap(tb_o, nW)
                        sch.op(ACT, lambda a, d=tb, n=nW: a.activation(
                            d, ps_sc[:, 0:n], AF.Tanh),
                            reads=[pssc_b], writes=[tb_b], tag="sc_tanh")
                        eb = ab.ap(eb_o[ebp], nW)
                        sch.op(ACT, lambda a, s=tb, d=eb: a.activation(
                            d, s, AF.Exp, scale=CAP),
                            reads=[tb_b], writes=[eb_b[ebp]], tag="sc_exp")

                    def emit_av(wave, ebp):
                        for wi, kb in enumerate(wave):
                            slot = (kb // 4) % 6
                            for qs in range(4):
                                qi = 4 * c + qs
                                dblk = kb - qi
                                if abs(dblk) > 8:
                                    continue
                                esl = ab.ap(eb_o[ebp] + wi * 512 + qs * 128, 128)
                                if dblk == 8:
                                    sch.op(DVE, lambda v, e=esl: v.tensor_mul(
                                        e, e, maskhi()),
                                        reads=[eb_b[ebp], CB["c_maskhi"]],
                                        writes=[eb_b[ebp]], tag="m_hi")
                                elif dblk == -8:
                                    sch.op(DVE, lambda v, e=esl: v.tensor_mul(
                                        e, e, masklo()),
                                        reads=[eb_b[ebp], CB["c_masklo"]],
                                        writes=[eb_b[ebp]], tag="m_lo")
                                vrhs = ab.ap(vr_o + slot * 4 * 516 +
                                             (kb % 4) * 516 + kv * 129, 129)
                                psl = ps_av[:, qs * 512:qs * 512 + 129]
                                st = not started[qs]
                                started[qs] = True
                                sp = (kb == min(NT - 1, qi + 8))
                                sch.op(PE, lambda t, l=esl, r=vrhs, p=psl, s=st,
                                       s2=sp: t.matmul(p, l, r, start=s, stop=s2),
                                       reads=[eb_b[ebp], vr_b[slot]],
                                       writes=[psav_b], tag="av_mm")

                    if pending_tail[0] is not None:
                        pass
                    emit_scores(waves[0])
                    if pending_tail[0] is not None:
                        pending_tail[0]()
                        pending_tail[0] = None
                    for w in range(len(waves)):
                        emit_acts(waves[w], w % 2)
                        if w + 1 < len(waves):
                            emit_scores(waves[w + 1])
                        emit_av(waves[w], w % 2)

                    def unit_tail(c=c, h=h):
                        for qs in range(4):
                            rc = af.ap(rcd_o + qs, 1)
                            den = ps_av[:, qs * 512 + 128:qs * 512 + 129]
                            sch.op(DVE, lambda v, d=rc, s=den: v.reciprocal(d, s),
                                   reads=[psav_b], writes=[rcd_b], tag="d_rcp")
                            num = ps_av[:, qs * 512:qs * 512 + 128]
                            ad = ab.ap(adiv_o + qs * 128, 128)
                            sch.op(DVE, lambda v, d=ad, s=num, r=rc:
                                   v.tensor_scalar_mul(d, s, r),
                                   reads=[psav_b, rcd_b], writes=[adiv_b],
                                   tag="d_mul")
                        for qs in range(4):
                            ad = ab.ap(adiv_o + qs * 128, 128)
                            sch.op(PE, lambda t, s=ad,
                                   p=ps_at[:, qs * 128:(qs + 1) * 128]:
                                   t.transpose(p, s, ident()),
                                   reads=[adiv_b, CB["c_ident"]], writes=[psat_b],
                                   tag="ao_tr")
                        hp = h % 2
                        ao = ab.ap(aotb_o[hp], 512)
                        sch.op(DVE, lambda v, d=ao: v.tensor_copy(d, ps_at[:]),
                               reads=[psat_b], writes=[aotb_b[hp]], tag="ao_ev")
                        adst = _dram_ap(aot_s, [(CH, 128), (1, CH)],
                                        (c * 16 + h) * 128 * CH)
                        sch.op(SP, lambda q, s=ao, d=adst: q.dma_start(out=d,
                                                                       in_=s),
                               reads=[aotb_b[hp]], dwrites=[DB["aot"]], dma=True,
                               tag="ao_st")

                    pending_tail[0] = unit_tail

            if pending_tail[0] is not None:
                pending_tail[0]()
                pending_tail[0] = None

        sch.set_fence()
        ab.reset()
        af.reset()

        sch.phase = "P3wo"
        with ExitStack() as pctx:
            ps_tr = pctx.enter_context(nc.psum_tensor("p3_tr", [128, 2048], BF16))
            ps_tr2 = pctx.enter_context(nc.psum_tensor("p3_tr2", [128, 2048], BF16))
            ps_w = pctx.enter_context(nc.psum_tensor("p3_w", [128, 1024], F32))
            pstr_t = [ps_tr, ps_tr2]
            ps_trb = [sch.buf("p3tr0"), sch.buf("p3tr1")]
            psw_b = [sch.buf("p3w0"), sch.buf("p3w1")]

            wo_o = ab.take(16 * 2048)
            wo_b = sch.buf("wo")
            wof_o = [af.take(2048)]
            wof_b = [sch.buf("wof0")]
            for hdc in range(16):
                par = 0
                stg = af.ap(wof_o[par], 2048)
                src = _dram_ap(wo_in, [(E, 128), (1, E)], hdc * 128 * E)
                sch.op(POOL, lambda g_, s=src, d=stg: g_.dma_start(out=d, in_=s),
                       writes=[wof_b[par]], dma=True, tag="wo_ld")
                dstb = ab.ap(wo_o + hdc * 2048, 2048)
                sch.op(ACT, lambda a, s=stg, d=dstb: a.copy(d, s),
                       reads=[wof_b[par]], writes=[wo_b], tag="wo_cvt")

            aoin_o = [ab.take(16 * 512), ab.take(16 * 512)]
            aoin_b = [sch.buf("ai0"), sch.buf("ai1")]
            x4f_o = [af.take(4 * 2048)]
            x4f_b = [sch.buf("x40")]
            r_o = af.take(2048)
            r_b = sch.buf("r")
            st_o = af.take(28)
            st_b = sch.buf("stats")
            x1b_o = [ab.take(2048), ab.take(2048)]
            x1b_b = [sch.buf("x1b0"), sch.buf("x1b1")]
            ex1_o = [ab.take(2048), ab.take(2048)]
            ex1_b = [sch.buf("ex0"), sch.buf("ex1")]

            def layer_norm(src_ap, src_buf, out_ap, out_buf, tag):
                bn24 = af.ap(st_o, 24)
                bn2 = af.ap(st_o + 24, 2)
                for i in range(4):
                    bno = af.ap(st_o + i * 6, 6)
                    bni = _ap3(src_ap, [(1, 512)], i * 512)
                    sch.op(DVE, (lambda o2, i2: (lambda v: v.bn_stats(o2, i2)))(
                        bno, bni),
                        reads=[src_buf], writes=[st_b], tag=tag + "_bns")
                sch.op(DVE, lambda v: v.bn_aggr(bn2, bn24),
                       reads=[st_b], writes=[st_b], tag=tag + "_bna")
                mean = af.ap(st_o + 24, 1)
                var = af.ap(st_o + 25, 1)
                sd = af.ap(st_o + 26, 1)
                sch.op(ACT, lambda a: a.activation(sd, var, AF.Sqrt,
                                                   bias=epsln_ap()),
                       reads=[st_b, epsln_b], writes=[st_b], tag=tag + "_sd")
                rs = af.ap(st_o + 27, 1)
                sch.op(DVE, lambda v: v.reciprocal(rs, sd),
                       reads=[st_b], writes=[st_b], tag=tag + "_rs")
                sch.op(DVE, lambda v: v.tensor_scalar(
                    out_ap, src_ap, mean, rs, ALU.subtract, ALU.mult),
                    reads=[src_buf, st_b], writes=[out_buf], tag=tag + "_out")

            def aoin_load(cc):
                aod = _ap3(ab.ap(aoin_o[cc % 2], 1), [(512, 16), (1, 512)])
                aos = _dram_ap(aot_s, [(CH, 128), (128 * CH, 16), (1, CH)],
                               cc * 16 * 128 * CH)
                sch.op(POOL, lambda g_, d=aod, s=aos: g_.dma_start(out=d, in_=s),
                       writes=[aoin_b[cc % 2]], dreads=[DB["aot"]], dma=True,
                       tag=f"ai_{cc}")

            aoin_load(0)
            for c in range(NCH):
                cp = c % 2
                if c + 1 < NCH:
                    aoin_load(c + 1)
                x4d = _ap3(af.ap(x4f_o[0], 1), [(2048, 4), (1, 2048)])
                xsrc = _dram_ap(x_in, [(E, 128), (128 * E, 4), (1, E)],
                                c * 4 * 128 * E)
                sch.op(POOL, lambda g_, d=x4d, s=xsrc: g_.dma_start(out=d, in_=s),
                       writes=[x4f_b[0]], dma=True, tag=f"x4_{c}")
                for ts in range(4):
                    for dmc in range(4):
                        par = dmc % 2
                        psl = ps_w[:, par * 512:(par + 1) * 512]
                        for hdc in range(16):
                            lhs = ab.ap(aoin_o[cp] + hdc * 512 + ts * 128, 128)
                            rhs = ab.ap(wo_o + hdc * 2048 + dmc * 512, 512)
                            sch.op(PE, lambda t, l=lhs, r2=rhs, p=psl,
                                   st=(hdc == 0), sp=(hdc == 15):
                                   t.matmul(p, l, r2, start=st, stop=sp),
                                   reads=[aoin_b[cp], wo_b], writes=[psw_b[par]],
                                   tag="wo_mm")
                        xsl = af.ap(x4f_o[0] + ts * 2048 + dmc * 512, 512)
                        rsl = af.ap(r_o + dmc * 512, 512)
                        sch.op(DVE, lambda v, p=psl, xs=xsl, d=rsl:
                               v.scalar_tensor_tensor(d, p, 1.0, xs, ALU.mult,
                                                      ALU.add),
                               reads=[psw_b[par], x4f_b[0]], writes=[r_b],
                               tag="wo_res")
                    if _DEBUG:
                        rdst = _dram_ap(dbg_r, [(E, 128), (1, E)],
                                        (c * 4 + ts) * 128 * E)
                        sch.op(SP, lambda q, d=rdst, s=af.ap(r_o, 2048):
                               q.dma_start(out=d, in_=s),
                               reads=[r_b], dma=True, tag="dbg_r")
                    x1t = ab.ap(x1b_o[ts % 2], 2048)
                    layer_norm(af.ap(r_o, 2048), r_b, x1t, x1b_b[ts % 2], "ln1")
                    if _DEBUG:
                        sdst = _dram_ap(dbg_st, [(28, 128), (1, 28)],
                                        (c * 4 + ts) * 128 * 28)
                        sch.op(SP, lambda q, d=sdst, s=af.ap(st_o, 28):
                               q.dma_start(out=d, in_=s),
                               reads=[st_b], dma=True, tag="dbg_st")
                    x1dst = _dram_ap(x1_s, [(E, 128), (1, E)],
                                     (c * 4 + ts) * 128 * E)
                    sch.op(SP, lambda q, s=x1t, d=x1dst: q.dma_start(out=d, in_=s),
                           reads=[x1b_b[ts % 2]], dwrites=[DB["x1"]], dma=True,
                           tag="x1_st")
                    pst = pstr_t[ts % 2]
                    for k in range(16):
                        sch.op(PE, (lambda kk, xv=x1t, p=pst: (lambda t:
                               t.transpose(p[:, kk * 128:(kk + 1) * 128],
                                           xv[:, kk * 128:(kk + 1) * 128],
                                           ident())))(k),
                            reads=[x1b_b[ts % 2], CB["c_ident"]],
                            writes=[ps_trb[ts % 2]], tag="x1_tr")
                    ex1 = ab.ap(ex1_o[ts % 2], 2048)
                    sch.op(DVE, lambda v, p=pst, d=ex1: v.tensor_copy(d, p[:]),
                           reads=[ps_trb[ts % 2]], writes=[ex1_b[ts % 2]],
                           tag="x1_ev")
                    xtd = _dram_ap(x1t_s, [(S, 128), (128 * S, 16), (1, 128)],
                                   c * 512 + ts * 128)
                    es = _ap3(ab.ap(ex1_o[ts % 2], 1), [(128, 16), (1, 128)])
                    sch.op(SP, lambda q, s=es, d=xtd: q.dma_start(out=d, in_=s),
                           reads=[ex1_b[ts % 2]], dwrites=[DB["x1t"]], dma=True,
                           tag="x1t_st")

        sch.set_fence()
        ab.reset()
        af.reset()

        sch.phase = "P4ffn"
        with ExitStack() as pctx:
            ps_u1 = pctx.enter_context(nc.psum_tensor("p4_u1", [128, 1024], F32))
            ps_u3 = pctx.enter_context(nc.psum_tensor("p4_u3", [128, 1024], F32))
            ps_d = pctx.enter_context(nc.psum_tensor("p4_d", [128, 2048], F32))
            psu1_b = [sch.buf("u10"), sch.buf("u11")]
            psu3_b = [sch.buf("u30"), sch.buf("u31")]
            psd_b = [sch.buf(f"pd{i}") for i in range(4)]

            x1tin_o = ab.take(16 * 512)
            x1tin_b = sch.buf("x1tin")
            x1tok_o = ab.take(4 * 2048)
            x1tok_b = sch.buf("x1tok")
            w1g_o = [ab.take(4096), ab.take(4096)]
            w3g_o = [ab.take(4096), ab.take(4096)]
            w1g_b = [sch.buf("w1g0"), sch.buf("w1g1")]
            w3g_b = [sch.buf("w3g0"), sch.buf("w3g1")]
            w2g_o = [ab.take(4096), ab.take(4096)]
            w2g_b = [sch.buf("w2g0"), sch.buf("w2g1")]
            gt_o = ab.take(64 * 512)
            gt_b = sch.buf("gt")
            s1_o = [af.take(512), af.take(512)]
            s1_b = [sch.buf("s10"), sch.buf("s11")]
            r4_o = [af.take(2048) for _ in range(4)]
            r4_b = [sch.buf(f"r4_{i}") for i in range(4)]
            outf_o = af.take(2048)
            outf_b = sch.buf("outf")
            st_o = af.take(28)
            st_b = sch.buf("st4")

            for p in range(NCH):
                if p == 0:
                    x1td = _ap3(ab.ap(x1tin_o, 1), [(512, 16), (1, 512)])
                    x1src = _dram_ap(x1t_s, [(S, 128), (128 * S, 16), (1, 512)], 0)
                    sch.op(POOL, lambda g_, d=x1td, s=x1src:
                           g_.dma_start(out=d, in_=s),
                           writes=[x1tin_b], dreads=[DB["x1t"]], dma=True,
                           tag="fx1t_0")
                xtk = _ap3(ab.ap(x1tok_o, 1), [(2048, 4), (1, 2048)])
                xts = _dram_ap(x1_s, [(E, 128), (128 * E, 4), (1, E)],
                               p * 4 * 128 * E)
                sch.op(POOL, lambda g_, d=xtk, s=xts: g_.dma_start(out=d, in_=s),
                       writes=[x1tok_b], dreads=[DB["x1"]], dma=True,
                       tag=f"fx1_{p}")

                def load_wgroup(g, par):
                    for (wsrc, woff, wbuf) in [(w1t_s, w1g_o, w1g_b),
                                               (w3t_s, w3g_o, w3g_b)]:
                        d = _ap3(ab.ap(woff[par], 1), [(256, 16), (1, 256)])
                        s = _dram_ap(wsrc, [(HF, 128), (128 * HF, 16), (1, 256)],
                                     g * 256)
                        sch.op(SP, lambda q, d2=d, s2=s: q.dma_start(out=d2, in_=s2),
                               writes=[wbuf[par]],
                               dreads=[DB["w1t"] if wsrc is w1t_s else DB["w3t"]],
                               dma=True, tag=f"wg_{p}_{g}")

                if p == 0:
                    load_wgroup(0, 0)
                for g in range(32):
                    par = g % 2
                    if g + 1 < 32:
                        load_wgroup(g + 1, (g + 1) % 2)
                    for hh in range(2):
                        hc = g * 2 + hh
                        up = hc % 2
                        psl1 = ps_u1[:, up * 512:(up + 1) * 512]
                        psl3 = ps_u3[:, up * 512:(up + 1) * 512]
                        for dc in range(16):
                            l1 = ab.ap(w1g_o[par] + dc * 256 + hh * 128, 128)
                            l3 = ab.ap(w3g_o[par] + dc * 256 + hh * 128, 128)
                            rr = ab.ap(x1tin_o + dc * 512, 512)
                            sch.op(PE, lambda t, l=l1, r=rr, p2=psl1, st=(dc == 0),
                                   sp=(dc == 15): t.matmul(p2, l, r, start=st,
                                                           stop=sp),
                                   reads=[w1g_b[par], x1tin_b],
                                   writes=[psu1_b[up]], tag="u1_mm")
                            sch.op(PE, lambda t, l=l3, r=rr, p2=psl3, st=(dc == 0),
                                   sp=(dc == 15): t.matmul(p2, l, r, start=st,
                                                           stop=sp),
                                   reads=[w3g_b[par], x1tin_b],
                                   writes=[psu3_b[up]], tag="u3_mm")
                        s1 = af.ap(s1_o[up], 512)
                        sch.op(ACT, lambda a, d=s1, p2=psl1: a.activation(
                            d, p2, AF.Silu),
                            reads=[psu1_b[up]], writes=[s1_b[up]], tag="silu")
                        gsl = ab.ap(gt_o + hc * 512, 512)
                        sch.op(DVE, lambda v, d=gsl, a2=s1, p2=psl3:
                               v.tensor_mul(d, a2, p2),
                               reads=[s1_b[up], psu3_b[up]], writes=[gt_b],
                               tag="g_mul")
                if p + 1 < NCH:
                    load_wgroup(0, 0)
                if p + 1 < NCH:
                    x1td2 = _ap3(ab.ap(x1tin_o, 1), [(512, 16), (1, 512)])
                    x1src2 = _dram_ap(x1t_s, [(S, 128), (128 * S, 16), (1, 512)],
                                      (p + 1) * 512)
                    sch.op(POOL, lambda g_, d=x1td2, s=x1src2:
                           g_.dma_start(out=d, in_=s),
                           writes=[x1tin_b], dreads=[DB["x1t"]], dma=True,
                           tag=f"fx1t_{p + 1}")
                for dmc in range(4):
                    def load_w2g(hb, par):
                        d = _ap3(ab.ap(w2g_o[par], 1), [(512, 8), (1, 512)])
                        s = _dram_ap(w2f_s, [(E, 128), (128 * E, 8), (1, 512)],
                                     hb * 8 * 128 * E + dmc * 512)
                        sch.op(SP, lambda q, d2=d, s2=s: q.dma_start(out=d2,
                                                                     in_=s2),
                               writes=[w2g_b[par]], dreads=[DB["w2f"]], dma=True,
                               tag=f"w2g_{p}_{dmc}_{hb}")

                    load_w2g(0, 0)
                    for hb in range(8):
                        par = hb % 2
                        if hb + 1 < 8:
                            load_w2g(hb + 1, (hb + 1) % 2)
                        for hh in range(8):
                            hc = hb * 8 + hh
                            rhs = ab.ap(w2g_o[par] + hh * 512, 512)
                            for ts in range(4):
                                lhs = ab.ap(gt_o + hc * 512 + ts * 128, 128)
                                psl = ps_d[:, ts * 512:(ts + 1) * 512]
                                sch.op(PE, lambda t, l=lhs, r=rhs, p2=psl,
                                       st=(hc == 0), sp=(hc == 63):
                                       t.matmul(p2, l, r, start=st, stop=sp),
                                       reads=[gt_b, w2g_b[par]],
                                       writes=[psd_b[ts]], tag="dn_mm")
                    for ts in range(4):
                        psl = ps_d[:, ts * 512:(ts + 1) * 512]
                        xsl = ab.ap(x1tok_o + ts * 2048 + dmc * 512, 512)
                        rsl = af.ap(r4_o[ts] + dmc * 512, 512)
                        sch.op(DVE, lambda v, p2=psl, xs=xsl, d=rsl:
                               v.scalar_tensor_tensor(d, p2, 1.0, xs, ALU.mult,
                                                      ALU.add),
                               reads=[psd_b[ts], x1tok_b], writes=[r4_b[ts]],
                               tag="dn_res")
                for ts in range(4):
                    rr = af.ap(r4_o[ts], 2048)
                    of = af.ap(outf_o, 2048)
                    bn24 = af.ap(st_o, 24)
                    bn2 = af.ap(st_o + 24, 2)
                    for i in range(4):
                        bno = af.ap(st_o + i * 6, 6)
                        bni = _ap3(rr, [(1, 512)], i * 512)
                        sch.op(DVE, (lambda o2, i2: (lambda v: v.bn_stats(
                            o2, i2)))(bno, bni),
                            reads=[r4_b[ts]], writes=[st_b], tag="ln2_bns")
                    sch.op(DVE, lambda v: v.bn_aggr(bn2, bn24),
                           reads=[st_b], writes=[st_b], tag="ln2_bna")
                    mean = af.ap(st_o + 24, 1)
                    var = af.ap(st_o + 25, 1)
                    sd = af.ap(st_o + 26, 1)
                    sch.op(ACT, lambda a: a.activation(sd, var, AF.Sqrt,
                                                       bias=epsln_ap()),
                           reads=[st_b, epsln_b], writes=[st_b], tag="ln2_sd")
                    rs = af.ap(st_o + 27, 1)
                    sch.op(DVE, lambda v: v.reciprocal(rs, sd),
                           reads=[st_b], writes=[st_b], tag="ln2_rs")
                    sch.op(DVE, lambda v, s=rr, d=of: v.tensor_scalar(
                        d, s, mean, rs, ALU.subtract, ALU.mult),
                        reads=[r4_b[ts], st_b], writes=[outf_b], tag="ln2_out")
                    odst = _dram_ap(out_t, [(E, 128), (1, E)],
                                    (p * 4 + ts) * 128 * E)
                    sch.op(ACT, lambda a, s=of, d=odst: a.dma_start(out=d, in_=s),
                           reads=[outf_b], dma=True, tag="out_st")

        finals = dict(sch.counts)
        sems = {}
        for name in _ALL_SEMS:
            sems[name] = ctx.enter_context(nc.semaphore(name))
        block = ctx.enter_context(nc.Block())
        sch.emit(block, sems, nc)

        @block.sync
        def _(sync):
            for qn in (Q_ACT, Q_SP, Q_POOL):
                if finals[qn]:
                    sync.wait_ge(sems[qn], finals[qn])

    return nc


_NC_CACHE = None


def _get_nc():
    global _NC_CACHE
    if _NC_CACHE is None:
        _NC_CACHE = build_nc()
    return _NC_CACHE


def kernel(L0_attn_hidden_states, L0_attn_wq, L0_attn_wk, L0_attn_wv, L0_attn_wo,
           L0_attn_query_scale, L0_attn_key_scale, L0_ffn_W1, L0_ffn_W2, L0_ffn_W3):
    consts = host_constants()
    nc = _get_nc()
    xs = np.ascontiguousarray(np.asarray(L0_attn_hidden_states, dtype=np.float32))
    shared = {
        "wq": np.ascontiguousarray(np.asarray(L0_attn_wq, np.float32)),
        "wk": np.ascontiguousarray(np.asarray(L0_attn_wk, np.float32)),
        "wv": np.ascontiguousarray(np.asarray(L0_attn_wv, np.float32)),
        "wo": np.ascontiguousarray(np.asarray(L0_attn_wo, np.float32)),
        "qscale": np.ascontiguousarray(np.asarray(L0_attn_query_scale, np.float32)),
        "kscale": np.ascontiguousarray(np.asarray(L0_attn_key_scale, np.float32)),
        "w1": np.ascontiguousarray(np.asarray(L0_ffn_W1, np.float32)),
        "w2": np.ascontiguousarray(np.asarray(L0_ffn_W2, np.float32)),
        "w3": np.ascontiguousarray(np.asarray(L0_ffn_W3, np.float32)),
    }
    shared.update(consts)
    in_maps = []
    for b in range(B):
        m = dict(shared)
        m["x"] = np.ascontiguousarray(xs[b])
        in_maps.append(m)
    res = run_bass_kernel_spmd(nc, in_maps, core_ids=list(range(B)))
    out = np.stack([r["out"] for r in res.results], axis=0)
    return out


# version 08
# speedup vs baseline: 1.2325x; 1.0414x over previous
import dataclasses
import numpy as np
import ml_dtypes
import concourse.bass as bass
import concourse.mybir as mybir
from concourse.bass_utils import run_bass_kernel_spmd
from contextlib import ExitStack

F32 = mybir.dt.float32
BF16 = mybir.dt.bfloat16
AF = mybir.ActivationFunctionType
ALU = mybir.AluOpType
AX = mybir.AxisListType

B = 8
S = 4096
E = 2048
NH = 16
NKV = 4
D = 128
CAP = 50.0
EPS_NRM = 1e-6
EPS_LN = 1e-5
HF = 8192
CH = 512
NCH = S // CH
NT = S // 128

_DEBUG = False


def _sylvester(n):
    H = np.array([[1.0]], dtype=np.float64)
    while H.shape[0] < n:
        H = np.block([[H, H], [H, -H]])
    return H


def host_constants():
    c = {}
    c["c_ident"] = np.eye(128, dtype=np.float32).astype(ml_dtypes.bfloat16)
    P = np.zeros((128, 128), np.float32)
    for i in range(64):
        P[i + 64, i] = -1.0
        P[i, i + 64] = 1.0
    c["c_p128"] = P.astype(ml_dtypes.bfloat16)
    c["c_ones"] = np.ones((128, 128), np.float32).astype(ml_dtypes.bfloat16)
    c["c_h64"] = (_sylvester(64) / 64.0).astype(ml_dtypes.bfloat16)
    c["c_h128"] = (_sylvester(128) / np.sqrt(2.0)).astype(ml_dtypes.bfloat16)
    kl = np.arange(128)[:, None]
    ql = np.arange(128)[None, :]
    c["c_maskhi"] = (kl <= ql).astype(np.float32).astype(ml_dtypes.bfloat16)
    c["c_masklo"] = (ql <= kl).astype(np.float32).astype(ml_dtypes.bfloat16)
    half = D // 2
    inv = 1.0 / 10000.0 ** (np.arange(half, dtype=np.float32) / half)
    ang = np.outer(np.arange(S, dtype=np.float32), inv)
    ang = np.concatenate([ang, ang], axis=-1)
    c["c_cosT"] = np.ascontiguousarray(np.cos(ang).T.astype(np.float32))
    c["c_sinT"] = np.ascontiguousarray(np.sin(ang).T.astype(np.float32))
    return c


PE, ACT, DVE, POOL, SP = "pe", "act", "dve", "pool", "sp"
Q_POOL, Q_SP, Q_ACT = "q_pool", "q_sp", "q_act"
_DMA_Q = {POOL: Q_POOL, SP: Q_SP, ACT: Q_ACT}
_ALL_SEMS = (PE, ACT, DVE, POOL, SP, Q_POOL, Q_SP, Q_ACT)


class Buf:
    __slots__ = ("writer", "readers", "name")

    def __init__(self, name, fence):
        self.writer = None
        self.readers = list(fence)
        self.name = name


class OpRec:
    __slots__ = ("build", "waits", "sem", "amount", "tag", "phase")

    def __init__(self, build, waits, sem, amount, tag, phase):
        self.build = build
        self.waits = waits
        self.sem = sem
        self.amount = amount
        self.tag = tag
        self.phase = phase


class DramBuf:
    __slots__ = ("writers", "name")

    def __init__(self, name):
        self.writers = []
        self.name = name


class Sched:
    def __init__(self):
        self.ops = {k: [] for k in (PE, ACT, DVE, POOL, SP)}
        self.counts = {k: 0 for k in _ALL_SEMS}
        self.fence = []
        self.phase = "init"

    def buf(self, name):
        return Buf(name, self.fence)

    def set_fence(self):
        self.fence = [(k, v) for k, v in self.counts.items() if v > 0]

    def op(self, eng, build, reads=(), writes=(), dreads=(), dwrites=(),
           dma=False, drain=False, tag=""):
        sem = _DMA_Q[eng] if dma else eng
        amount = 16 if dma else 1
        waits = {}

        def add(tok):
            if tok is None:
                return
            s, v = tok
            if (not dma) and eng == PE and s == PE:
                return
            if waits.get(s, 0) < v:
                waits[s] = v

        for b in reads:
            add(b.writer)
        for b in writes:
            add(b.writer)
            for t in b.readers:
                add(t)
        for db in dreads:
            for t in db.writers:
                add(t)
        self.counts[sem] += amount
        tok = (sem, self.counts[sem])
        self.ops[eng].append(OpRec(build, waits, sem, amount, tag, self.phase))
        for b in reads:
            b.readers.append(tok)
        for b in writes:
            b.writer = tok
            b.readers = []
        for db in dwrites:
            db.writers.append(tok)
        if drain:
            assert dma
            self.ops[eng].append(OpRec(
                lambda e: e.nop(), {sem: self.counts[sem]}, eng, 1,
                tag + "_drain", self.phase))
            self.counts[eng] += 1
        return tok

    def emit(self, block, sems, nc=None):
        eng_of = {PE: "tensor", ACT: "scalar", DVE: "vector", POOL: "gpsimd",
                  SP: "sync"}
        for eng_name, ops in self.ops.items():
            if not ops:
                continue
            deco = getattr(block, eng_of[eng_name])

            def make(ops_local):
                def _(eng):
                    from contextlib import ExitStack as _ES
                    waited = {}
                    cur_phase = [None]
                    stk = _ES()
                    for rec in ops_local:
                        if nc is not None and rec.phase != cur_phase[0]:
                            stk.close()
                            stk = _ES()
                            stk.enter_context(nc.named_scope(rec.phase))
                            cur_phase[0] = rec.phase
                        for s, v in sorted(rec.waits.items()):
                            if waited.get(s, 0) >= v:
                                continue
                            eng.wait_ge(sems[s], v)
                            waited[s] = v
                        rec.build(eng).then_inc(sems[rec.sem], rec.amount)
                    stk.close()
                return _

            deco(make(ops))


class Arena:
    def __init__(self, nc, name, dtype, n_elems):
        self.t = nc.alloc_sbuf_tensor(name, [128, n_elems], dtype)
        self.off = 0
        self.n = n_elems
        self.mark = 0

    def take(self, elems):
        o = self.off
        self.off += elems
        assert self.off <= self.n, f"{self.t} arena overflow {self.off}>{self.n}"
        return o

    def set_mark(self):
        self.mark = self.off

    def reset(self):
        self.off = self.mark

    def ap(self, off, elems, parts=128):
        return self.t[0:parts, off:off + elems]


def _ap3(ap, dims, offset_elems=0):
    part_dim = list(ap.ap[0])
    new_dims = [part_dim] + [[s, c] for (s, c) in dims]
    return dataclasses.replace(ap, offset=ap.offset + offset_elems, ap=new_dims)


def _dram_ap(t, dims, offset_elems=0):
    full = t.ap()
    return dataclasses.replace(full, offset=full.offset + offset_elems,
                               ap=[[s, c] for (s, c) in dims])


def build_nc():
    nc = bass.Bass("TRN2", target_bir_lowering=False)

    x_in = nc.dram_tensor("x", [S, E], F32, kind="ExternalInput")
    wq_in = nc.dram_tensor("wq", [E, NH * D], F32, kind="ExternalInput")
    wk_in = nc.dram_tensor("wk", [E, NKV * D], F32, kind="ExternalInput")
    wv_in = nc.dram_tensor("wv", [E, NKV * D], F32, kind="ExternalInput")
    wo_in = nc.dram_tensor("wo", [NH * D, E], F32, kind="ExternalInput")
    qs_in = nc.dram_tensor("qscale", [D], F32, kind="ExternalInput")
    ks_in = nc.dram_tensor("kscale", [D], F32, kind="ExternalInput")
    w1_in = nc.dram_tensor("w1", [HF, E], F32, kind="ExternalInput")
    w2_in = nc.dram_tensor("w2", [E, HF], F32, kind="ExternalInput")
    w3_in = nc.dram_tensor("w3", [HF, E], F32, kind="ExternalInput")
    cin = {}
    for name, shape, dt in [
        ("c_ident", [128, 128], BF16), ("c_p128", [128, 128], BF16),
        ("c_ones", [128, 128], BF16), ("c_h64", [64, 64], BF16),
        ("c_h128", [128, 128], BF16), ("c_maskhi", [128, 128], BF16),
        ("c_masklo", [128, 128], BF16), ("c_cosT", [128, S], F32),
        ("c_sinT", [128, S], F32),
    ]:
        cin[name] = nc.dram_tensor(name, shape, dt, kind="ExternalInput")
    out_t = nc.dram_tensor("out", [S, E], F32, kind="ExternalOutput")

    skind = "ExternalOutput" if _DEBUG else "Internal"
    w1t_s = nc.dram_tensor("w1t_s", [16, 128, HF], BF16, kind="Internal")
    w3t_s = nc.dram_tensor("w3t_s", [16, 128, HF], BF16, kind="Internal")
    b_s = nc.dram_tensor("b_s", [64, 128, E], BF16, kind=skind)
    w2f_s = nc.dram_tensor("w2f_s", [64, 128, E], BF16, kind=skind)
    qt_s = nc.dram_tensor("qt_s", [NCH, NH, 128, CH], BF16, kind=skind)
    kt_s = nc.dram_tensor("kt_s", [NCH, NKV, 128, CH], BF16, kind=skind)
    v_s = nc.dram_tensor("v_s", [NCH, 4, 128, 516], BF16, kind=skind)
    aot_s = nc.dram_tensor("aot_s", [NCH, 16, 128, CH], BF16, kind=skind)
    x1_s = nc.dram_tensor("x1_s", [NT, 128, E], BF16, kind=skind)
    x1t_s = nc.dram_tensor("x1t_s", [16, 128, S], BF16, kind=skind)
    dbg_r = (nc.dram_tensor("dbg_r", [NT, 128, E], F32, kind="ExternalOutput")
             if _DEBUG else None)
    dbg_st = (nc.dram_tensor("dbg_st", [NT, 128, 28], F32, kind="ExternalOutput")
              if _DEBUG else None)

    sch = Sched()
    DB = {n: DramBuf(n) for n in ("w1t", "w3t", "b", "w2f", "qt", "kt", "v",
                                  "aot", "x1", "x1t")}

    with ExitStack() as ctx:
        ab = Arena(nc, "AB", BF16, 77700)
        af = Arena(nc, "AF", F32, 12400)

        co = {}
        CB = {}
        for name, elems, parts in [
            ("c_ident", 128, 128), ("c_p128", 128, 128), ("c_ones", 128, 128),
            ("c_h64", 64, 64), ("c_h128", 128, 128), ("c_maskhi", 128, 128),
            ("c_masklo", 128, 128),
        ]:
            co[name] = ab.take(elems)
            CB[name] = sch.buf(name)
            dst = ab.ap(co[name], elems, parts)
            src = cin[name].ap()
            sch.op(POOL, (lambda d, s: (lambda g: g.dma_start(out=d, in_=s)))(dst, src),
                   writes=[CB[name]], dma=True, drain=(parts != 128),
                   tag=f"ld_{name}")
        ident = lambda: ab.ap(co["c_ident"], 128)
        p128 = lambda: ab.ap(co["c_p128"], 128)
        ones = lambda: ab.ap(co["c_ones"], 128)
        h64 = lambda: ab.ap(co["c_h64"], 64, 64)
        h128 = lambda: ab.ap(co["c_h128"], 128)
        maskhi = lambda: ab.ap(co["c_maskhi"], 128)
        masklo = lambda: ab.ap(co["c_masklo"], 128)

        kc_o = af.take(2)
        kc_b = sch.buf("kc")
        qs_dst = af.ap(kc_o, 1)
        ks_dst = af.ap(kc_o + 1, 1)
        sch.op(POOL, lambda g: g.dma_start(
            out=qs_dst, in_=_dram_ap(qs_in, [(1, 128), (1, 1)]),
            allow_slow_non_contiguous=True), writes=[kc_b], dma=True, drain=True, tag="ld_qs")
        sch.op(POOL, lambda g: g.dma_start(
            out=ks_dst, in_=_dram_ap(ks_in, [(1, 128), (1, 1)]),
            allow_slow_non_contiguous=True), writes=[kc_b], dma=True, drain=True, tag="ld_ks")
        sch.op(DVE, lambda v: v.tensor_mul(qs_dst, qs_dst, ks_dst),
               reads=[kc_b], writes=[kc_b], tag="kc_mul")
        sch.op(DVE, lambda v: v.tensor_scalar_mul(
            qs_dst, qs_dst, float(1.0 / (CAP * np.sqrt(D)))),
            reads=[kc_b], writes=[kc_b], tag="kc_sc")
        kc_ap = lambda: af.ap(kc_o, 1)

        epsln_o = af.take(1)
        epsln_b = sch.buf("epsln")
        sch.op(DVE, lambda v: v.memset(af.ap(epsln_o, 1), EPS_LN),
               writes=[epsln_b], tag="eps_set")
        epsln_ap = lambda: af.ap(epsln_o, 1)
        epsnr_o = af.take(1)
        epsnr_b = sch.buf("epsnr")
        sch.op(DVE, lambda v: v.memset(af.ap(epsnr_o, 1), EPS_NRM),
               writes=[epsnr_b], tag="epsn_set")
        epsnr_ap = lambda: af.ap(epsnr_o, 1)

        ab.set_mark()
        af.set_mark()

        sch.phase = "P0wprep"
        with ExitStack() as pctx:
            ps_tr = pctx.enter_context(nc.psum_tensor("p0_tr", [128, 2048], BF16))
            ps_tr2 = pctx.enter_context(nc.psum_tensor("p0_tr2", [128, 2048], BF16))
            ps_f = pctx.enter_context(nc.psum_tensor("p0_f", [64, 1024], F32))
            ps_g = pctx.enter_context(nc.psum_tensor("p0_g", [128, 1024], F32))
            pstr_t = [ps_tr, ps_tr2]
            ps_trb = [sch.buf("p0tr0"), sch.buf("p0tr1")]
            psf_b = [sch.buf("p0f0"), sch.buf("p0f1")]
            psg_b = [sch.buf("p0g0"), sch.buf("p0g1")]

            slabf_o = [af.take(2048), af.take(2048), af.take(2048)]
            slabf_b = [sch.buf("sf0"), sch.buf("sf1"), sch.buf("sf2")]
            slabb_o = [ab.take(2048), ab.take(2048)]
            evacb_o = [ab.take(2048), ab.take(2048)]
            slabb_b = [sch.buf("sb0"), sch.buf("sb1")]
            evacb_b = [sch.buf("ev0"), sch.buf("ev1")]

            def slab_load(w_dram, hc, par):
                sf = af.ap(slabf_o[par], 2048)
                src = _dram_ap(w_dram, [(2048, 128), (1, 2048)], hc * 128 * 2048)
                sch.op(POOL, lambda g, s=src, d=sf: g.dma_start(out=d, in_=s),
                       writes=[slabf_b[par]], dma=True, tag=f"w_ld_{hc}")

            for wi, (w_dram, wt_dram) in enumerate([(w1_in, w1t_s), (w3_in, w3t_s)]):
                for hc in range(64):
                    gidx = wi * 64 + hc
                    if gidx == 0:
                        slab_load(w1_in, 0, 0)
                        slab_load(w1_in, 1, 1)
                    nxt = gidx + 2
                    if nxt < 128:
                        slab_load(w1_in if nxt < 64 else w3_in, nxt % 64, nxt % 3)
                    par = gidx % 3
                    bpar = gidx % 2
                    sf = af.ap(slabf_o[par], 2048)
                    sb = ab.ap(slabb_o[bpar], 2048)
                    sch.op(ACT, lambda a, s=sf, d=sb: a.copy(d, s),
                           reads=[slabf_b[par]], writes=[slabb_b[bpar]],
                           tag="w_cvt")
                    ev = ab.ap(evacb_o[gidx % 2], 2048)
                    pst = pstr_t[gidx % 2]
                    for k in range(16):
                        sch.op(PE, (lambda kk, sbv=sb, p=pst: (lambda t: t.transpose(
                            p[:, kk * 128:(kk + 1) * 128],
                            sbv[:, kk * 128:(kk + 1) * 128], ident())))(k),
                            reads=[slabb_b[bpar], CB["c_ident"]],
                            writes=[ps_trb[gidx % 2]], tag="w_tr")
                    sch.op(DVE, lambda v, p=pst, d=ev: v.tensor_copy(d, p[:]),
                           reads=[ps_trb[gidx % 2]], writes=[evacb_b[gidx % 2]],
                           tag="w_ev")
                    dst = _dram_ap(wt_dram, [(HF, 128), (128 * HF, 16), (1, 128)],
                                   hc * 128)
                    esrc = _ap3(ev, [(128, 16), (1, 128)])
                    sch.op(SP, lambda q, s=esrc, d=dst: q.dma_start(out=d, in_=s),
                           reads=[evacb_b[gidx % 2]],
                           dwrites=[DB["w1t"] if wi == 0 else DB["w3t"]],
                           dma=True, tag="w_st")

            t2f_o = [af.take(4096)]
            t2f_b = [sch.buf("t2f0")]
            t2b_o = [ab.take(4096), ab.take(4096)]
            bb_o = [ab.take(4096), ab.take(4096)]
            t2b_b = [sch.buf("t2b0"), sch.buf("t2b1")]
            bb_b = [sch.buf("bb0"), sch.buf("bb1")]

            def f1_load(g):
                tf2 = af.ap(t2f_o[0], 4096, 64)
                s2 = _dram_ap(w2_in, [(128, 64), (8192, 32), (1, 128)],
                              g * 32 * 8192)
                sch.op(POOL, lambda g_, s=s2, d=tf2: g_.dma_start(out=d, in_=s),
                       writes=[t2f_b[0]], dma=True, tag=f"f1_ld_{g}")

            f1_load(0)
            for g in range(64):
                par = g % 2
                tf = af.ap(t2f_o[0], 4096, 64)
                tb = ab.ap(t2b_o[par], 4096, 64)
                sch.op(ACT, lambda a, s=tf, d=tb: a.copy(d, s),
                       reads=[t2f_b[0]], writes=[t2b_b[par]], tag="f1_cvt")
                if g + 1 < 64:
                    f1_load(g + 1)
                bbv = ab.ap(bb_o[g % 2], 4096, 64)
                for m in range(8):
                    mp = m % 2
                    rhs = _ap3(tb, [(1, 16), (128, 32)], m * 16)
                    psl = ps_f[:, mp * 512:(mp + 1) * 512]
                    sch.op(PE, lambda t, r=rhs, p=psl: t.matmul(
                        p, h64(), r, start=True, stop=True),
                        reads=[t2b_b[par], CB["c_h64"]], writes=[psf_b[mp]],
                        tag="f1_mm")
                    dstb = _ap3(bbv, [(1, 512)], m * 512)
                    sch.op(DVE, lambda v, p=psl, d=dstb: v.tensor_copy(d, p),
                           reads=[psf_b[mp]], writes=[bb_b[g % 2]], tag="f1_ev")
                dst = _dram_ap(b_s, [(128 * E, 64), (E, 128), (1, 32)], g * 32)
                bsrc = _ap3(bbv, [(32, 128), (1, 32)])
                sch.op(SP, lambda q, s=bsrc, d=dst: q.dma_start(out=d, in_=s),
                       reads=[bb_b[g % 2]], dwrites=[DB["b"]], dma=True,
                       tag=f"f1_st_{g}")

            bslab_o = [ab.take(2048) for _ in range(4)]
            w2fe_o = [ab.take(2048), ab.take(2048)]
            bslab_b = [sch.buf(f"bs{i}") for i in range(4)]
            w2fe_b = [sch.buf("we0"), sch.buf("we1")]

            def f2_load(ip):
                bsl2 = ab.ap(bslab_o[ip % 4], 2048)
                s2 = _dram_ap(b_s, [(E, 128), (1, E)], ip * 128 * E)
                sch.op(POOL, lambda g_, s=s2, d=bsl2: g_.dma_start(out=d, in_=s),
                       writes=[bslab_b[ip % 4]], dreads=[DB["b"]], dma=True,
                       tag=f"f2_ld_{ip}")

            for j in range(3):
                f2_load(j)
            for ip in range(64):
                par = ip % 4
                if ip + 3 < 64:
                    f2_load(ip + 3)
                bsl = ab.ap(bslab_o[par], 2048)
                wfe = ab.ap(w2fe_o[ip % 2], 2048)
                for dc in range(4):
                    dp = dc % 2
                    psl = ps_g[:, dp * 512:(dp + 1) * 512]
                    sch.op(PE, lambda t, p=psl, r=bsl[:, dc * 512:(dc + 1) * 512]:
                           t.matmul(p, h128(), r, start=True, stop=True),
                           reads=[bslab_b[par], CB["c_h128"]], writes=[psg_b[dp]],
                           tag="f2_mm")
                    sch.op(DVE, lambda v, p=psl, d=wfe[:, dc * 512:(dc + 1) * 512]:
                           v.tensor_copy(d, p),
                           reads=[psg_b[dp]], writes=[w2fe_b[ip % 2]], tag="f2_ev")
                dst = _dram_ap(w2f_s, [(E, 128), (1, E)], ip * 128 * E)
                sch.op(SP, lambda q, s=wfe, d=dst: q.dma_start(out=d, in_=s),
                       reads=[w2fe_b[ip % 2]], dwrites=[DB["w2f"]], dma=True,
                       tag=f"f2_st_{ip}")

        sch.set_fence()
        ab.reset()
        af.reset()

        sch.phase = "P1qkv"
        with ExitStack() as pctx:
            ps_tr = pctx.enter_context(nc.psum_tensor("p1_tr", [128, 2048], BF16))
            ps_a = pctx.enter_context(nc.psum_tensor("p1_a", [128, 1024], F32))
            ps_r = pctx.enter_context(nc.psum_tensor("p1_r", [128, 1024], F32))
            ps_s = pctx.enter_context(nc.psum_tensor("p1_s", [128, 1024], F32))
            pstr_t = [ps_tr, ps_tr]
            ps_trb0 = sch.buf("p1tr0")
            ps_trb = [ps_trb0, ps_trb0]
            psa_b = [sch.buf("p1a0"), sch.buf("p1a1")]
            psr_b = [sch.buf("p1r0"), sch.buf("p1r1")]
            pss_b = [sch.buf("p1s0"), sch.buf("p1s1")]

            wq_o = ab.take(16 * 2048)
            wk_o = ab.take(16 * 512)
            wv_o = ab.take(16 * 512)
            wq_b = sch.buf("wq"); wk_b = sch.buf("wk"); wv_b = sch.buf("wv")
            xf_o = [af.take(2048), af.take(2048)]
            xf_b = [sch.buf("xf0"), sch.buf("xf1")]
            wstg_o = xf_o
            wstg_b = xf_b
            idx = 0
            for (wt, off, width, buf) in [(wq_in, wq_o, 2048, wq_b),
                                          (wk_in, wk_o, 512, wk_b),
                                          (wv_in, wv_o, 512, wv_b)]:
                for dm in range(16):
                    par = idx % 2
                    idx += 1
                    stg = af.ap(wstg_o[par], width)
                    src = _dram_ap(wt, [(width, 128), (1, width)], dm * 128 * width)
                    sch.op(POOL, lambda g_, s=src, d=stg: g_.dma_start(out=d, in_=s),
                           writes=[wstg_b[par]], dma=True, tag="wqkv_ld")
                    dstb = ab.ap(off + dm * width, width)
                    sch.op(ACT, lambda a, s=stg, d=dstb: a.copy(d, s),
                           reads=[wstg_b[par]], writes=[buf], tag="wqkv_cvt")

            xb_o = [ab.take(2048), ab.take(2048)]
            xb_b = [sch.buf("xb0"), sch.buf("xb1")]
            xt_o = ab.take(16 * 512)
            xt_b = sch.buf("xt")
            cs_o = [af.take(512), af.take(512)]
            sn_o = [af.take(512), af.take(512)]
            cs_b = [sch.buf("cs0"), sch.buf("cs1")]
            sn_b = [sch.buf("sn0"), sch.buf("sn1")]
            qbf_o = [ab.take(512) for _ in range(3)]
            qbf_b = [sch.buf(f"qb{i}") for i in range(3)]
            t1_o = [af.take(512) for _ in range(3)]
            t2_o = [af.take(512) for _ in range(3)]
            qp_o = [af.take(512) for _ in range(3)]
            t1_b = [sch.buf(f"t1{i}") for i in range(3)]
            t2_b = [sch.buf(f"t2{i}") for i in range(3)]
            qp_b = [sch.buf(f"qp{i}") for i in range(3)]
            sqb_o = [ab.take(512) for _ in range(3)]
            sqb_b = [sch.buf(f"sq{i}") for i in range(3)]
            nrm_o = [af.take(512) for _ in range(3)]
            nrm_b = [sch.buf(f"nr{i}") for i in range(3)]
            rcp_o = t2_o
            rcp_b = t2_b
            kf_o = t1_o
            kf_b = t1_b
            qtc_o = ab.take(16 * 512); qtc_b = sch.buf("qtc")
            ktc_o = ab.take(4 * 512); ktc_b = sch.buf("ktc")
            vt_o = ab.take(4 * 516); vt_b = sch.buf("vt")

            def rope_norm(par, ci, psl, psl_buf, out_off, out_buf, is_k, c):
                qb = ab.ap(qbf_o[ci], 512)
                psr = ps_r[:, par * 512:(par + 1) * 512]
                pss = ps_s[:, par * 512:(par + 1) * 512]
                sch.op(DVE, lambda v, p=psl, d=qb: v.tensor_copy(d, p),
                       reads=[psl_buf], writes=[qbf_b[ci]], tag="r_ev")
                sch.op(PE, lambda t, s=qb, p=psr: t.matmul(p, p128(), s,
                                                           start=True, stop=True),
                       reads=[qbf_b[ci], CB["c_p128"]], writes=[psr_b[par]],
                       tag="r_rot")
                csl = af.ap(cs_o[c % 2], 512)
                snl = af.ap(sn_o[c % 2], 512)
                t1 = af.ap(t1_o[ci], 512)
                t2 = af.ap(t2_o[ci], 512)
                qp = af.ap(qp_o[ci], 512)
                sch.op(DVE, lambda v, q=qb: v.tensor_mul(t1, q, csl),
                       reads=[qbf_b[ci], cs_b[c % 2]], writes=[t1_b[ci]],
                       tag="r_t1")
                sch.op(DVE, lambda v, p=psr: v.tensor_mul(t2, p, snl),
                       reads=[psr_b[par], sn_b[c % 2]], writes=[t2_b[ci]],
                       tag="r_t2")
                sch.op(DVE, lambda v: v.tensor_add(qp, t1, t2),
                       reads=[t1_b[ci], t2_b[ci]], writes=[qp_b[ci]], tag="r_qp")
                sq = ab.ap(sqb_o[ci], 512)
                sch.op(DVE, lambda v: v.tensor_mul(sq, qp, qp),
                       reads=[qp_b[ci]], writes=[sqb_b[ci]], tag="r_sq")
                sch.op(PE, lambda t, p=pss: t.matmul(p, ones(), sq,
                                                     start=True, stop=True),
                       reads=[sqb_b[ci], CB["c_ones"]], writes=[pss_b[par]],
                       tag="r_ss")
                nr = af.ap(nrm_o[ci], 512)
                sch.op(ACT, lambda a, p=pss: a.activation(nr, p, AF.Sqrt),
                       reads=[pss_b[par]], writes=[nrm_b[ci]], tag="r_sqrt")
                sch.op(DVE, lambda v: v.tensor_scalar_add(nr, nr, EPS_NRM),
                       reads=[nrm_b[ci]], writes=[nrm_b[ci]], tag="r_eps")
                rc = af.ap(rcp_o[ci], 512)
                sch.op(DVE, lambda v: v.reciprocal(rc, nr),
                       reads=[nrm_b[ci]], writes=[rcp_b[ci]], tag="r_rcp")
                outap = ab.ap(out_off, 512)
                if is_k:
                    kf = af.ap(kf_o[ci], 512)
                    sch.op(DVE, lambda v: v.tensor_mul(kf, qp, rc),
                           reads=[qp_b[ci], rcp_b[ci]], writes=[kf_b[ci]],
                           tag="r_km")
                    sch.op(DVE, lambda v: v.tensor_scalar_mul(outap, kf, kc_ap()),
                           reads=[kf_b[ci], kc_b], writes=[out_buf], tag="r_ks")
                else:
                    sch.op(DVE, lambda v: v.tensor_mul(outap, qp, rc),
                           reads=[qp_b[ci], rcp_b[ci]], writes=[out_buf],
                           tag="r_qm")

            for c in range(NCH):
                cp = c % 2
                csl = af.ap(cs_o[cp], 512)
                snl = af.ap(sn_o[cp], 512)
                sch.op(POOL, lambda g_, d=csl, s=cin["c_cosT"].ap()[:, c * CH:(c + 1) * CH]:
                       g_.dma_start(out=d, in_=s), writes=[cs_b[cp]], dma=True,
                       tag="cs_ld")
                sch.op(POOL, lambda g_, d=snl, s=cin["c_sinT"].ap()[:, c * CH:(c + 1) * CH]:
                       g_.dma_start(out=d, in_=s), writes=[sn_b[cp]], dma=True,
                       tag="sn_ld")
                for t4 in range(4):
                    par = t4 % 2
                    xf = af.ap(xf_o[par], 2048)
                    xb = ab.ap(xb_o[par], 2048)
                    row0 = (c * 4 + t4) * 128
                    src = _dram_ap(x_in, [(E, 128), (1, E)], row0 * E)
                    sch.op(POOL, lambda g_, s=src, d=xf: g_.dma_start(out=d, in_=s),
                           writes=[xf_b[par]], dma=True, tag=f"x_ld_{c}_{t4}")
                    sch.op(ACT, lambda a, s=xf, d=xb: a.copy(d, s),
                           reads=[xf_b[par]], writes=[xb_b[par]], tag="x_cvt")
                    pst = pstr_t[par]
                    for k in range(16):
                        sch.op(PE, (lambda kk, xbv=xb, p=pst: (lambda t: t.transpose(
                            p[:, kk * 128:(kk + 1) * 128],
                            xbv[:, kk * 128:(kk + 1) * 128], ident())))(k),
                            reads=[xb_b[par], CB["c_ident"]], writes=[ps_trb[par]],
                            tag="x_tr")
                    xt_dst = _ap3(ab.ap(xt_o, 1), [(512, 16), (1, 128)], t4 * 128)
                    sch.op(DVE, lambda v, p=pst, d=xt_dst: v.tensor_copy(d, p[:]),
                           reads=[ps_trb[par]], writes=[xt_b], tag="x_ev")
                for h in range(NH):
                    par = h % 2
                    psl = ps_a[:, par * 512:(par + 1) * 512]
                    for dc in range(16):
                        lhs = ab.ap(wq_o + dc * 2048 + h * 128, 128)
                        rhs = ab.ap(xt_o + dc * 512, 512)
                        sch.op(PE, lambda t, l=lhs, r=rhs, p=psl, st=(dc == 0),
                               sp=(dc == 15): t.matmul(p, l, r, start=st, stop=sp),
                               reads=[wq_b, xt_b], writes=[psa_b[par]], tag="q_mm")
                    rope_norm(par, h % 3, psl, psa_b[par], qtc_o + h * 512,
                              qtc_b, False, c)
                dst = _dram_ap(qt_s, [(CH, 128), (128 * CH, 16), (1, CH)],
                               c * NH * 128 * CH)
                qsrc = _ap3(ab.ap(qtc_o, 1), [(512, 16), (1, 512)])
                sch.op(SP, lambda q, s=qsrc, d=dst: q.dma_start(out=d, in_=s),
                       reads=[qtc_b], dwrites=[DB["qt"]], dma=True, tag="qt_st")
                for kv in range(NKV):
                    par = kv % 2
                    psl = ps_a[:, par * 512:(par + 1) * 512]
                    for dc in range(16):
                        lhs = ab.ap(wk_o + dc * 512 + kv * 128, 128)
                        rhs = ab.ap(xt_o + dc * 512, 512)
                        sch.op(PE, lambda t, l=lhs, r=rhs, p=psl, st=(dc == 0),
                               sp=(dc == 15): t.matmul(p, l, r, start=st, stop=sp),
                               reads=[wk_b, xt_b], writes=[psa_b[par]], tag="k_mm")
                    rope_norm(par, kv % 3, psl, psa_b[par], ktc_o + kv * 512,
                              ktc_b, True, c)
                dst = _dram_ap(kt_s, [(CH, 128), (128 * CH, 4), (1, CH)],
                               c * NKV * 128 * CH)
                ksrc = _ap3(ab.ap(ktc_o, 1), [(512, 4), (1, 512)])
                sch.op(SP, lambda q, s=ksrc, d=dst: q.dma_start(out=d, in_=s),
                       reads=[ktc_b], dwrites=[DB["kt"]], dma=True, tag="kt_st")
                ones_dst = _ap3(ab.ap(vt_o, 1), [(516, 4), (129, 4), (1, 1)], 128)
                sch.op(DVE, lambda v, d=ones_dst: v.memset(d, 1.0),
                       writes=[vt_b], tag="v_ones")
                for t4 in range(4):
                    par = t4 % 2
                    psl = ps_a[:, par * 512:(par + 1) * 512]
                    for dc in range(16):
                        lhs = ab.ap(xt_o + dc * 512 + t4 * 128, 128)
                        rhs = ab.ap(wv_o + dc * 512, 512)
                        sch.op(PE, lambda t, l=lhs, r=rhs, p=psl, st=(dc == 0),
                               sp=(dc == 15): t.matmul(p, l, r, start=st, stop=sp),
                               reads=[wv_b, xt_b], writes=[psa_b[par]], tag="v_mm")
                    vdst = _ap3(ab.ap(vt_o, 1), [(129, 4), (1, 128)], t4 * 516)
                    sch.op(DVE, lambda v, p=psl, d=vdst: v.tensor_copy(d, p),
                           reads=[psa_b[par]], writes=[vt_b], tag="v_ev")
                    vst = _dram_ap(v_s, [(516, 128), (1, 516)],
                                   (c * 4 + t4) * 128 * 516)
                    vsr = _ap3(ab.ap(vt_o, 1), [(1, 516)], t4 * 516)
                    sch.op(SP, lambda q, s=vsr, d=vst: q.dma_start(out=d, in_=s),
                           reads=[vt_b], dwrites=[DB["v"]], dma=True, tag="v_st")

        sch.set_fence()
        ab.reset()
        af.reset()

        sch.phase = "P2scores"
        with ExitStack() as pctx:
            ps_sc = pctx.enter_context(nc.psum_tensor("p2_sc", [128, 1536], F32))
            ps_av = pctx.enter_context(nc.psum_tensor("p2_av", [128, 2048], F32))
            ps_at = pctx.enter_context(nc.psum_tensor("p2_at", [128, 512], BF16))
            pssc_b = sch.buf("ps_sc")
            psav_b = sch.buf("ps_av")
            psat_b = sch.buf("ps_at")

            qin_o = [ab.take(16 * 512), ab.take(16 * 512)]
            qin_b = [sch.buf("qin0"), sch.buf("qin1")]
            kr_o = ab.take(6 * 4 * 512)
            kr_b = [sch.buf(f"kr{s_}") for s_ in range(6)]
            vr_o = ab.take(6 * 4 * 516)
            vr_b = [sch.buf(f"vr{s_}") for s_ in range(6)]
            tb_o = af.take(1536)
            tb_b = sch.buf("tb")
            eb_o = [ab.take(1536), ab.take(1536)]
            eb_b = [sch.buf("eb0"), sch.buf("eb1")]
            adiv_o = ab.take(4 * 128)
            adiv_b = sch.buf("adiv")
            rcd_o = af.take(4)
            rcd_b = sch.buf("rcd")
            aotb_o = [ab.take(512), ab.take(512)]
            aotb_b = [sch.buf("ao0"), sch.buf("ao1")]

            def load_kv_slot(cc):
                slot = cc % 6
                kd = _ap3(ab.ap(kr_o + slot * 4 * 512, 1), [(512, 4), (1, 512)])
                ks = _dram_ap(kt_s, [(CH, 128), (128 * CH, 4), (1, CH)],
                              cc * NKV * 128 * CH)
                sch.op(POOL, lambda g_, d=kd, s=ks: g_.dma_start(out=d, in_=s),
                       writes=[kr_b[slot]], dreads=[DB["kt"]], dma=True,
                       tag=f"kr_{cc}")
                vd = _ap3(ab.ap(vr_o + slot * 4 * 516, 1), [(516, 4), (1, 516)])
                vs2 = _dram_ap(v_s, [(516, 128), (128 * 516, 4), (1, 516)],
                               cc * 4 * 128 * 516)
                sch.op(POOL, lambda g_, d=vd, s=vs2: g_.dma_start(out=d, in_=s),
                       writes=[vr_b[slot]], dreads=[DB["v"]], dma=True,
                       tag=f"vr_{cc}")

            def load_q_chunk(cc):
                par = cc % 2
                qd = _ap3(ab.ap(qin_o[par], 1), [(512, 16), (1, 512)])
                qs2 = _dram_ap(qt_s, [(CH, 128), (128 * CH, 16), (1, CH)],
                               cc * NH * 128 * CH)
                sch.op(POOL, lambda g_, d=qd, s=qs2: g_.dma_start(out=d, in_=s),
                       writes=[qin_b[par]], dreads=[DB["qt"]], dma=True,
                       tag=f"q_{cc}")

            for cc in range(3):
                load_kv_slot(cc)
            load_q_chunk(0)
            pending_tail = [None]

            for c in range(NCH):
                if c + 1 < NCH:
                    load_q_chunk(c + 1)
                if c + 3 < NCH:
                    load_kv_slot(c + 3)
                kb_lo = max(0, 4 * c - 8)
                kb_hi = min(NT - 1, 4 * c + 11)
                kbs = list(range(kb_lo, kb_hi + 1))
                for h in range(NH):
                    kv = h // 4
                    qrhs = ab.ap(qin_o[c % 2] + h * 512, 512)
                    started = [False] * 4
                    waves = [kbs[w0:w0 + 3] for w0 in range(0, len(kbs), 3)]

                    def emit_scores(wave):
                        for wi, kb in enumerate(wave):
                            slot = (kb // 4) % 6
                            lhs = ab.ap(kr_o + slot * 4 * 512 + kv * 512 +
                                        (kb % 4) * 128, 128)
                            psl = ps_sc[:, wi * 512:(wi + 1) * 512]
                            sch.op(PE, lambda t, l=lhs, r=qrhs, p=psl:
                                   t.matmul(p, l, r, start=True, stop=True),
                                   reads=[kr_b[slot], qin_b[c % 2]],
                                   writes=[pssc_b], tag="sc_mm")

                    def emit_acts(wave, ebp):
                        nW = len(wave) * 512
                        tb = af.ap(tb_o, nW)
                        sch.op(ACT, lambda a, d=tb, n=nW: a.activation(
                            d, ps_sc[:, 0:n], AF.Tanh),
                            reads=[pssc_b], writes=[tb_b], tag="sc_tanh")
                        eb = ab.ap(eb_o[ebp], nW)
                        sch.op(ACT, lambda a, s=tb, d=eb: a.activation(
                            d, s, AF.Exp, scale=CAP),
                            reads=[tb_b], writes=[eb_b[ebp]], tag="sc_exp")

                    def emit_av(wave, ebp):
                        for wi, kb in enumerate(wave):
                            slot = (kb // 4) % 6
                            for qs in range(4):
                                qi = 4 * c + qs
                                dblk = kb - qi
                                if abs(dblk) > 8:
                                    continue
                                esl = ab.ap(eb_o[ebp] + wi * 512 + qs * 128, 128)
                                if dblk == 8:
                                    sch.op(DVE, lambda v, e=esl: v.tensor_mul(
                                        e, e, maskhi()),
                                        reads=[eb_b[ebp], CB["c_maskhi"]],
                                        writes=[eb_b[ebp]], tag="m_hi")
                                elif dblk == -8:
                                    sch.op(DVE, lambda v, e=esl: v.tensor_mul(
                                        e, e, masklo()),
                                        reads=[eb_b[ebp], CB["c_masklo"]],
                                        writes=[eb_b[ebp]], tag="m_lo")
                                vrhs = ab.ap(vr_o + slot * 4 * 516 +
                                             (kb % 4) * 516 + kv * 129, 129)
                                psl = ps_av[:, qs * 512:qs * 512 + 129]
                                st = not started[qs]
                                started[qs] = True
                                sp = (kb == min(NT - 1, qi + 8))
                                sch.op(PE, lambda t, l=esl, r=vrhs, p=psl, s=st,
                                       s2=sp: t.matmul(p, l, r, start=s, stop=s2),
                                       reads=[eb_b[ebp], vr_b[slot]],
                                       writes=[psav_b], tag="av_mm")

                    if pending_tail[0] is not None:
                        pass
                    emit_scores(waves[0])
                    if pending_tail[0] is not None:
                        pending_tail[0]()
                        pending_tail[0] = None
                    for w in range(len(waves)):
                        emit_acts(waves[w], w % 2)
                        if w + 1 < len(waves):
                            emit_scores(waves[w + 1])
                        emit_av(waves[w], w % 2)

                    def unit_tail(c=c, h=h):
                        for qs in range(4):
                            rc = af.ap(rcd_o + qs, 1)
                            den = ps_av[:, qs * 512 + 128:qs * 512 + 129]
                            sch.op(DVE, lambda v, d=rc, s=den: v.reciprocal(d, s),
                                   reads=[psav_b], writes=[rcd_b], tag="d_rcp")
                            num = ps_av[:, qs * 512:qs * 512 + 128]
                            ad = ab.ap(adiv_o + qs * 128, 128)
                            sch.op(DVE, lambda v, d=ad, s=num, r=rc:
                                   v.tensor_scalar_mul(d, s, r),
                                   reads=[psav_b, rcd_b], writes=[adiv_b],
                                   tag="d_mul")
                        for qs in range(4):
                            ad = ab.ap(adiv_o + qs * 128, 128)
                            sch.op(PE, lambda t, s=ad,
                                   p=ps_at[:, qs * 128:(qs + 1) * 128]:
                                   t.transpose(p, s, ident()),
                                   reads=[adiv_b, CB["c_ident"]], writes=[psat_b],
                                   tag="ao_tr")
                        hp = h % 2
                        ao = ab.ap(aotb_o[hp], 512)
                        sch.op(DVE, lambda v, d=ao: v.tensor_copy(d, ps_at[:]),
                               reads=[psat_b], writes=[aotb_b[hp]], tag="ao_ev")
                        adst = _dram_ap(aot_s, [(CH, 128), (1, CH)],
                                        (c * 16 + h) * 128 * CH)
                        sch.op(SP, lambda q, s=ao, d=adst: q.dma_start(out=d,
                                                                       in_=s),
                               reads=[aotb_b[hp]], dwrites=[DB["aot"]], dma=True,
                               tag="ao_st")

                    pending_tail[0] = unit_tail

            if pending_tail[0] is not None:
                pending_tail[0]()
                pending_tail[0] = None

        sch.set_fence()
        ab.reset()
        af.reset()

        sch.phase = "P3wo"
        with ExitStack() as pctx:
            ps_tr = pctx.enter_context(nc.psum_tensor("p3_tr", [128, 2048], BF16))
            ps_tr2 = pctx.enter_context(nc.psum_tensor("p3_tr2", [128, 2048], BF16))
            ps_w = pctx.enter_context(nc.psum_tensor("p3_w", [128, 1024], F32))
            pstr_t = [ps_tr, ps_tr2]
            ps_trb = [sch.buf("p3tr0"), sch.buf("p3tr1")]
            psw_b = [sch.buf("p3w0"), sch.buf("p3w1")]

            wo_o = ab.take(16 * 2048)
            wo_b = sch.buf("wo")
            wof_o = [af.take(2048)]
            wof_b = [sch.buf("wof0")]
            for hdc in range(16):
                par = 0
                stg = af.ap(wof_o[par], 2048)
                src = _dram_ap(wo_in, [(E, 128), (1, E)], hdc * 128 * E)
                sch.op(POOL, lambda g_, s=src, d=stg: g_.dma_start(out=d, in_=s),
                       writes=[wof_b[par]], dma=True, tag="wo_ld")
                dstb = ab.ap(wo_o + hdc * 2048, 2048)
                sch.op(ACT, lambda a, s=stg, d=dstb: a.copy(d, s),
                       reads=[wof_b[par]], writes=[wo_b], tag="wo_cvt")

            aoin_o = [ab.take(16 * 512), ab.take(16 * 512)]
            aoin_b = [sch.buf("ai0"), sch.buf("ai1")]
            x4f_o = [af.take(4 * 2048)]
            x4f_b = [sch.buf("x40")]
            r_o = af.take(2048)
            r_b = sch.buf("r")
            st_o = af.take(28)
            st_b = sch.buf("stats")
            x1b_o = [ab.take(2048), ab.take(2048)]
            x1b_b = [sch.buf("x1b0"), sch.buf("x1b1")]
            ex1_o = [ab.take(2048), ab.take(2048)]
            ex1_b = [sch.buf("ex0"), sch.buf("ex1")]

            def layer_norm(src_ap, src_buf, out_ap, out_buf, tag):
                bn24 = af.ap(st_o, 24)
                bn2 = af.ap(st_o + 24, 2)
                for i in range(4):
                    bno = af.ap(st_o + i * 6, 6)
                    bni = _ap3(src_ap, [(1, 512)], i * 512)
                    sch.op(DVE, (lambda o2, i2: (lambda v: v.bn_stats(o2, i2)))(
                        bno, bni),
                        reads=[src_buf], writes=[st_b], tag=tag + "_bns")
                sch.op(DVE, lambda v: v.bn_aggr(bn2, bn24),
                       reads=[st_b], writes=[st_b], tag=tag + "_bna")
                mean = af.ap(st_o + 24, 1)
                var = af.ap(st_o + 25, 1)
                sd = af.ap(st_o + 26, 1)
                sch.op(ACT, lambda a: a.activation(sd, var, AF.Sqrt,
                                                   bias=epsln_ap()),
                       reads=[st_b, epsln_b], writes=[st_b], tag=tag + "_sd")
                rs = af.ap(st_o + 27, 1)
                sch.op(DVE, lambda v: v.reciprocal(rs, sd),
                       reads=[st_b], writes=[st_b], tag=tag + "_rs")
                sch.op(DVE, lambda v: v.tensor_scalar(
                    out_ap, src_ap, mean, rs, ALU.subtract, ALU.mult),
                    reads=[src_buf, st_b], writes=[out_buf], tag=tag + "_out")

            def aoin_load(cc):
                aod = _ap3(ab.ap(aoin_o[cc % 2], 1), [(512, 16), (1, 512)])
                aos = _dram_ap(aot_s, [(CH, 128), (128 * CH, 16), (1, CH)],
                               cc * 16 * 128 * CH)
                sch.op(POOL, lambda g_, d=aod, s=aos: g_.dma_start(out=d, in_=s),
                       writes=[aoin_b[cc % 2]], dreads=[DB["aot"]], dma=True,
                       tag=f"ai_{cc}")

            aoin_load(0)
            for c in range(NCH):
                cp = c % 2
                if c + 1 < NCH:
                    aoin_load(c + 1)
                x4d = _ap3(af.ap(x4f_o[0], 1), [(2048, 4), (1, 2048)])
                xsrc = _dram_ap(x_in, [(E, 128), (128 * E, 4), (1, E)],
                                c * 4 * 128 * E)
                sch.op(POOL, lambda g_, d=x4d, s=xsrc: g_.dma_start(out=d, in_=s),
                       writes=[x4f_b[0]], dma=True, tag=f"x4_{c}")
                for ts in range(4):
                    for dmc in range(4):
                        par = dmc % 2
                        psl = ps_w[:, par * 512:(par + 1) * 512]
                        for hdc in range(16):
                            lhs = ab.ap(aoin_o[cp] + hdc * 512 + ts * 128, 128)
                            rhs = ab.ap(wo_o + hdc * 2048 + dmc * 512, 512)
                            sch.op(PE, lambda t, l=lhs, r2=rhs, p=psl,
                                   st=(hdc == 0), sp=(hdc == 15):
                                   t.matmul(p, l, r2, start=st, stop=sp),
                                   reads=[aoin_b[cp], wo_b], writes=[psw_b[par]],
                                   tag="wo_mm")
                        xsl = af.ap(x4f_o[0] + ts * 2048 + dmc * 512, 512)
                        rsl = af.ap(r_o + dmc * 512, 512)
                        sch.op(DVE, lambda v, p=psl, xs=xsl, d=rsl:
                               v.scalar_tensor_tensor(d, p, 1.0, xs, ALU.mult,
                                                      ALU.add),
                               reads=[psw_b[par], x4f_b[0]], writes=[r_b],
                               tag="wo_res")
                    if _DEBUG:
                        rdst = _dram_ap(dbg_r, [(E, 128), (1, E)],
                                        (c * 4 + ts) * 128 * E)
                        sch.op(SP, lambda q, d=rdst, s=af.ap(r_o, 2048):
                               q.dma_start(out=d, in_=s),
                               reads=[r_b], dma=True, tag="dbg_r")
                    x1t = ab.ap(x1b_o[ts % 2], 2048)
                    layer_norm(af.ap(r_o, 2048), r_b, x1t, x1b_b[ts % 2], "ln1")
                    if _DEBUG:
                        sdst = _dram_ap(dbg_st, [(28, 128), (1, 28)],
                                        (c * 4 + ts) * 128 * 28)
                        sch.op(SP, lambda q, d=sdst, s=af.ap(st_o, 28):
                               q.dma_start(out=d, in_=s),
                               reads=[st_b], dma=True, tag="dbg_st")
                    x1dst = _dram_ap(x1_s, [(E, 128), (1, E)],
                                     (c * 4 + ts) * 128 * E)
                    sch.op(SP, lambda q, s=x1t, d=x1dst: q.dma_start(out=d, in_=s),
                           reads=[x1b_b[ts % 2]], dwrites=[DB["x1"]], dma=True,
                           tag="x1_st")
                    pst = pstr_t[ts % 2]
                    for k in range(16):
                        sch.op(PE, (lambda kk, xv=x1t, p=pst: (lambda t:
                               t.transpose(p[:, kk * 128:(kk + 1) * 128],
                                           xv[:, kk * 128:(kk + 1) * 128],
                                           ident())))(k),
                            reads=[x1b_b[ts % 2], CB["c_ident"]],
                            writes=[ps_trb[ts % 2]], tag="x1_tr")
                    ex1 = ab.ap(ex1_o[ts % 2], 2048)
                    sch.op(DVE, lambda v, p=pst, d=ex1: v.tensor_copy(d, p[:]),
                           reads=[ps_trb[ts % 2]], writes=[ex1_b[ts % 2]],
                           tag="x1_ev")
                    xtd = _dram_ap(x1t_s, [(S, 128), (128 * S, 16), (1, 128)],
                                   c * 512 + ts * 128)
                    es = _ap3(ab.ap(ex1_o[ts % 2], 1), [(128, 16), (1, 128)])
                    sch.op(SP, lambda q, s=es, d=xtd: q.dma_start(out=d, in_=s),
                           reads=[ex1_b[ts % 2]], dwrites=[DB["x1t"]], dma=True,
                           tag="x1t_st")

        sch.set_fence()
        ab.reset()
        af.reset()

        sch.phase = "P4ffn"
        with ExitStack() as pctx:
            ps_u1 = pctx.enter_context(nc.psum_tensor("p4_u1", [128, 1024], F32))
            ps_u3 = pctx.enter_context(nc.psum_tensor("p4_u3", [128, 1024], F32))
            ps_d = pctx.enter_context(nc.psum_tensor("p4_d", [128, 2048], F32))
            psu1_b = [sch.buf("u10"), sch.buf("u11")]
            psu3_b = [sch.buf("u30"), sch.buf("u31")]
            psd_b = [sch.buf(f"pd{i}") for i in range(4)]

            x1tin_o = ab.take(16 * 512)
            x1tin_b = sch.buf("x1tin")
            x1tok_o = ab.take(4 * 2048)
            x1tok_b = sch.buf("x1tok")
            w1g_o = [ab.take(4096), ab.take(4096)]
            w3g_o = [ab.take(4096), ab.take(4096)]
            w1g_b = [sch.buf("w1g0"), sch.buf("w1g1")]
            w3g_b = [sch.buf("w3g0"), sch.buf("w3g1")]
            w2g_o = [ab.take(4096), ab.take(4096)]
            w2g_b = [sch.buf("w2g0"), sch.buf("w2g1")]
            gt_o = ab.take(64 * 512)
            gt_b = sch.buf("gt")
            s1_o = [af.take(512), af.take(512)]
            s1_b = [sch.buf("s10"), sch.buf("s11")]
            r4_o = [af.take(2048) for _ in range(4)]
            r4_b = [sch.buf(f"r4_{i}") for i in range(4)]
            outf_o = af.take(2048)
            outf_b = sch.buf("outf")
            st_o = af.take(28)
            st_b = sch.buf("st4")

            for p in range(NCH):
                if p == 0:
                    x1td = _ap3(ab.ap(x1tin_o, 1), [(512, 16), (1, 512)])
                    x1src = _dram_ap(x1t_s, [(S, 128), (128 * S, 16), (1, 512)], 0)
                    sch.op(POOL, lambda g_, d=x1td, s=x1src:
                           g_.dma_start(out=d, in_=s),
                           writes=[x1tin_b], dreads=[DB["x1t"]], dma=True,
                           tag="fx1t_0")
                xtk = _ap3(ab.ap(x1tok_o, 1), [(2048, 4), (1, 2048)])
                xts = _dram_ap(x1_s, [(E, 128), (128 * E, 4), (1, E)],
                               p * 4 * 128 * E)
                sch.op(POOL, lambda g_, d=xtk, s=xts: g_.dma_start(out=d, in_=s),
                       writes=[x1tok_b], dreads=[DB["x1"]], dma=True,
                       tag=f"fx1_{p}")

                def load_wgroup(g, par):
                    for (wsrc, woff, wbuf) in [(w1t_s, w1g_o, w1g_b),
                                               (w3t_s, w3g_o, w3g_b)]:
                        d = _ap3(ab.ap(woff[par], 1), [(256, 16), (1, 256)])
                        s = _dram_ap(wsrc, [(HF, 128), (128 * HF, 16), (1, 256)],
                                     g * 256)
                        sch.op(SP, lambda q, d2=d, s2=s: q.dma_start(out=d2, in_=s2),
                               writes=[wbuf[par]],
                               dreads=[DB["w1t"] if wsrc is w1t_s else DB["w3t"]],
                               dma=True, tag=f"wg_{p}_{g}")

                if p == 0:
                    load_wgroup(0, 0)
                for g in range(32):
                    par = g % 2
                    if g + 1 < 32:
                        load_wgroup(g + 1, (g + 1) % 2)
                    for hh in range(2):
                        hc = g * 2 + hh
                        up = hc % 2
                        psl1 = ps_u1[:, up * 512:(up + 1) * 512]
                        psl3 = ps_u3[:, up * 512:(up + 1) * 512]
                        for dc in range(16):
                            l1 = ab.ap(w1g_o[par] + dc * 256 + hh * 128, 128)
                            l3 = ab.ap(w3g_o[par] + dc * 256 + hh * 128, 128)
                            rr = ab.ap(x1tin_o + dc * 512, 512)
                            sch.op(PE, lambda t, l=l1, r=rr, p2=psl1, st=(dc == 0),
                                   sp=(dc == 15): t.matmul(p2, l, r, start=st,
                                                           stop=sp),
                                   reads=[w1g_b[par], x1tin_b],
                                   writes=[psu1_b[up]], tag="u1_mm")
                            sch.op(PE, lambda t, l=l3, r=rr, p2=psl3, st=(dc == 0),
                                   sp=(dc == 15): t.matmul(p2, l, r, start=st,
                                                           stop=sp),
                                   reads=[w3g_b[par], x1tin_b],
                                   writes=[psu3_b[up]], tag="u3_mm")
                        s1 = af.ap(s1_o[up], 512)
                        sch.op(ACT, lambda a, d=s1, p2=psl1: a.activation(
                            d, p2, AF.Silu),
                            reads=[psu1_b[up]], writes=[s1_b[up]], tag="silu")
                        gsl = ab.ap(gt_o + hc * 512, 512)
                        sch.op(DVE, lambda v, d=gsl, a2=s1, p2=psl3:
                               v.tensor_mul(d, a2, p2),
                               reads=[s1_b[up], psu3_b[up]], writes=[gt_b],
                               tag="g_mul")
                if p + 1 < NCH:
                    load_wgroup(0, 0)
                if p + 1 < NCH:
                    x1td2 = _ap3(ab.ap(x1tin_o, 1), [(512, 16), (1, 512)])
                    x1src2 = _dram_ap(x1t_s, [(S, 128), (128 * S, 16), (1, 512)],
                                      (p + 1) * 512)
                    sch.op(POOL, lambda g_, d=x1td2, s=x1src2:
                           g_.dma_start(out=d, in_=s),
                           writes=[x1tin_b], dreads=[DB["x1t"]], dma=True,
                           tag=f"fx1t_{p + 1}")
                def load_w2g(dmc2, hb, par):
                    d = _ap3(ab.ap(w2g_o[par], 1), [(512, 8), (1, 512)])
                    s = _dram_ap(w2f_s, [(E, 128), (128 * E, 8), (1, 512)],
                                 hb * 8 * 128 * E + dmc2 * 512)
                    sch.op(SP, lambda q, d2=d, s2=s: q.dma_start(out=d2, in_=s2),
                           writes=[w2g_b[par]], dreads=[DB["w2f"]], dma=True,
                           tag=f"w2g_{p}_{dmc2}_{hb}")

                if p == 0:
                    load_w2g(0, 0, 0)
                for dmc in range(4):
                    for hb in range(8):
                        par = hb % 2
                        if hb + 1 < 8:
                            load_w2g(dmc, hb + 1, (hb + 1) % 2)
                        elif dmc + 1 < 4:
                            load_w2g(dmc + 1, 0, (hb + 1) % 2)
                        elif p + 1 < NCH:
                            load_w2g(0, 0, (hb + 1) % 2)
                        for hh in range(8):
                            hc = hb * 8 + hh
                            rhs = ab.ap(w2g_o[par] + hh * 512, 512)
                            for ts in range(4):
                                lhs = ab.ap(gt_o + hc * 512 + ts * 128, 128)
                                psl = ps_d[:, ts * 512:(ts + 1) * 512]
                                sch.op(PE, lambda t, l=lhs, r=rhs, p2=psl,
                                       st=(hc == 0), sp=(hc == 63):
                                       t.matmul(p2, l, r, start=st, stop=sp),
                                       reads=[gt_b, w2g_b[par]],
                                       writes=[psd_b[ts]], tag="dn_mm")
                    for ts in range(4):
                        psl = ps_d[:, ts * 512:(ts + 1) * 512]
                        xsl = ab.ap(x1tok_o + ts * 2048 + dmc * 512, 512)
                        rsl = af.ap(r4_o[ts] + dmc * 512, 512)
                        sch.op(DVE, lambda v, p2=psl, xs=xsl, d=rsl:
                               v.scalar_tensor_tensor(d, p2, 1.0, xs, ALU.mult,
                                                      ALU.add),
                               reads=[psd_b[ts], x1tok_b], writes=[r4_b[ts]],
                               tag="dn_res")
                for ts in range(4):
                    rr = af.ap(r4_o[ts], 2048)
                    of = af.ap(outf_o, 2048)
                    bn24 = af.ap(st_o, 24)
                    bn2 = af.ap(st_o + 24, 2)
                    for i in range(4):
                        bno = af.ap(st_o + i * 6, 6)
                        bni = _ap3(rr, [(1, 512)], i * 512)
                        sch.op(DVE, (lambda o2, i2: (lambda v: v.bn_stats(
                            o2, i2)))(bno, bni),
                            reads=[r4_b[ts]], writes=[st_b], tag="ln2_bns")
                    sch.op(DVE, lambda v: v.bn_aggr(bn2, bn24),
                           reads=[st_b], writes=[st_b], tag="ln2_bna")
                    mean = af.ap(st_o + 24, 1)
                    var = af.ap(st_o + 25, 1)
                    sd = af.ap(st_o + 26, 1)
                    sch.op(ACT, lambda a: a.activation(sd, var, AF.Sqrt,
                                                       bias=epsln_ap()),
                           reads=[st_b, epsln_b], writes=[st_b], tag="ln2_sd")
                    rs = af.ap(st_o + 27, 1)
                    sch.op(DVE, lambda v: v.reciprocal(rs, sd),
                           reads=[st_b], writes=[st_b], tag="ln2_rs")
                    sch.op(DVE, lambda v, s=rr, d=of: v.tensor_scalar(
                        d, s, mean, rs, ALU.subtract, ALU.mult),
                        reads=[r4_b[ts], st_b], writes=[outf_b], tag="ln2_out")
                    odst = _dram_ap(out_t, [(E, 128), (1, E)],
                                    (p * 4 + ts) * 128 * E)
                    sch.op(ACT, lambda a, s=of, d=odst: a.dma_start(out=d, in_=s),
                           reads=[outf_b], dma=True, tag="out_st")

        finals = dict(sch.counts)
        sems = {}
        for name in _ALL_SEMS:
            sems[name] = ctx.enter_context(nc.semaphore(name))
        block = ctx.enter_context(nc.Block())
        sch.emit(block, sems, nc)

        @block.sync
        def _(sync):
            for qn in (Q_ACT, Q_SP, Q_POOL):
                if finals[qn]:
                    sync.wait_ge(sems[qn], finals[qn])

    return nc


_NC_CACHE = None


def _get_nc():
    global _NC_CACHE
    if _NC_CACHE is None:
        _NC_CACHE = build_nc()
    return _NC_CACHE


def kernel(L0_attn_hidden_states, L0_attn_wq, L0_attn_wk, L0_attn_wv, L0_attn_wo,
           L0_attn_query_scale, L0_attn_key_scale, L0_ffn_W1, L0_ffn_W2, L0_ffn_W3):
    consts = host_constants()
    nc = _get_nc()
    xs = np.ascontiguousarray(np.asarray(L0_attn_hidden_states, dtype=np.float32))
    shared = {
        "wq": np.ascontiguousarray(np.asarray(L0_attn_wq, np.float32)),
        "wk": np.ascontiguousarray(np.asarray(L0_attn_wk, np.float32)),
        "wv": np.ascontiguousarray(np.asarray(L0_attn_wv, np.float32)),
        "wo": np.ascontiguousarray(np.asarray(L0_attn_wo, np.float32)),
        "qscale": np.ascontiguousarray(np.asarray(L0_attn_query_scale, np.float32)),
        "kscale": np.ascontiguousarray(np.asarray(L0_attn_key_scale, np.float32)),
        "w1": np.ascontiguousarray(np.asarray(L0_ffn_W1, np.float32)),
        "w2": np.ascontiguousarray(np.asarray(L0_ffn_W2, np.float32)),
        "w3": np.ascontiguousarray(np.asarray(L0_ffn_W3, np.float32)),
    }
    shared.update(consts)
    in_maps = []
    for b in range(B):
        m = dict(shared)
        m["x"] = np.ascontiguousarray(xs[b])
        in_maps.append(m)
    res = run_bass_kernel_spmd(nc, in_maps, core_ids=list(range(B)))
    out = np.stack([r["out"] for r in res.results], axis=0)
    return out
